# version 03
# baseline (speedup 1.0000x reference)
import numpy as np
import ml_dtypes
import concourse.bass as bass
import concourse.mybir as mybir
from concourse.bass_utils import run_bass_kernel_spmd

F32 = mybir.dt.float32
BF16 = mybir.dt.bfloat16
I32 = mybir.dt.int32
AF = mybir.ActivationFunctionType
ALU = mybir.AluOpType
AX = mybir.AxisListType
bfnp = ml_dtypes.bfloat16

HEADS, DH, SEG, TT = 16, 64, 32, 128
SCALE = DH ** -0.5
D, DQKV, DFF, RANK = 1024, 3072, 4096, 256
N_TOK = 8192
NT_FULL = N_TOK // TT
ST_TILES = 4
LN_EPS = 1e-5
MAGIC = 0x5F3759DF
ENGS = ("sync", "tensor", "vector", "scalar", "gpsimd")
ESEM = dict(sync="esync", tensor="pe", vector="dve", scalar="act",
            gpsimd="pool")


def _np_consts(n):
    ident = np.eye(TT, dtype=np.float32).astype(bfnp)
    segid = np.arange(TT) // SEG
    maskup = (segid[:, None] < segid[None, :]).astype(np.float32).astype(bfnp)
    causal = ((segid[:, None] == segid[None, :]) &
              (np.arange(TT)[:, None] >= np.arange(TT)[None, :]))
    causal = causal.astype(np.float32).astype(bfnp)
    j = np.arange(0, DH, 2, dtype=np.float32)
    freqs = (10000.0 ** (-j / DH)).astype(np.float32)
    ang = np.arange(n, dtype=np.float32)[:, None] * freqs[None, :]
    cs = np.concatenate([np.cos(ang), np.sin(ang)], axis=1).astype(np.float32)
    ones_row = np.ones((1, TT), dtype=np.float32).astype(bfnp)
    onesc_bf = np.ones((TT, 1), dtype=np.float32).astype(bfnp)
    zero_row = np.zeros((1, 512), dtype=np.float32).astype(bfnp)
    one_one = np.ones((1, 1), dtype=np.float32).astype(bfnp)
    magic_t = np.full((TT, 1), MAGIC, dtype=np.int32)
    zbd0 = np.zeros((TT, 16), dtype=np.float32)
    for c in range(8):
        zbd0[0:64, 2 * c] = 1e-6
        zbd0[64:128, 2 * c + 1] = 1e-6
    return dict(ident=ident, maskup=maskup, causal=causal, cs=cs,
                ones_row=ones_row, onesc_bf=onesc_bf, zero_row=zero_row,
                one_one=one_one, magic=magic_t, zbd0=zbd0)


class Plan:
    def __init__(self, until_tag=None):
        self.until = until_tag
        self.frozen = False
        self.steps = []
        self.cnt = {k: 0 for k in
                    ("dmaw", "dmax", "dmac", "dmao", "pe", "dve", "act",
                     "pool", "esync")}
        self.sems = {}
        self.pending_barrier = {}

    def step(self, engine, fn, waits=(), inc=None, inc_amt=1, reads=(),
             writes=(), tag="", force=False):
        if self.until is not None and self.until in tag:
            self.frozen = True
        if self.frozen and not force:
            if inc is not None:
                return (inc, self.cnt[inc])
            return None
        ev = None
        if inc is not None:
            self.cnt[inc] += inc_amt
            ev = (inc, self.cnt[inc])
        w = [x for x in waits if x is not None]
        if engine in self.pending_barrier:
            w = self.pending_barrier.pop(engine) + w
        self.steps.append(dict(engine=engine, fn=fn, waits=w, inc=inc,
                               inc_amt=inc_amt, reads=set(reads),
                               writes=set(writes), tag=tag))
        return ev

    def barrier(self, tag=""):
        prev = None
        for e in ENGS:
            sem = ESEM[e]
            def f(eng, e=e):
                return (eng.engine_nop() if e in ("vector", "gpsimd")
                        else eng.nop())
            prev = self.step(e, f, waits=[prev] if prev else (),
                             inc=sem, tag=tag + f".bar.{e}")
        for e in ENGS[:-1]:
            self.pending_barrier[e] = [prev]

    def run_engine(self, eng, engine_name):
        last_writes = set()
        for st in self.steps:
            if st["engine"] != engine_name:
                continue
            waited = False
            for semname, val in st["waits"]:
                eng.wait_ge(self.sems[semname], val)
                waited = True
            if waited:
                last_writes = set()
            if st["reads"] & last_writes or st["writes"] & last_writes:
                if engine_name in ("vector", "gpsimd"):
                    eng.engine_nop()
                elif engine_name == "scalar":
                    eng.drain()
            inst = st["fn"](eng)
            if st["inc"] is not None:
                assert inst is not None, st["tag"]
                inst.then_inc(self.sems[st["inc"]], st["inc_amt"])
            last_writes = st["writes"]


class PsumGen:

    def __init__(self):
        self.regions = set()

    def new_gen(self):
        self.regions = set()

    def start(self, col_lo, col_hi):
        key = (col_lo, col_hi)
        st = key not in self.regions
        self.regions.add(key)
        return st


def build(nt=NT_FULL, taps=(), until_tag=None):
    nc = bass.Bass("TRN2", target_bir_lowering=False, debug=False)
    n = nt * TT
    C = _np_consts(n)

    dx = nc.dram_tensor("x", [n, D], F32, kind="ExternalInput").ap()
    dwq = nc.dram_tensor("w_qkv", [D, DQKV], F32, kind="ExternalInput").ap()
    dwo = nc.dram_tensor("w_out", [D, D], F32, kind="ExternalInput").ap()
    dlng = nc.dram_tensor("ln_g", [1, D], F32, kind="ExternalInput").ap()
    dlnb = nc.dram_tensor("ln_b", [1, D], F32, kind="ExternalInput").ap()
    dbeta = nc.dram_tensor("mem_beta", [1, HEADS], F32, kind="ExternalInput").ap()
    dU1 = nc.dram_tensor("U1", [D, RANK], F32, kind="ExternalInput").ap()
    dV1 = nc.dram_tensor("V1", [RANK, DFF], F32, kind="ExternalInput").ap()
    db1 = nc.dram_tensor("b1", [1, DFF], F32, kind="ExternalInput").ap()
    dU2 = nc.dram_tensor("U2", [DFF, RANK], F32, kind="ExternalInput").ap()
    dV2 = nc.dram_tensor("V2", [RANK, D], F32, kind="ExternalInput").ap()
    db2 = nc.dram_tensor("b2", [1, D], F32, kind="ExternalInput").ap()
    dout = nc.dram_tensor("out", [n, D], F32, kind="ExternalOutput").ap()

    dident = nc.inline_tensor(C["ident"], "c_ident").ap()
    dmaskup = nc.inline_tensor(C["maskup"], "c_maskup").ap()
    dcausal = nc.inline_tensor(C["causal"], "c_causal").ap()
    dcs = nc.inline_tensor(C["cs"], "c_cs").ap()
    dones_row = nc.inline_tensor(C["ones_row"], "c_onesrow").ap()
    donesc = nc.inline_tensor(C["onesc_bf"], "c_onescol").ap()
    dzero_row = nc.inline_tensor(C["zero_row"], "c_zerorow").ap()
    done_one = nc.inline_tensor(C["one_one"], "c_oneone").ap()
    dmagic = nc.inline_tensor(C["magic"], "c_magic").ap()
    dzbd0 = nc.inline_tensor(C["zbd0"], "c_zbd0").ap()

    S = lambda name, shape, dt=BF16: nc.alloc_sbuf_tensor(name, shape, dt).ap()
    WQ = [S(f"WQ{c}", [TT, DQKV]) for c in range(8)]
    WO = [S(f"WO{c}", [TT, D]) for c in range(8)]
    U1w = [S(f"U1_{c}", [TT, RANK]) for c in range(8)]
    V1w = [S(f"V1_{r}", [TT, DFF]) for r in range(2)]
    U2w = [S(f"U2_{c}", [TT, RANK]) for c in range(32)]
    V2w = [S(f"V2_{r}", [TT, D]) for r in range(2)]
    bq_row = S("bq_row", [1, DQKV])
    b2bc = S("b2bc", [TT, D], F32)
    b1fm = S("b1fm", [TT, 32], F32)
    beta_bc = S("beta_bc", [TT, HEADS], F32)
    ombeta_bc = S("ombeta_bc", [TT, HEADS], F32)
    lngf = S("lngf", [TT, 8], F32)
    lnbf = S("lnbf", [TT, 8])
    ident = S("ident", [TT, TT])
    maskup = S("maskup", [TT, TT])
    causal = S("causal", [TT, TT])
    ones_row = S("ones_row", [1, TT])
    onesc = S("onesc", [TT, 1])
    zero_row = S("zero_row", [1, 512])
    one_one = S("one_one", [1, 1])
    magic = S("magic", [TT, 1], I32)
    zerocol = S("zerocol", [TT, 1], F32)

    zbd = S("zbd", [TT, HEADS], F32)
    zbd_bf = S("zbd_bf", [TT, HEADS])
    membf = S("membf", [TT, 1024])

    xin = [S(f"xin{i}", [TT, D], F32) for i in range(2)]
    csb = [S(f"cs{i}", [TT, 64], F32) for i in range(4)]
    xn2 = S("xn2", [TT, D])
    q_bf = S("q_bf", [TT, D])
    k_bf = S("k_bf", [TT, D])
    v_bf = S("v_bf", [TT, D])
    qr = S("qr", [TT, D])
    kr = S("kr", [TT, D])
    SQm = S("SQm", [TT, D])
    SKm = S("SKm", [TT, D])
    SKf = S("SKf", [TT, D])
    krz = S("krz", [TT, 2048])
    SQz = S("SQz", [TT, 2048])
    SKtz = S("SKtz", [TT, 2048])
    Pbuf = S("Pbuf", [TT, 2048])
    Pm = S("Pm", [TT, 2048])
    bvec = S("bvec", [TT, D], F32)
    g1 = S("g1", [TT, D])
    o_acc = S("o_acc", [TT, D], F32)
    x2 = [S(f"x2_{i}", [TT, D], F32) for i in range(ST_TILES)]
    P2 = S("P2", [TT, RANK])
    P2fw = S("P2fw", [TT, 2 * ST_TILES * TT])
    Hbuf = [S(f"Hbuf{i}", [TT, ST_TILES * TT]) for i in range(2)]
    Q2 = S("Q2", [TT, RANK])
    scr_bf = S("scr_bf", [TT, 1024])
    scr_f = S("scr_f", [TT, D], F32)
    sumv = S("sumv", [TT, 1], F32)
    sumsq = S("sumsq", [TT, 1], F32)
    stats = S("stats", [TT, 16], F32)
    negmu = S("negmu", [TT, 4], F32)
    rstd = S("rstd", [TT, 4], F32)
    den = S("den", [TT, HEADS], F32)
    denq = S("denq", [TT, HEADS], F32)
    rden = S("rden", [TT, HEADS], F32)
    rdq = S("rdq", [TT, HEADS], F32)
    negmax = S("negmax", [TT, HEADS], F32)
    lsum = S("lsum", [TT, HEADS], F32)
    rl = S("rl", [TT, HEADS], F32)

    canq = S("canq", [TT, D]) if "canq" in taps else None
    canqb = S("canqb", [TT, D]) if "canq" in taps else None
    cancs = S("cancs", [TT, 64], F32) if "canq" in taps else None
    xn2f, qrf = xn2, qr
    SKt = qr
    xn3 = SQm
    xn3f = xn3
    b_bf = v_bf
    g2 = k_bf
    o_bf, o_fm, ao, fout = q_bf, k_bf, bvec, o_acc
    PTb = Pbuf
    KTm = Pbuf
    SmU = S("SmU_dbg", [TT, 2048]) if "pm" in taps else Pm
    if "xn2" in taps:
        xn2f = S("xn2f_dbg", [TT, D])
    uvec = g1
    Q2f = S("Q2f_dbg", [TT, 2 * TT]) if "q2s0" in taps else Q2
    sq_scr = scr_bf[:, 0:1024]
    rowf = scr_f[0:1, 0:1024]
    rowbf = scr_bf[0:1, 0:1024]

    tap_srcs = dict(xn2=xn2, q=q_bf, k=k_bf, v=v_bf, qqr=qr, kkr=kr, sq=SQm,
                    canq=canq, canqb=canqb, cancs=cancs, bqr=qr, bpb=Pbuf,
                    negmax=negmax, fs0=o_acc, q2s0=Q2, p2f=P2fw, xn3t=xn3,
                    hs0=Hbuf[1],
                    sk=SKm, u=uvec, o=q_bf, aao=bvec, den=den, denq=denq,
                    mem=membf, lsum=lsum, x2_0=x2[0], b=bvec, ktm=KTm,
                    pm=Pm)
    dtap = {}
    for tname in taps:
        if tname == "x2late":
            continue
        src = tap_srcs[tname]
        dtap[tname] = nc.dram_tensor(f"tap_{tname}",
                                     [n, src.shape[-1]], src.dtype,
                                     kind="ExternalOutput").ap()

    PS_ST = nc.alloc_psum_tensor("ps_state", [TT, 1024], F32).ap()
    PS_A = nc.alloc_psum_tensor("ps_a", [TT, 1024], F32).ap()
    PS_B = nc.alloc_psum_tensor("ps_b", [TT, 1024], F32).ap()
    PS_C = nc.alloc_psum_tensor("ps_c", [TT, 1024], F32).ap()
    GA, GB, GC = PsumGen(), PsumGen(), PsumGen()
    GEN = {id(PS_A): GA, id(PS_B): GB, id(PS_C): GC}

    P = Plan(until_tag=until_tag)

    def pe(fns, waits=(), tag=""):
        def f(eng):
            last = None
            for fn in fns:
                last = fn(eng)
            return last
        return P.step("tensor", f, waits=waits, inc="pe", tag=tag)

    def vop(fn, waits=(), reads=(), writes=(), tag=""):
        return P.step("vector", fn, waits=waits, inc="dve", reads=reads,
                      writes=writes, tag=tag)

    def aop(fn, waits=(), reads=(), writes=(), tag=""):
        def fd(eng):
            fn(eng)
            return eng.drain()
        return P.step("scalar", fd, waits=waits, inc="act", reads=reads,
                      writes=writes, tag=tag)

    def gop(fn, waits=(), reads=(), writes=(), tag=""):
        return P.step("gpsimd", fn, waits=waits, inc="pool", reads=reads,
                      writes=writes, tag=tag)

    def dma(out, in_, waits=(), sem="dmaw", tag=""):
        def f(eng):
            return eng.dma_start(out=out, in_=in_)
        return P.step("sync", f, waits=waits, inc=sem, inc_amt=16, tag=tag)

    def hsl(t, h):
        return t[64 * (h % 2):64 * (h % 2) + 64,
                 TT * (h // 2):TT * (h // 2) + TT]

    def transpose_set(src, dst, ps, nchunks, waits, tag=""):
        g = GEN[id(ps)]
        g.new_gen()
        fns = []
        for c in range(nchunks):
            st = g.start(TT * c, TT * (c + 1))
            def f(eng, c=c, st=st):
                return eng.matmul(ps[:, TT * c:TT * (c + 1)],
                                  src[:, TT * c:TT * (c + 1)], ident,
                                  start=st, stop=True)
            fns.append(f)
        ev = pe(fns, waits=waits, tag=tag + ".t")
        def fe(eng):
            return eng.copy(out=dst[:, 0:TT * nchunks],
                            in_=ps[:, 0:TT * nchunks])
        return aop(fe, waits=[ev], writes=(dst.name,), tag=tag + ".ev")

    def transpose_zsplit(src, dst, ps, waits, tag=""):
        g = GEN[id(ps)]
        g.new_gen()
        fns = []
        for c in range(8):
            st = g.start(TT * c, TT * (c + 1))
            def f(eng, c=c, st=st):
                return eng.matmul(ps[:, TT * c:TT * (c + 1)],
                                  src[:, TT * c:TT * (c + 1)], ident,
                                  start=st, stop=True)
            fns.append(f)
        ev = pe(fns, waits=waits, tag=tag + ".t")
        top = dst.rearrange("p (c b k) -> p c b k", b=2, k=TT)[0:64, :, 0, :]
        bot = dst.rearrange("p (c b k) -> p c b k", b=2, k=TT)[64:128, :, 1, :]
        pst = ps.rearrange("p (c k) -> p c k", k=TT)
        e1 = aop(lambda eng: eng.copy(out=top, in_=pst[0:64]),
                 waits=[ev], writes=(dst.name,), tag=tag + ".e1")
        e2 = aop(lambda eng: eng.copy(out=bot, in_=pst[64:128]),
                 waits=[ev], writes=(dst.name,), tag=tag + ".e2")
        return e2

    def rsqrt_chain(src_col, dst_col, tag=""):
        s1, s2 = stats[:, 2:3], stats[:, 3:4]
        i1 = s1.bitcast(I32)
        fns = [
            lambda eng: eng.tensor_scalar(out=i1, in0=src_col.bitcast(I32),
                                          scalar1=0.5, scalar2=None,
                                          op0=ALU.mult),
            lambda eng: eng.tensor_tensor(out=i1, in0=magic, in1=i1,
                                          op=ALU.subtract),
            lambda eng: eng.tensor_tensor(out=s2, in0=s1, in1=s1, op=ALU.mult),
            lambda eng: eng.tensor_tensor(out=s2, in0=s2, in1=src_col,
                                          op=ALU.mult),
            lambda eng: eng.tensor_scalar(out=s2, in0=s2, scalar1=-0.5,
                                          scalar2=1.5, op0=ALU.mult,
                                          op1=ALU.add),
            lambda eng: eng.tensor_tensor(out=dst_col, in0=s1, in1=s2,
                                          op=ALU.mult),
        ]
        ev = None
        for i, fn in enumerate(fns):
            ev = P.step("gpsimd", fn, reads=("stats",),
                        writes=("stats", "rstd"),
                        inc=("pool" if i == len(fns) - 1 else None),
                        tag=tag + f".rs{i}")
        return ev

    def ln_stats_tail(site, tag):
        st = stats
        P.step("gpsimd", lambda eng: eng.tensor_scalar(
            out=negmu[:, site:site + 1], in0=sumv, scalar1=-1.0 / D,
            scalar2=None, op0=ALU.mult), reads=("sumv",), writes=("negmu",),
            tag=tag + ".mu")
        P.step("gpsimd", lambda eng: eng.tensor_tensor(
            out=st[:, 1:2], in0=negmu[:, site:site + 1],
            in1=negmu[:, site:site + 1], op=ALU.mult),
            reads=("negmu",), writes=("stats",), tag=tag + ".m2")
        P.step("gpsimd", lambda eng: eng.tensor_scalar(
            out=st[:, 0:1], in0=sumsq, scalar1=1.0 / D, scalar2=None,
            op0=ALU.mult), reads=("sumsq",), writes=("stats",), tag=tag + ".s2")
        P.step("gpsimd", lambda eng: eng.tensor_tensor(
            out=st[:, 0:1], in0=st[:, 0:1], in1=st[:, 1:2], op=ALU.subtract),
            reads=("stats",), writes=("stats",), tag=tag + ".var")
        P.step("gpsimd", lambda eng: eng.tensor_scalar(
            out=st[:, 1:2], in0=st[:, 0:1], scalar1=LN_EPS, scalar2=None,
            op0=ALU.add), reads=("stats",), writes=("stats",), tag=tag + ".ve")

    def ln_site(xten, xname, site, waits, fused, tag):
        ev_sq = aop(lambda eng: eng.activation(out=sq_scr, in_=xten,
                                               func=AF.Square,
                                               accum_out=sumsq),
                    waits=waits, reads=(xname,), writes=("sq_scr", "sumsq"),
                    tag=tag + ".sq")
        ev_sum = vop(lambda eng: eng.tensor_reduce(out=sumv, in_=xten,
                                                   axis=AX.X, op=ALU.add),
                     waits=waits, reads=(xname,), writes=("sumv",),
                     tag=tag + ".sum")
        P.step("gpsimd", lambda eng: eng.engine_nop(),
               waits=[ev_sq, ev_sum], tag=tag + ".w")
        ln_stats_tail(site, tag)
        st = stats
        if not fused:
            return rsqrt_chain(st[:, 1:2], rstd[:, site:site + 1], tag=tag)
        rsqrt_chain(st[:, 1:2], st[:, 4:5], tag=tag + ".r1")
        P.step("gpsimd", lambda eng: eng.tensor_tensor(
            out=st[:, 5:6], in0=st[:, 4:5], in1=st[:, 4:5], op=ALU.mult),
            reads=("stats",), writes=("stats",), tag=tag)
        P.step("gpsimd", lambda eng: eng.tensor_tensor(
            out=st[:, 5:6], in0=st[:, 5:6], in1=st[:, 0:1], op=ALU.mult),
            reads=("stats",), writes=("stats",), tag=tag)
        P.step("gpsimd", lambda eng: eng.tensor_scalar(
            out=st[:, 5:6], in0=st[:, 5:6], scalar1=LN_EPS, scalar2=None,
            op0=ALU.add), reads=("stats",), writes=("stats",), tag=tag)
        rsqrt_chain(st[:, 5:6], st[:, 6:7], tag=tag + ".r2")
        return P.step("gpsimd", lambda eng: eng.tensor_tensor(
            out=rstd[:, site:site + 1], in0=st[:, 4:5], in1=st[:, 6:7],
            op=ALU.mult), inc="pool", reads=("stats",), writes=("rstd",),
            tag=tag + ".r12")

    def bcast16(src16, cnt=16):
        return src16[:, 0:cnt].unsqueeze(2).broadcast_to([TT, cnt, DH])

    def tap(name, ev, t0):
        if name in dtap:
            src = tap_srcs[name]
            dma(dtap[name][t0:t0 + TT, :], src, waits=[ev], sem="dmao",
                tag=f"tap.{name}")

    ev_xd_pre, ev_cd_pre = {}, {}
    prev_xd, prev_cd = (), ()
    for it in range(min(2, nt)):
        ev_xd_pre[it] = dma(xin[it % 2], dx[it * TT:(it + 1) * TT, :],
                            sem="dmax", waits=prev_xd,
                            tag=f"prep.xd{it}")
        prev_xd = [ev_xd_pre[it]]
    for it in range(min(4, nt)):
        ev_cd_pre[it] = dma(csb[it % 4], dcs[it * TT:(it + 1) * TT, :],
                            sem="dmac", waits=prev_cd,
                            tag=f"prep.cd{it}")
        prev_cd = [ev_cd_pre[it]]
    for out, in_ in [(ident, dident), (maskup, dmaskup), (causal, dcausal),
                     (ones_row, dones_row), (onesc, donesc),
                     (zero_row, dzero_row), (one_one, done_one),
                     (magic, dmagic), (zbd, dzbd0)]:
        ev_consts = dma(out, in_, tag="prep.c")
    vop(lambda eng: eng.memset(zerocol, 0.0), writes=("zerocol",),
        tag="prep.zc")
    for tz, nm in ((krz, "krz"), (SQz, "SQz"), (SKtz, "SKtz")):
        gop(lambda eng, tz=tz: eng.memset(tz, 0.0), writes=(nm,),
            tag="prep.z" + nm)
    _zero_all = [xn2, q_bf, k_bf, v_bf, qr, kr, SQm, SKm, SKf, Pbuf, Pm,
                 bvec, g1, o_acc, sumv, sumsq, stats, negmu,
                 rstd, den, denq, rden, rdq, negmax, lsum, rl,
                 P2, P2fw, Hbuf[0], Hbuf[1], Q2] + x2
    for zi, zt in enumerate(_zero_all):
        vop(lambda eng, zt=zt: eng.memset(zt, 0.0), writes=("zz",),
            tag=f"prep.zz{zi}")
    ev_zbdbf = vop(lambda eng: eng.tensor_copy(out=zbd_bf, in_=zbd),
                   waits=[ev_consts], writes=("zbd_bf",), tag="prep.zbdbf")

    GST = PsumGen()
    _stz = [
        (lambda eng: eng.matmul(PS_ST[:, 0:512], ones_row, zero_row,
                                start=True, stop=True)),
        (lambda eng: eng.matmul(PS_ST[:, 512:1024], ones_row, zero_row,
                                start=True, stop=True))]
    for h in range(HEADS):
        ro = 64 * (h % 2)
        c0 = TT * (h // 2) + DH * (h % 2)
        def fz(eng, ro=ro, c0=c0):
            return eng.matmul(PS_ST[ro:ro + 64, c0:c0 + DH],
                              ones_row[0:1, 0:64], zero_row[0:1, 0:DH],
                              start=True, stop=True)
        _stz.append(fz)
    ev_stinit = pe(_stz, waits=[ev_consts], tag="prep.stzero")
    ev_membf = aop(lambda eng: eng.copy(out=membf, in_=PS_ST),
                   waits=[ev_stinit], writes=("membf",), tag="prep.stev")

    def row_to_fm(drow, total, dst_f32, dst_bf, chain, tag):
        col = 0
        last = chain
        GB.new_gen()
        while col < total:
            w = min(1024, total - col)
            evd = dma(rowf[0:1, 0:w], drow[0:1, col:col + w],
                      waits=[last], tag=tag + f".d{col}")
            evc = vop(lambda eng, w=w: eng.tensor_copy(
                out=rowbf[0:1, 0:w].bitcast(BF16), in_=rowf[0:1, 0:w]),
                waits=[evd], reads=("rowf",), writes=("rowbf",),
                tag=tag + f".c{col}")
            fns = []
            for c in range(w // TT):
                cc = col // TT + c
                st = GB.start(cc, cc + 1)
                def f(eng, c=c, cc=cc, st=st):
                    return eng.matmul(PS_B[:, cc:cc + 1],
                                      rowbf[0:1, TT * c:TT * (c + 1)].bitcast(BF16),
                                      one_one, start=st, stop=True)
                fns.append(f)
            last = pe(fns, waits=[evc, ev_consts], tag=tag + f".m{col}")
            col += w
        ncol = total // TT
        evs = last
        if dst_f32 is not None:
            evs = vop(lambda eng: eng.tensor_copy(out=dst_f32,
                                                  in_=PS_B[:, 0:ncol]),
                      waits=[last], writes=(dst_f32.name,), tag=tag + ".e")
        if dst_bf is not None:
            evs = aop(lambda eng: eng.copy(out=dst_bf, in_=PS_B[:, 0:ncol]),
                      waits=[last], writes=(dst_bf.name,), tag=tag + ".eb")
        return evs

    ev = row_to_fm(dlng, D, lngf, None, ev_consts, "prep.lng")
    ev = row_to_fm(dlnb, D, None, lnbf, ev, "prep.lnb")
    ev_b1 = row_to_fm(db1, DFF, b1fm, None, ev, "prep.b1")

    ev_bd = dma(stats[0:1, 0:16], dbeta, waits=[ev_b1], tag="prep.beta.d")
    ev_sig = aop(lambda eng: eng.activation(out=rowbf[0:1, 0:16].bitcast(BF16),
                                            in_=stats[0:1, 0:16],
                                            func=AF.Sigmoid),
                 waits=[ev_bd], writes=("rowbf",), tag="prep.beta.sig")
    GC.new_gen()
    ev_bmm = pe([lambda eng: eng.matmul(PS_C[:, 0:16], ones_row,
                                        rowbf[0:1, 0:16].bitcast(BF16),
                                        start=True, stop=True)],
                waits=[ev_sig], tag="prep.beta.mm")
    ev_beta = vop(lambda eng: eng.tensor_copy(out=beta_bc, in_=PS_C[:, 0:16]),
                  waits=[ev_bmm], writes=("beta_bc",), tag="prep.beta.e")
    ev_ombeta = vop(lambda eng: eng.tensor_scalar(
        out=ombeta_bc, in0=beta_bc, scalar1=-1.0, scalar2=1.0, op0=ALU.mult,
        op1=ALU.add), waits=[ev_beta], reads=("beta_bc",),
        writes=("ombeta_bc",), tag="prep.beta.om")

    ev_b2d = dma(rowf[0:1, 0:D], db2, waits=[ev_ombeta], tag="prep.b2.d")
    ev_b2c = vop(lambda eng: eng.tensor_copy(out=rowbf[0:1, 0:D].bitcast(BF16),
                                             in_=rowf[0:1, 0:D]),
                 waits=[ev_b2d], reads=("rowf",), writes=("rowbf",),
                 tag="prep.b2.c")
    GC.new_gen()
    ev_b2m = pe([lambda eng: eng.matmul(PS_C[:, 0:512], ones_row,
                                        rowbf[0:1, 0:512].bitcast(BF16),
                                        start=True, stop=True),
                 lambda eng: eng.matmul(PS_C[:, 512:1024], ones_row,
                                        rowbf[0:1, 512:1024].bitcast(BF16),
                                        start=True, stop=True)],
                waits=[ev_b2c, ev_beta], tag="prep.b2.mm")
    ev_b2 = vop(lambda eng: eng.tensor_copy(out=b2bc, in_=PS_C),
                waits=[ev_b2m], writes=("b2bc",), tag="prep.b2.e")

    GA.new_gen()
    bq_ps = {0: PS_A, 1: PS_C, 2: PS_B}
    ev_last = ev_b2
    ev_bq_mm = {0: None, 1: None, 2: None}
    for cch in range(8):
        for blk in range(3):
            evd = dma(scr_f, dwq[TT * cch:TT * (cch + 1),
                                 1024 * blk:1024 * (blk + 1)],
                      waits=[ev_last], tag=f"prep.wq{cch}.{blk}.d")
            evc = vop(lambda eng: eng.tensor_copy(out=scr_bf[:, 0:1024],
                                                  in_=scr_f),
                      waits=[evd], reads=("scr_f",), writes=("scr_bf",),
                      tag=f"prep.wq{cch}.{blk}.c")
            ps = bq_ps[blk]
            fns = []
            for g in range(2):
                def f(eng, g=g, ps=ps, cch=cch):
                    return eng.matmul(ps[0:1, 512 * g:512 * (g + 1)],
                                      lnbf[:, cch:cch + 1],
                                      scr_bf[:, 512 * g:512 * (g + 1)],
                                      start=(cch == 0), stop=(cch == 7))
                fns.append(f)
            ev_bq_mm[blk] = pe(fns, waits=[evc], tag=f"prep.bq{cch}.{blk}")
            dst = WQ[cch][:, 1024 * blk:1024 * (blk + 1)]
            if blk < 2:
                dst = dst.rearrange("p (h a j) -> p h a j", h=16, a=2,
                                    j=32).transpose([0, 1, 3, 2])
            def fw(eng, dst=dst, cch=cch):
                return eng.tensor_scalar(out=dst, in0=scr_f,
                                         scalar1=lngf[:, cch:cch + 1],
                                         scalar2=None, op0=ALU.mult)
            ev_last = vop(fw, waits=[ev_bq_mm[blk]], reads=("scr_f",),
                          writes=(f"WQ{cch}",), tag=f"prep.wqw{cch}.{blk}")
    ev_bq = ev_last
    for blk, ps in ((0, PS_A), (1, PS_C), (2, PS_B)):
        src = ps[0:1, 0:1024]
        dst = bq_row[0:1, 1024 * blk:1024 * (blk + 1)]
        if blk < 2:
            dst = dst.rearrange("p (h a j) -> p h a j", h=16, a=2,
                                j=32).transpose([0, 1, 3, 2])
        ev_bq = aop(lambda eng, dst=dst, src=src: eng.copy(out=dst, in_=src),
                    waits=[ev_last], writes=("bq_row",), tag=f"prep.bqe{blk}")

    def load_cast(pieces, piece_cols, chain, tag):
        last = chain
        for i, (dst, src) in enumerate(pieces):
            evd = dma(scr_f[:, 0:piece_cols], src, waits=[last],
                      tag=f"{tag}.d{i}")
            last = vop(lambda eng, dst=dst: eng.tensor_copy(
                out=dst, in_=scr_f[:, 0:piece_cols]),
                waits=[evd], reads=("scr_f",), writes=("wgt",),
                tag=f"{tag}.c{i}")
        return last

    if "xn2" in taps:
        dbq = nc.dram_tensor("tap_bq", [1, DQKV], BF16, kind="ExternalOutput").ap()
        dwq0 = nc.dram_tensor("tap_wq0", [TT, DQKV], BF16, kind="ExternalOutput").ap()
        dlngf = nc.dram_tensor("tap_lngf", [TT, 8], F32, kind="ExternalOutput").ap()
        dma(dbq, bq_row, waits=[ev_bq], sem="dmao", tag="prep.tapbq")
        dlnbf = nc.dram_tensor("tap_lnbf", [TT, 8], BF16, kind="ExternalOutput").ap()
        dma(dlnbf, lnbf, waits=[ev_bq], sem="dmao", tag="prep.taplnbf")
        dma(dwq0, WQ[0], waits=[ev_bq], sem="dmao", tag="prep.tapwq")
        dma(dlngf, lngf, waits=[ev_bq], sem="dmao", tag="prep.taplngf")
    ev_w = load_cast([(WO[c], dwo[TT * c:TT * (c + 1), :]) for c in range(8)],
                     D, ev_bq, "prep.wo")
    ev_w = load_cast([(U1w[c], dU1[TT * c:TT * (c + 1), :]) for c in range(8)],
                     RANK, ev_w, "prep.u1")
    pieces = []
    for r in range(2):
        for gp in range(4):
            pieces.append((V1w[r][:, 1024 * gp:1024 * (gp + 1)],
                           dV1[TT * r:TT * (r + 1),
                               1024 * gp:1024 * (gp + 1)]))
    ev_w = load_cast(pieces, 1024, ev_w, "prep.v1")
    ev_w = load_cast([(U2w[c], dU2[TT * c:TT * (c + 1), :])
                      for c in range(32)], RANK, ev_w, "prep.u2")
    ev_wlast = load_cast([(V2w[r], dV2[TT * r:TT * (r + 1), :])
                          for r in range(2)], D, ev_w, "prep.v2")

    P.barrier("prep")

    ev_x2_free = [None] * ST_TILES
    ev_state_delta = ev_stinit
    ev_zbd_prev = ev_zbdbf
    ev_x2_of, ev_kr_of = {}, {}
    st_ctx = {}

    for it in range(nt):
        ib = it % 2
        ic = it % 4
        t0 = it * TT
        tg = f"t{it}"
        if it < 2:
            ev_xd = ev_xd_pre[it]
        else:
            ev_xd = dma(xin[ib], dx[t0:t0 + TT, :], sem="dmax",
                        waits=list(prev_xd) + [ev_x2_of[it - 2]],
                        tag=tg + ".xd")
            prev_xd = [ev_xd]
        if it < 4:
            ev_cd = ev_cd_pre[it]
        else:
            ev_cd = dma(csb[ic], dcs[t0:t0 + TT, :], sem="dmac",
                        waits=list(prev_cd) + [ev_kr_of[it - 4]],
                        tag=tg + ".cd")
            prev_cd = [ev_cd]

        ev_rstd0 = ln_site(xin[ib], f"xin{ib}", 0, [ev_xd], True, tg + ".ln0")
        ev_xn2 = vop(lambda eng, ib=ib: eng.tensor_scalar(
            out=xn2, in0=xin[ib], scalar1=negmu[:, 0:1], scalar2=rstd[:, 0:1],
            op0=ALU.add, op1=ALU.mult),
            waits=[ev_rstd0], reads=(f"xin{ib}", "negmu", "rstd"),
            writes=("xn2",), tag=tg + ".xn2")

        ev_xn2f = transpose_set(xn2, xn2f, PS_A, 8, waits=[ev_xn2],
                                tag=tg + ".xn2T")

        qkv_ps = {0: PS_A, 1: PS_B, 2: PS_C}
        ev_qkv = {}
        for blk in range(3):
            ps = qkv_ps[blk]
            GEN[id(ps)].new_gen()
            fns = []
            for gidx in range(2):
                c0 = 512 * gidx
                def fb(eng, blk=blk, c0=c0, ps=ps):
                    return eng.matmul(
                        ps[:, c0:c0 + 512], ones_row,
                        bq_row[0:1, 1024 * blk + c0:1024 * blk + c0 + 512],
                        start=True, stop=False)
                fns.append(fb)
                for cch in range(8):
                    def fm_(eng, blk=blk, cch=cch, c0=c0, ps=ps):
                        return eng.matmul(
                            ps[:, c0:c0 + 512],
                            xn2f[:, TT * cch:TT * (cch + 1)],
                            WQ[cch][:, 1024 * blk + c0:1024 * blk + c0 + 512],
                            start=False, stop=(cch == 7))
                    fns.append(fm_)
            ev_qkv[blk] = pe(fns, waits=[ev_xn2f], tag=tg + f".qkv{blk}")
        ev_qe = aop(lambda eng: eng.copy(out=q_bf, in_=PS_A),
                    waits=[ev_qkv[0]], writes=("q_bf",), tag=tg + ".qe")
        ev_ke = aop(lambda eng: eng.copy(out=k_bf, in_=PS_B),
                    waits=[ev_qkv[1]], writes=("k_bf",), tag=tg + ".ke")
        ev_ve = aop(lambda eng: eng.copy(out=v_bf, in_=PS_C),
                    waits=[ev_qkv[2]], writes=("v_bf",), tag=tg + ".ve")

        cc_b = csb[ic][:, 0:32].unsqueeze(1).broadcast_to([TT, 16, 32])
        ss_b = csb[ic][:, 32:64].unsqueeze(1).broadcast_to([TT, 16, 32])
        def rope(src, dst, ev_src, ev_scr_free, tagr):
            v3 = src.rearrange("p (h a j) -> p (h a) j", a=2, j=32)
            d3 = dst.rearrange("p (h a j) -> p (h a) j", a=2, j=32)
            se, so = v3[:, 0::2, :], v3[:, 1::2, :]
            de, do = d3[:, 0::2, :], d3[:, 1::2, :]
            tA = scr_bf[:, 0:512].rearrange("p (h j) -> p h j", j=32)
            tB = scr_bf[:, 512:1024].rearrange("p (h j) -> p h j", j=32)
            dnop = lambda t: vop(lambda eng: eng.engine_nop(), tag=t)
            e1 = vop(lambda eng: eng.tensor_tensor(out=tA, in0=se, in1=cc_b,
                                                   op=ALU.mult),
                     waits=[ev_src, ev_cd, ev_scr_free], reads=(src.name,),
                     writes=("scr_bf",), tag=tagr + ".1")
            e2 = vop(lambda eng: eng.tensor_tensor(out=tB, in0=so, in1=ss_b,
                                                   op=ALU.mult),
                     waits=[ev_src, ev_cd, ev_scr_free], reads=(src.name,),
                     writes=("scr_bf",), tag=tagr + ".2")
            dnop(tagr + ".n2")
            e5 = vop(lambda eng: eng.tensor_tensor(out=de, in0=tA, in1=tB,
                                                   op=ALU.subtract),
                     waits=[e2], reads=("scr_bf",), writes=(dst.name,),
                     tag=tagr + ".5")
            dnop(tagr + ".n5")
            e3 = vop(lambda eng: eng.tensor_tensor(out=tA, in0=so, in1=cc_b,
                                                   op=ALU.mult),
                     reads=(src.name,), writes=("scr_bf",), tag=tagr + ".3")
            e4 = vop(lambda eng: eng.tensor_tensor(out=tB, in0=se, in1=ss_b,
                                                   op=ALU.mult),
                     waits=[e5], reads=(src.name,), writes=("scr_bf",),
                     tag=tagr + ".4")
            dnop(tagr + ".n4")
            e6 = vop(lambda eng: eng.tensor_tensor(out=do, in0=tA, in1=tB,
                                                   op=ALU.add),
                     waits=[e4], reads=("scr_bf",), writes=(dst.name,),
                     tag=tagr + ".6")
            dnop(tagr + ".n6")
            return e6
        ev_qr = rope(q_bf, qr, ev_qe, None, tg + ".ropeq")
        ev_tapqr = None
        if "bqr" in dtap and it == 0:
            def ftq(eng):
                return eng.dma_start(out=dtap["bqr"][0:TT, :], in_=qr)
            ev_tapqr = P.step("sync", ftq, waits=[ev_qr], inc="dmao",
                              inc_amt=16, tag=tg + ".bqr")
        ev_kr = rope(k_bf, kr, ev_ke, ev_qr, tg + ".ropek")
        ev_kr_of[it] = ev_kr
        if "canq" in dtap and it == 0:
            ecq = vop(lambda eng: eng.tensor_copy(out=canq, in_=qr),
                      waits=[ev_qr], reads=("qr",), writes=("canq",),
                      tag=tg + ".canq")
            ecs2 = aop(lambda eng: eng.copy(out=cancs, in_=csb[0]),
                       waits=[ev_cd], writes=("cancs",), tag=tg + ".cancs")
            ecqb = gop(lambda eng: eng.tensor_copy(out=canqb, in_=q_bf),
                       waits=[ev_qe], reads=("q_bf",), writes=("canqb",),
                       tag=tg + ".canqb")
            tap("canq", ecq, 0)
            tap("cancs", ecs2, 0)
            tap("canqb", ecqb, 0)

        escr = scr_bf[:, 0:1024]
        def elu(src, dst, ev_src, waits_m, tagr):
            e1 = aop(lambda eng: eng.activation(out=escr, in_=src,
                                                func=AF.Exp),
                     waits=[ev_src] + waits_m, reads=(src.name,),
                     writes=("scr_bf",), tag=tagr + ".e")
            e2 = vop(lambda eng: eng.tensor_scalar(out=escr, in0=escr,
                                                   scalar1=1.0, scalar2=None,
                                                   op0=ALU.min),
                     waits=[e1], reads=("scr_bf",), writes=("scr_bf",),
                     tag=tagr + ".m")
            e3 = vop(lambda eng: eng.scalar_tensor_tensor(
                out=dst, in0=src, scalar=0.0, in1=escr, op0=ALU.max,
                op1=ALU.add), waits=[e2], reads=(src.name, "scr_bf"),
                writes=(dst.name,), tag=tagr + ".s")
            return e3
        ev_SQ = elu(q_bf, SQm, ev_qr, [ev_kr], tg + ".eluq")
        ev_SK = elu(k_bf, SKm, ev_kr, [ev_SQ], tg + ".eluk")

        wq_settle, wk_settle = [], []
        if it < 2:
            wq_settle = [ev_kr]
            ev_kset = vop(lambda eng: eng.tensor_copy(out=kr[:, 0:64],
                                                      in_=kr[:, 0:64]),
                          waits=[ev_kr], reads=("kr",), writes=("kr",),
                          tag=tg + ".kset")
            wk_settle = [ev_kset]
        ev_qrf = transpose_set(qr, qrf, PS_B, 8,
                               waits=[ev_qr, ev_ke, ev_tapqr] + wq_settle,
                               tag=tg + ".qrT")
        ev_krz = transpose_zsplit(kr, krz, PS_C,
                                  waits=[ev_kr, ev_ve] + wk_settle,
                                  tag=tg + ".krT")
        ev_SQz = transpose_zsplit(SQm, SQz, PS_A, waits=[ev_SQ, ev_qe],
                                  tag=tg + ".sqT")
        ev_SKf = transpose_set(SKm, SKf, PS_B, 8, waits=[ev_SK, ev_qrf],
                               tag=tg + ".skT")

        ev_nms_r = [None, None]
        ev_nm_r = [None, None]
        for r, ps in ((0, PS_A), (1, PS_C)):
            g = GEN[id(ps)]
            g.new_gen()
            fns = []
            for hh in range(8):
                h = 8 * r + hh
                st = g.start(TT * hh, TT * (hh + 1))
                def f(eng, h=h, hh=hh, ps=ps, st=st):
                    return eng.matmul(ps[:, TT * hh:TT * (hh + 1)],
                                      qrf[:, TT * (h // 2):TT * (h // 2) + TT],
                                      krz[:, TT * h:TT * (h + 1)],
                                      start=st, stop=True)
                fns.append(f)
            w = [ev_qrf, ev_krz] + ([ev_SQz] if r == 0 else [])
            ev_mm = pe(fns, waits=w, tag=tg + f".sc{r}")
            ev_nm_r[r] = vop(lambda eng, ps=ps, r=r: eng.tensor_reduce(
                out=negmax[:, 8 * r:8 * r + 8],
                in_=ps.rearrange("p (h k) -> p h k", k=TT),
                axis=AX.X, op=ALU.max, negate=True),
                waits=[ev_mm], writes=("negmax",), tag=tg + f".nm{r}")
            ev_nms_r[r] = vop(lambda eng, r=r: eng.tensor_scalar(
                out=negmax[:, 8 * r:8 * r + 8],
                in0=negmax[:, 8 * r:8 * r + 8], scalar1=SCALE, scalar2=None,
                op0=ALU.mult), waits=[ev_nm_r[r]], reads=("negmax",),
                writes=("negmax",), tag=tg + f".nms{r}")
        ev_P = [None, None]
        ev_pmsk_r = [None, None]
        cb3 = causal.unsqueeze(1).broadcast_to([TT, 8, TT])
        for r, ps in ((0, PS_A), (1, PS_C)):
            evp = None
            for hh in range(8):
                h = 8 * r + hh
                def fe(eng, h=h, hh=hh, ps=ps):
                    return eng.activation(out=Pbuf[:, TT * h:TT * (h + 1)],
                                          in_=ps[:, TT * hh:TT * (hh + 1)],
                                          func=AF.Exp, bias=negmax[:, h:h + 1],
                                          scale=SCALE)
                if hh == 0:
                    ws = ([ev_nms_r[0], ev_nm_r[1]] if r == 0
                          else [ev_nms_r[1], ev_pmsk_r[0]])
                else:
                    ws = ()
                evp = aop(fe, waits=ws, reads=("negmax",), writes=("Pbuf",),
                          tag=tg + f".exp{h}")
            ev_P[r] = evp
            ev_pmsk_r[r] = vop(lambda eng, r=r: eng.tensor_tensor(
                out=Pm[:, 1024 * r:1024 * (r + 1)].rearrange(
                    "p (h k) -> p h k", k=TT),
                in0=Pbuf[:, 1024 * r:1024 * (r + 1)].rearrange(
                    "p (h k) -> p h k", k=TT), in1=cb3,
                op=ALU.mult), waits=[ev_P[r]], reads=("Pbuf",),
                writes=("Pm",), tag=tg + f".pmsk{r}")
        ev_tappb = None
        if "bpb" in dtap and it == 0:
            def ftp(eng):
                return eng.dma_start(out=dtap["bpb"][0:TT, :], in_=Pbuf)
            ev_tappb = P.step("sync", ftp, waits=[ev_P[0], ev_P[1]],
                              inc="dmao", inc_amt=16, tag=tg + ".bpb")
        ev_ttr = vop(lambda eng: eng.tensor_reduce(
            out=lsum, in_=Pm.rearrange("p (h k) -> p h k", k=TT), axis=AX.X,
            op=ALU.add), waits=[ev_pmsk_r[1]], reads=("Pm",),
            writes=("lsum",), tag=tg + ".lsum")
        ev_rl = vop(lambda eng: eng.reciprocal(out=rl, in_=lsum),
                    waits=[ev_ttr], reads=("lsum",), writes=("rl",),
                    tag=tg + ".rl")
        ev_rl2 = vop(lambda eng: eng.tensor_tensor(out=rl, in0=rl,
                                                   in1=ombeta_bc,
                                                   op=ALU.mult),
                     waits=[ev_rl], reads=("rl",), writes=("rl",),
                     tag=tg + ".rl2")

        ev_PT = [None, None]
        for r, ps in ((0, PS_A), (1, PS_C)):
            g = GEN[id(ps)]
            g.new_gen()
            fns = []
            for hh in range(8):
                h = 8 * r + hh
                st = g.start(TT * hh, TT * (hh + 1))
                def f(eng, h=h, hh=hh, ps=ps, st=st):
                    return eng.matmul(ps[:, TT * hh:TT * (hh + 1)],
                                      Pm[:, TT * h:TT * (h + 1)], ident,
                                      start=st, stop=True)
                fns.append(f)
            ev_mm = pe(fns, waits=[ev_ttr], tag=tg + f".pT{r}")
            def fe(eng, r=r, ps=ps):
                return eng.copy(out=PTb[:, 1024 * r:1024 * (r + 1)],
                                in_=ps[:, 0:1024])
            ev_PT[r] = aop(fe, waits=[ev_mm, ev_tappb], writes=("Pbuf",),
                           tag=tg + f".pTe{r}")

        GB.new_gen()
        fns = []
        for h in range(HEADS):
            st = GB.start(DH * h, DH * (h + 1))
            def f(eng, h=h, st=st):
                return eng.matmul(PS_B[:, DH * h:DH * (h + 1)],
                                  PTb[:, TT * h:TT * (h + 1)],
                                  v_bf[:, DH * h:DH * (h + 1)],
                                  start=st, stop=True)
            fns.append(f)
        ev_sdpa = pe(fns, waits=[ev_PT[0], ev_PT[1], ev_ve, ev_SKf],
                     tag=tg + ".sdpa")
        ev_omix1 = vop(lambda eng: eng.tensor_tensor(
            out=scr_f.rearrange("p (h d) -> p h d", d=DH),
            in0=PS_B.rearrange("p (h d) -> p h d", d=DH), in1=bcast16(rl),
            op=ALU.mult), waits=[ev_sdpa, ev_rl2], reads=("rl",),
            writes=("scr_f",), tag=tg + ".omix1")

        GA.new_gen()
        ev_zp = pe([
            lambda eng: eng.matmul(PS_A[:, 0:512], maskup, SKm[:, 0:512],
                                   start=True, stop=True),
            lambda eng: eng.matmul(PS_A[:, 512:1024], maskup, SKm[:, 512:1024],
                                   start=True, stop=True)],
            waits=[ev_SK, ev_PT[0]], tag=tg + ".zp")
        GC.new_gen()
        fns = []
        for c in range(8):
            def f(eng, c=c):
                return eng.matmul(PS_C[:, 992 + 2 * c:994 + 2 * c],
                                  SKf[:, TT * c:TT * (c + 1)],
                                  zbd_bf[:, 2 * c:2 * c + 2],
                                  start=True, stop=True)
            fns.append(f)
        for h in range(HEADS):
            def f(eng, h=h):
                return eng.matmul(PS_C[:, 976 + h:977 + h],
                                  SQz[:, TT * h:TT * (h + 1)],
                                  zbd_bf[:, h:h + 1],
                                  start=True, stop=True)
            fns.append(f)
        ev_z0 = pe(fns, waits=[ev_SKf, ev_SQz, ev_zbd_prev, ev_PT[1]],
                   tag=tg + ".z0")
        ev_dp = vop(lambda eng: eng.tensor_tensor(
            out=scr_bf[:, 0:1024], in0=SKm, in1=PS_A, op=ALU.mult),
            waits=[ev_zp, ev_kr, ev_qr], reads=("SKm",), writes=("scr_bf",),
            tag=tg + ".dp")
        ev_dr = vop(lambda eng: eng.tensor_reduce(
            out=den, in_=scr_bf[:, 0:1024].rearrange("p (h d) -> p h d",
                                                     d=DH),
            axis=AX.X, op=ALU.add), waits=[ev_dp], reads=("scr_bf",),
            writes=("den",), tag=tg + ".dr")
        ev_den = vop(lambda eng: eng.tensor_tensor(
            out=den, in0=den, in1=PS_C[:, 992:1008], op=ALU.add),
            waits=[ev_dr, ev_z0], reads=("den",), writes=("den",),
            tag=tg + ".den")
        ev_qp = vop(lambda eng: eng.tensor_tensor(
            out=scr_bf[:, 0:1024], in0=SQm, in1=PS_A, op=ALU.mult),
            waits=[ev_zp, ev_dr], reads=("SQm",), writes=("scr_bf",),
            tag=tg + ".qp")
        ev_qred = vop(lambda eng: eng.tensor_reduce(
            out=denq, in_=scr_bf[:, 0:1024].rearrange(
                "p (h d) -> p h d", d=DH), axis=AX.X, op=ALU.add),
            waits=[ev_qp], reads=("scr_bf",), writes=("denq",),
            tag=tg + ".qred")
        ev_denq = vop(lambda eng: eng.scalar_tensor_tensor(
            out=denq, in0=PS_C[:, 976:992], scalar=1e-6, in1=denq,
            op0=ALU.add, op1=ALU.add), waits=[ev_qred, ev_z0],
            reads=("denq",), writes=("denq",), tag=tg + ".denq")
        ev_rden = vop(lambda eng: eng.reciprocal(out=rden, in_=den),
                      waits=[ev_den], reads=("den",), writes=("rden",),
                      tag=tg + ".rden")
        ev_rdq0 = vop(lambda eng: eng.reciprocal(out=rdq, in_=denq),
                      waits=[ev_denq], reads=("denq",), writes=("rdq",),
                      tag=tg + ".rdq0")
        ev_rdq = vop(lambda eng: eng.tensor_tensor(out=rdq, in0=rdq,
                                                   in1=beta_bc, op=ALU.mult),
                     waits=[ev_rdq0], reads=("rdq",), writes=("rdq",),
                     tag=tg + ".rdq")
        ev_skt = vop(lambda eng: eng.tensor_tensor(
            out=SKt.rearrange("p (h d) -> p h d", d=DH),
            in0=SKm.rearrange("p (h d) -> p h d", d=DH),
            in1=bcast16(rden), op=ALU.mult),
            waits=[ev_rden], reads=("SKm", "rden"), writes=("SKt",),
            tag=tg + ".skt")
        ev_sktz = transpose_zsplit(SKt, SKtz, PS_A,
                                   waits=[ev_skt, ev_dp, ev_qp],
                                   tag=tg + ".sktT")

        mupb = maskup.unsqueeze(1).broadcast_to([TT, 8, TT])
        ev_KT = [None, None]
        for r, ps in ((0, PS_B), (1, PS_C)):
            g = GEN[id(ps)]
            g.new_gen()
            fns = []
            for hh in range(8):
                h = 8 * r + hh
                st = g.start(TT * hh, TT * (hh + 1))
                def f(eng, h=h, hh=hh, ps=ps, st=st):
                    return eng.matmul(ps[:, TT * hh:TT * (hh + 1)],
                                      SKf[:, TT * (h // 2):TT * (h // 2) + TT],
                                      SKtz[:, TT * h:TT * (h + 1)],
                                      start=st, stop=True)
                fns.append(f)
            w = [ev_sktz, ev_omix1 if r == 0 else ev_denq]
            if r == 1:
                w.append(ev_den)
            ev_mm = pe(fns, waits=w, tag=tg + f".kt{r}")
            def fe(eng, r=r, ps=ps):
                return eng.tensor_tensor(
                    out=KTm[:, 1024 * r:1024 * (r + 1)].rearrange(
                        "p (h k) -> p h k", k=TT),
                    in0=ps.rearrange("p (h k) -> p h k", k=TT),
                    in1=mupb, op=ALU.mult)
            ev_KT[r] = vop(fe, waits=[ev_mm], writes=("KTm",),
                           tag=tg + f".kte{r}")
        ev_Sq = [None, None]
        for r, ps in ((0, PS_B), (1, PS_C)):
            g = GEN[id(ps)]
            g.new_gen()
            fns = []
            for hh in range(8):
                h = 8 * r + hh
                st = g.start(TT * hh, TT * (hh + 1))
                def f(eng, h=h, hh=hh, ps=ps, st=st):
                    return eng.matmul(ps[:, TT * hh:TT * (hh + 1)],
                                      SKf[:, TT * (h // 2):TT * (h // 2) + TT],
                                      SQz[:, TT * h:TT * (h + 1)],
                                      start=st, stop=True)
                fns.append(f)
            ev_mm = pe(fns, waits=[ev_KT[r]], tag=tg + f".sqm{r}")
            def fe(eng, r=r, ps=ps):
                return eng.tensor_tensor(
                    out=SmU[:, 1024 * r:1024 * (r + 1)].rearrange(
                        "p (h k) -> p h k", k=TT),
                    in0=ps.rearrange("p (h k) -> p h k", k=TT),
                    in1=mupb, op=ALU.mult)
            ev_Sq[r] = vop(fe, waits=[ev_mm], writes=("SmU",),
                           tag=tg + f".sqe{r}")

        GA.new_gen()
        fns = []
        for h in range(HEADS):
            st = GA.start(DH * h, DH * (h + 1))
            mcol = TT * (h // 2) + DH * (h % 2)
            def f(eng, h=h, st=st, mcol=mcol):
                return eng.matmul(PS_A[:, DH * h:DH * (h + 1)],
                                  SKtz[:, TT * h:TT * (h + 1)],
                                  membf[:, mcol:mcol + DH],
                                  start=st, stop=True)
            fns.append(f)
        ev_pv = pe(fns, waits=[ev_sktz, ev_membf], tag=tg + ".pv")
        ev_b = vop(lambda eng: eng.scalar_tensor_tensor(
            out=bvec, in0=PS_A, scalar=-1.0, in1=v_bf, op0=ALU.mult,
            op1=ALU.add), waits=[ev_pv, ev_ve], writes=("bvec",),
            tag=tg + ".b")
        ev_bbf = gop(lambda eng: eng.tensor_copy(out=b_bf, in_=bvec),
                     waits=[ev_b], reads=("bvec",), writes=("b_bf",),
                     tag=tg + ".bbf")

        prev_reader = {id(PS_B): ev_Sq[0], id(PS_C): ev_Sq[1]}
        g_in, ev_in = b_bf, ev_bbf
        ev_u = None
        ev_h3_mm = None
        for i3, (ps, dst) in enumerate(((PS_B, g1), (PS_C, g2),
                                        (PS_B, uvec))):
            g = GEN[id(ps)]
            g.new_gen()
            fns = []
            for h in range(HEADS):
                st = g.start(DH * h, DH * (h + 1))
                def f(eng, h=h, ps=ps, g_in=g_in, st=st):
                    return eng.matmul(ps[:, DH * h:DH * (h + 1)],
                                      KTm[:, TT * h:TT * (h + 1)],
                                      g_in[:, DH * h:DH * (h + 1)],
                                      start=st, stop=True)
                fns.append(f)
            ev_mm = pe(fns, waits=[ev_in, prev_reader[id(ps)], ev_KT[1]],
                       tag=tg + f".h{i3}")
            ev_h3_mm = ev_mm
            def fe(eng, ps=ps, dst=dst):
                return eng.scalar_tensor_tensor(out=dst, in0=ps, scalar=-1.0,
                                                in1=bvec, op0=ALU.mult,
                                                op1=ALU.add)
            evg = vop(fe, waits=[ev_mm], reads=("bvec",), writes=(dst.name,),
                      tag=tg + f".g{i3}")
            prev_reader[id(ps)] = evg
            g_in, ev_in = dst, evg
        ev_u = ev_in

        GC.new_gen()
        fns = []
        for h in range(HEADS):
            GC.start(DH * h, DH * (h + 1))
            mcol = TT * (h // 2) + DH * (h % 2)
            def f(eng, h=h, mcol=mcol):
                return eng.matmul(PS_C[:, DH * h:DH * (h + 1)],
                                  SQz[:, TT * h:TT * (h + 1)],
                                  membf[:, mcol:mcol + DH],
                                  start=True, stop=False)
            fns.append(f)
            def f2(eng, h=h):
                return eng.matmul(PS_C[:, DH * h:DH * (h + 1)],
                                  SmU[:, TT * h:TT * (h + 1)],
                                  uvec[:, DH * h:DH * (h + 1)],
                                  start=False, stop=True)
            fns.append(f2)
        ev_num = pe(fns, waits=[ev_u, ev_membf, prev_reader[id(PS_C)]],
                    tag=tg + ".num")
        ev_omix2 = vop(lambda eng: eng.tensor_tensor(
            out=o_acc.rearrange("p (h d) -> p h d", d=DH),
            in0=PS_C.rearrange("p (h d) -> p h d", d=DH), in1=bcast16(rdq),
            op=ALU.mult), waits=[ev_num, ev_rdq], reads=("rdq",),
            writes=("o_acc",), tag=tg + ".omix2")
        ev_o = gop(lambda eng: eng.tensor_tensor(out=o_bf, in0=o_acc,
                                                 in1=scr_f, op=ALU.add),
                   waits=[ev_omix2, ev_omix1], reads=("o_acc", "scr_f"),
                   writes=("q_bf",), tag=tg + ".o")

        fns = []
        for h in range(HEADS):
            def f(eng, h=h):
                ro = 64 * (h % 2)
                c0 = TT * (h // 2) + DH * (h % 2)
                return eng.matmul(PS_ST[ro:ro + 64, c0:c0 + DH],
                                  SKm[:, DH * h:DH * (h + 1)],
                                  uvec[:, DH * h:DH * (h + 1)],
                                  start=False, stop=True)
            fns.append(f)
        for h in range(HEADS):
            def f(eng, h=h):
                ro = 64 * (h % 2)
                return eng.matmul(PS_B[ro:ro + 64,
                                       1016 + (h // 2):1017 + (h // 2)],
                                  SKm[:, DH * h:DH * (h + 1)], onesc,
                                  start=True, stop=True)
            fns.append(f)
        ev_dmem = pe(fns, waits=[ev_u, ev_state_delta, ev_membf,
                                 prev_reader[id(PS_B)]],
                     tag=tg + ".dmem")
        ev_state_delta = ev_dmem
        zview_e = zbd[0:64, :].rearrange("p (c a) -> p a c", a=2)[:, 0, :]
        zview_o = zbd[64:128, :].rearrange("p (c a) -> p a c", a=2)[:, 1, :]
        ev_z1 = vop(lambda eng: eng.tensor_tensor(
            out=zview_e, in0=zview_e, in1=PS_B[0:64, 1016:1024], op=ALU.add),
            waits=[ev_dmem], reads=("zbd",), writes=("zbd",), tag=tg + ".z1")
        ev_z2 = vop(lambda eng: eng.tensor_tensor(
            out=zview_o, in0=zview_o, in1=PS_B[64:128, 1016:1024],
            op=ALU.add), waits=[ev_dmem], reads=("zbd",), writes=("zbd",),
            tag=tg + ".z2")
        ev_zbd_prev = vop(lambda eng: eng.tensor_copy(out=zbd_bf, in_=zbd),
                          waits=[ev_z1, ev_z2], reads=("zbd",),
                          writes=("zbd_bf",), tag=tg + ".zbf")
        ev_membf = aop(lambda eng: eng.copy(out=membf, in_=PS_ST),
                       waits=[ev_dmem], writes=("membf",), tag=tg + ".stev")

        ev_ofm = transpose_set(o_bf, o_fm, PS_B, 8,
                               waits=[ev_o, ev_z1, ev_z2], tag=tg + ".oT")
        GC.new_gen()
        fns = []
        for gidx in range(2):
            c0 = 512 * gidx
            for cch in range(8):
                def f(eng, cch=cch, c0=c0):
                    return eng.matmul(PS_C[:, c0:c0 + 512],
                                      o_fm[:, TT * cch:TT * (cch + 1)],
                                      WO[cch][:, c0:c0 + 512],
                                      start=(cch == 0), stop=(cch == 7))
                fns.append(f)
        ev_womm = pe(fns, waits=[ev_ofm, ev_omix2], tag=tg + ".wo")
        ev_ao = vop(lambda eng: eng.scalar_tensor_tensor(
            out=ao, in0=PS_C, scalar=0.0,
            in1=zerocol.broadcast_to([TT, D]), op0=ALU.add, op1=ALU.add,
            accum_out=sumv), waits=[ev_womm, ev_bbf],
            writes=("bvec", "sumv"), tag=tg + ".ao")
        ev_aosq = aop(lambda eng: eng.activation(out=sq_scr, in_=ao,
                                                 func=AF.Square,
                                                 accum_out=sumsq),
                      waits=[ev_ao], reads=("bvec",),
                      writes=("sq_scr", "sumsq"), tag=tg + ".aosq")
        P.step("gpsimd", lambda eng: eng.engine_nop(),
               waits=[ev_ao, ev_aosq], tag=tg + ".lnw")
        ln_stats_tail(1, tg + ".ln1")
        ev_rstd1 = rsqrt_chain(stats[:, 1:2], rstd[:, 1:2], tag=tg + ".ln1r")
        ev_t1 = vop(lambda eng: eng.tensor_scalar(
            out=scr_f, in0=ao, scalar1=negmu[:, 1:2], scalar2=rstd[:, 1:2],
            op0=ALU.add, op1=ALU.mult),
            waits=[ev_rstd1, ev_o], reads=("bvec", "negmu", "rstd"),
            writes=("scr_f",), tag=tg + ".lnap1")
        jslot = it % ST_TILES
        ev_x2 = vop(lambda eng, ib=ib, jslot=jslot: eng.tensor_tensor(
            out=x2[jslot], in0=scr_f, in1=xin[ib], op=ALU.add),
            waits=[ev_t1, ev_x2_free[jslot]], reads=("scr_f", f"xin{ib}"),
            writes=(f"x2_{jslot}",), tag=tg + ".x2")
        ev_x2_of[it] = ev_x2

        ev_rstd2 = ln_site(x2[jslot], f"x2_{jslot}", 2,
                           [ev_x2, ev_aosq], False, tg + ".ln2")
        ev_xn3 = vop(lambda eng, jslot=jslot: eng.tensor_scalar(
            out=xn3, in0=x2[jslot], scalar1=negmu[:, 2:3],
            scalar2=rstd[:, 2:3], op0=ALU.add, op1=ALU.mult),
            waits=[ev_rstd2], reads=(f"x2_{jslot}", "negmu", "rstd"),
            writes=("xn3",), tag=tg + ".xn3")
        ev_xn3f = transpose_set(xn3, xn3f, PS_B, 8, waits=[ev_xn3, ev_ofm],
                                tag=tg + ".xn3T")
        GC.new_gen()
        fns = []
        for cch in range(8):
            def f(eng, cch=cch):
                return eng.matmul(PS_C[:, 0:256],
                                  xn3f[:, TT * cch:TT * (cch + 1)],
                                  U1w[cch], start=(cch == 0), stop=(cch == 7))
            fns.append(f)
        ev_u1mm = pe(fns, waits=[ev_xn3f, ev_ao], tag=tg + ".u1")
        ev_p2 = aop(lambda eng: eng.copy(out=P2, in_=PS_C[:, 0:256]),
                    waits=[ev_u1mm], writes=("P2",), tag=tg + ".p2")
        fns = []
        for rc in range(2):
            def f(eng, rc=rc):
                return eng.matmul(PS_C[:, 256 + TT * rc:256 + TT * (rc + 1)],
                                  P2[:, TT * rc:TT * (rc + 1)], ident,
                                  start=True, stop=True)
            fns.append(f)
        ev_p2t = pe(fns, waits=[ev_p2], tag=tg + ".p2T")
        p2dst = P2fw.rearrange("p (r t k) -> p r t k", r=2, t=ST_TILES)
        def fe(eng, jslot=jslot):
            return eng.copy(out=p2dst[:, :, jslot, :],
                            in_=PS_C[:, 256:512].rearrange(
                                "p (r k) -> p r k", r=2))
        ev_p2fw = aop(fe, waits=[ev_p2t], writes=("P2fw",), tag=tg + ".p2fw")

        tap("xn2", ev_xn2, t0)
        tap("q", ev_qe, t0)
        tap("k", ev_ke, t0)
        tap("v", ev_ve, t0)
        tap("qqr", ev_qr, t0)
        tap("kkr", ev_kr, t0)
        tap("sq", ev_SQ, t0)
        tap("sk", ev_SK, t0)
        tap("u", ev_u, t0)
        tap("o", ev_o, t0)
        tap("aao", ev_ao, t0)
        tap("den", ev_den, t0)
        tap("denq", ev_denq, t0)
        tap("mem", ev_membf, t0)
        tap("lsum", ev_ttr, t0)
        tap("b", ev_b, t0)
        tap("ktm", ev_KT[1], t0)
        if jslot == 0:
            tap("x2_0", ev_x2, t0)
        tap("pm", ev_ttr, t0)
        tap("negmax", ev_ttr, t0)

        tap("xn3t", ev_xn3, t0)
        st_ctx[jslot] = dict(ev_p2fw=ev_p2fw, it=it)
        if jslot != ST_TILES - 1 and it != nt - 1:
            P.barrier(tg)
            continue

        nslots = jslot + 1
        ev_allp2 = [st_ctx[j]["ev_p2fw"] for j in range(nslots)]
        for _ev in ev_allp2:
            tap("p2f", _ev, 0)
        GA.new_gen()
        GB.new_gen()
        ev_gel = [None, None]
        ev_q2mm = None
        for fc in range(32):
            pp = fc % 2
            hb = Hbuf[pp]
            c0 = 512 * pp
            fns = []
            for r2 in range(2):
                def f(eng, r2=r2, fc=fc, c0=c0):
                    return eng.matmul(PS_A[:, c0:c0 + TT * nslots],
                                      V1w[r2][:, TT * fc:TT * (fc + 1)],
                                      P2fw[:, 512 * r2:512 * r2 + TT * nslots],
                                      start=(r2 == 0), stop=(r2 == 1))
                fns.append(f)
            if fc < 2:
                w = list(ev_allp2) + [ev_b, ev_pv]
            else:
                w = [ev_gel[pp]]
            ev_v1mm = pe(fns, waits=w, tag=tg + f".v1_{fc}")
            def fg(eng, fc=fc, hb=hb, c0=c0):
                return eng.activation(out=hb[:, 0:TT * nslots],
                                      in_=PS_A[:, c0:c0 + TT * nslots],
                                      func=AF.Gelu_apprx_tanh,
                                      bias=b1fm[:, fc:fc + 1])
            ev_g = aop(fg, waits=[ev_v1mm], writes=(hb.name,),
                       tag=tg + f".gel{fc}")
            fns = []
            if fc == 0:
                for j in range(nslots):
                    GB.start(RANK * j, RANK * (j + 1))
                    def fz(eng, j=j):
                        return eng.matmul(PS_B[:, RANK * j:RANK * (j + 1)],
                                          ones_row,
                                          zero_row[0:1, 0:RANK],
                                          start=True, stop=False)
                    fns.append(fz)
            for j in range(nslots):
                def f(eng, j=j, fc=fc, hb=hb):
                    return eng.matmul(PS_B[:, RANK * j:RANK * (j + 1)],
                                      hb[:, TT * j:TT * (j + 1)], U2w[fc],
                                      start=False, stop=(fc == 31))
                fns.append(f)
            w = [ev_g]
            if fc == 0:
                w += [ev_dmem, ev_z1, ev_z2, ev_xn3f]
            ev_q2mm = pe(fns, waits=w, tag=tg + f".u2_{fc}")
            ev_gel[pp] = ev_q2mm

        ev_prev_tail = None
        for j in reversed(range(nslots)):
            jt = st_ctx[j]["it"]
            jt0 = jt * TT
            ttg = f"t{jt}.ffn"
            ev_q2 = aop(lambda eng, j=j: eng.copy(
                out=Q2, in_=PS_B[:, RANK * j:RANK * (j + 1)]),
                waits=[ev_q2mm, ev_prev_tail], writes=("Q2",),
                tag=ttg + ".q2")
            GC.new_gen()
            fns = []
            for rc in range(2):
                st = GC.start(TT * rc, TT * (rc + 1))
                def f(eng, rc=rc, st=st):
                    return eng.matmul(PS_C[:, TT * rc:TT * (rc + 1)],
                                      Q2[:, TT * rc:TT * (rc + 1)], ident,
                                      start=st, stop=True)
                fns.append(f)
            ev_q2t = pe(fns, waits=[ev_q2, ev_p2fw], tag=ttg + ".q2T")
            ev_q2f = aop(lambda eng: eng.copy(out=Q2f, in_=PS_C[:, 0:256]),
                         waits=[ev_q2t], writes=("Q2f",), tag=ttg + ".q2f")
            GC.new_gen()
            fns = []
            for gidx in range(2):
                c0 = 512 * gidx
                for rc in range(2):
                    def f(eng, rc=rc, c0=c0):
                        return eng.matmul(PS_C[:, c0:c0 + 512],
                                          Q2f[:, TT * rc:TT * (rc + 1)],
                                          V2w[rc][:, c0:c0 + 512],
                                          start=(rc == 0), stop=(rc == 1))
                    fns.append(f)
            ev_v2 = pe(fns, waits=[ev_q2f], tag=ttg + ".v2")
            ev_f = vop(lambda eng: eng.scalar_tensor_tensor(
                out=fout, in0=PS_C, scalar=0.0, in1=b2bc, op0=ALU.add,
                op1=ALU.add, accum_out=sumv),
                waits=[ev_v2], writes=("o_acc", "sumv"), tag=ttg + ".f")
            tap("fs0", ev_f, jt0)
            tap("q2s0", ev_q2, jt0)
            ev_fsq = aop(lambda eng: eng.activation(out=sq_scr, in_=fout,
                                                    func=AF.Square,
                                                    accum_out=sumsq),
                         waits=[ev_f], reads=("o_acc",),
                         writes=("sq_scr", "sumsq"), tag=ttg + ".fsq")
            P.step("gpsimd", lambda eng: eng.engine_nop(),
                   waits=[ev_f, ev_fsq], tag=ttg + ".w")
            ln_stats_tail(3, ttg + ".ln3")
            ev_rstd3 = rsqrt_chain(stats[:, 1:2], rstd[:, 3:4],
                                   tag=ttg + ".ln3r")
            ev_t3 = vop(lambda eng: eng.tensor_scalar(
                out=scr_f, in0=fout, scalar1=negmu[:, 3:4],
                scalar2=rstd[:, 3:4], op0=ALU.add, op1=ALU.mult),
                waits=[ev_rstd3], reads=("o_acc", "negmu", "rstd"),
                writes=("scr_f",), tag=ttg + ".lap")
            ev_y = vop(lambda eng, j=j: eng.tensor_tensor(
                out=x2[j], in0=scr_f, in1=x2[j], op=ALU.add),
                waits=[ev_t3], reads=("scr_f",), writes=(f"x2_{j}",),
                tag=ttg + ".y")
            ev_od = dma(dout[jt0:jt0 + TT, :], x2[j], waits=[ev_y],
                        sem="dmao", tag=ttg + ".od")
            ev_x2_free[j] = ev_od
            ev_prev_tail = ev_t3
        st_ctx = {}
        P.barrier(tg + ".st")

    if "x2late" in taps:
        late = dict(x2late=x2[0], aol=bvec, scrfl=scr_f, negmul=negmu,
                    rstdl=rstd, sumvl=sumv, sumsql=sumsq, obl=q_bf,
                    oaccl=o_acc, lsuml=lsum, rll=rl, xinl=xin[0],
                    pml=Pm, pbl=Pbuf, negmaxl=negmax, qrl=qr, krzl=krz,
                    **({"canql": canq, "canqbl": canqb, "cancsl": cancs}
                       if canq is not None else {}))
        w = [(s, P.cnt[s]) for s in ("pe", "dve", "act", "pool")]
        for nm, buf in late.items():
            dl = nc.dram_tensor(f"tap_{nm}", list(buf.shape), buf.dtype,
                                kind="ExternalOutput").ap()
            def fxl(eng, dl=dl, buf=buf):
                return eng.dma_start(out=dl, in_=buf)
            P.step("sync", fxl, inc="dmao", inc_amt=16, tag=f"late.{nm}",
                   force=True, waits=w)
            w = []
    if P.frozen:
        def ffz(eng):
            return eng.dma_start(out=dout[0:TT, :], in_=xin[0])
        P.step("sync", ffz, inc="dmao", inc_amt=16, tag="final.frz",
               force=True)
    P.step("sync", lambda eng: eng.nop(),
           waits=[("dmao", P.cnt["dmao"])], tag="final", force=True)

    with (nc.semaphore("s_dmaw") as s_dmaw, nc.semaphore("s_dmax") as s_dmax,
          nc.semaphore("s_dmac") as s_dmac,
          nc.semaphore("s_dmao") as s_dmao, nc.semaphore("s_pe") as s_pe,
          nc.semaphore("s_dve") as s_dve, nc.semaphore("s_act") as s_act,
          nc.semaphore("s_pool") as s_pool,
          nc.semaphore("s_esync") as s_esync, nc.Block() as block):
        P.sems = dict(dmaw=s_dmaw, dmax=s_dmax, dmac=s_dmac, dmao=s_dmao,
                      pe=s_pe, dve=s_dve, act=s_act, pool=s_pool,
                      esync=s_esync)

        @block.sync
        def _(eng):
            P.run_engine(eng, "sync")

        @block.tensor
        def _(eng):
            P.run_engine(eng, "tensor")

        @block.vector
        def _(eng):
            P.run_engine(eng, "vector")

        @block.scalar
        def _(eng):
            P.run_engine(eng, "scalar")

        @block.gpsimd
        def _(eng):
            P.run_engine(eng, "gpsimd")

    return nc


def kernel(L0_attn_x, L0_attn_w_qkv, L0_attn_w_out, L0_attn_ln_g, L0_attn_ln_b,
           L0_attn_mem_beta, L0_ffn_U1, L0_ffn_V1, L0_ffn_b1, L0_ffn_U2,
           L0_ffn_V2, L0_ffn_b2):
    x = np.ascontiguousarray(np.asarray(L0_attn_x, dtype=np.float32))
    B = x.shape[0]
    nc = build(NT_FULL)
    common = dict(
        w_qkv=np.ascontiguousarray(L0_attn_w_qkv, dtype=np.float32),
        w_out=np.ascontiguousarray(L0_attn_w_out, dtype=np.float32),
        ln_g=np.asarray(L0_attn_ln_g, dtype=np.float32).reshape(1, D),
        ln_b=np.asarray(L0_attn_ln_b, dtype=np.float32).reshape(1, D),
        mem_beta=np.asarray(L0_attn_mem_beta,
                            dtype=np.float32).reshape(1, HEADS),
        U1=np.ascontiguousarray(L0_ffn_U1, dtype=np.float32),
        V1=np.ascontiguousarray(L0_ffn_V1, dtype=np.float32),
        b1=np.asarray(L0_ffn_b1, dtype=np.float32).reshape(1, DFF),
        U2=np.ascontiguousarray(L0_ffn_U2, dtype=np.float32),
        V2=np.ascontiguousarray(L0_ffn_V2, dtype=np.float32),
        b2=np.asarray(L0_ffn_b2, dtype=np.float32).reshape(1, D),
    )
    in_maps = [dict(common, x=np.ascontiguousarray(x[b])) for b in range(B)]
    res = run_bass_kernel_spmd(nc, in_maps, core_ids=list(range(B)))
    return np.stack([res.results[b]["out"] for b in range(B)], axis=0)


# version 04
# speedup vs baseline: 1.0155x; 1.0155x over previous
import numpy as np
import ml_dtypes
import concourse.bass as bass
import concourse.mybir as mybir
from concourse.bass_utils import run_bass_kernel_spmd

F32 = mybir.dt.float32
BF16 = mybir.dt.bfloat16
I32 = mybir.dt.int32
AF = mybir.ActivationFunctionType
ALU = mybir.AluOpType
AX = mybir.AxisListType
bfnp = ml_dtypes.bfloat16

HEADS, DH, SEG, TT = 16, 64, 32, 128
SCALE = DH ** -0.5
D, DQKV, DFF, RANK = 1024, 3072, 4096, 256
N_TOK = 8192
NT_FULL = N_TOK // TT
ST_TILES = 4
LN_EPS = 1e-5
MAGIC = 0x5F3759DF
ENGS = ("sync", "tensor", "vector", "scalar", "gpsimd")
ESEM = dict(sync="esync", tensor="pe", vector="dve", scalar="act",
            gpsimd="pool")


def _np_consts(n):
    ident = np.eye(TT, dtype=np.float32).astype(bfnp)
    segid = np.arange(TT) // SEG
    maskup = (segid[:, None] < segid[None, :]).astype(np.float32).astype(bfnp)
    causal = ((segid[:, None] == segid[None, :]) &
              (np.arange(TT)[:, None] >= np.arange(TT)[None, :]))
    causal = causal.astype(np.float32).astype(bfnp)
    j = np.arange(0, DH, 2, dtype=np.float32)
    freqs = (10000.0 ** (-j / DH)).astype(np.float32)
    ang = np.arange(n, dtype=np.float32)[:, None] * freqs[None, :]
    cs = np.concatenate([np.cos(ang), np.sin(ang)], axis=1).astype(np.float32)
    ones_row = np.ones((1, TT), dtype=np.float32).astype(bfnp)
    onesc_bf = np.ones((TT, 1), dtype=np.float32).astype(bfnp)
    zero_row = np.zeros((1, 512), dtype=np.float32).astype(bfnp)
    one_one = np.ones((1, 1), dtype=np.float32).astype(bfnp)
    magic_t = np.full((TT, 1), MAGIC, dtype=np.int32)
    zbd0 = np.zeros((TT, 16), dtype=np.float32)
    for c in range(8):
        zbd0[0:64, 2 * c] = 1e-6
        zbd0[64:128, 2 * c + 1] = 1e-6
    return dict(ident=ident, maskup=maskup, causal=causal, cs=cs,
                ones_row=ones_row, onesc_bf=onesc_bf, zero_row=zero_row,
                one_one=one_one, magic=magic_t, zbd0=zbd0)


class Plan:
    def __init__(self, until_tag=None):
        self.until = until_tag
        self.frozen = False
        self.steps = []
        self.cnt = {k: 0 for k in
                    ("dmaw", "dmax", "dmac", "dmao", "pe", "dve", "act",
                     "pool", "esync")}
        self.sems = {}
        self.pending_barrier = {}

    def step(self, engine, fn, waits=(), inc=None, inc_amt=1, reads=(),
             writes=(), tag="", force=False):
        if self.until is not None and self.until in tag:
            self.frozen = True
        if self.frozen and not force:
            if inc is not None:
                return (inc, self.cnt[inc])
            return None
        ev = None
        if inc is not None:
            self.cnt[inc] += inc_amt
            ev = (inc, self.cnt[inc])
        w = [x for x in waits if x is not None]
        if engine in self.pending_barrier:
            w = self.pending_barrier.pop(engine) + w
        self.steps.append(dict(engine=engine, fn=fn, waits=w, inc=inc,
                               inc_amt=inc_amt, reads=set(reads),
                               writes=set(writes), tag=tag))
        return ev

    def barrier(self, tag=""):
        prev = None
        for e in ENGS:
            sem = ESEM[e]
            def f(eng, e=e):
                return (eng.engine_nop() if e in ("vector", "gpsimd")
                        else eng.nop())
            prev = self.step(e, f, waits=[prev] if prev else (),
                             inc=sem, tag=tag + f".bar.{e}")
        for e in ENGS[:-1]:
            self.pending_barrier[e] = [prev]

    def run_engine(self, eng, engine_name):
        last_writes = set()
        for st in self.steps:
            if st["engine"] != engine_name:
                continue
            waited = False
            for semname, val in st["waits"]:
                eng.wait_ge(self.sems[semname], val)
                waited = True
            if waited:
                last_writes = set()
            if st["reads"] & last_writes or st["writes"] & last_writes:
                if engine_name in ("vector", "gpsimd"):
                    eng.engine_nop()
                elif engine_name == "scalar":
                    eng.drain()
            inst = st["fn"](eng)
            if st["inc"] is not None:
                assert inst is not None, st["tag"]
                inst.then_inc(self.sems[st["inc"]], st["inc_amt"])
            last_writes = st["writes"]


class PsumGen:

    def __init__(self):
        self.regions = set()

    def new_gen(self):
        self.regions = set()

    def start(self, col_lo, col_hi):
        key = (col_lo, col_hi)
        st = key not in self.regions
        self.regions.add(key)
        return st


def build(nt=NT_FULL, taps=(), until_tag=None):
    nc = bass.Bass("TRN2", target_bir_lowering=False, debug=False)
    n = nt * TT
    C = _np_consts(n)

    dx = nc.dram_tensor("x", [n, D], F32, kind="ExternalInput").ap()
    dwq = nc.dram_tensor("w_qkv", [D, DQKV], F32, kind="ExternalInput").ap()
    dwo = nc.dram_tensor("w_out", [D, D], F32, kind="ExternalInput").ap()
    dlng = nc.dram_tensor("ln_g", [1, D], F32, kind="ExternalInput").ap()
    dlnb = nc.dram_tensor("ln_b", [1, D], F32, kind="ExternalInput").ap()
    dbeta = nc.dram_tensor("mem_beta", [1, HEADS], F32, kind="ExternalInput").ap()
    dU1 = nc.dram_tensor("U1", [D, RANK], F32, kind="ExternalInput").ap()
    dV1 = nc.dram_tensor("V1", [RANK, DFF], F32, kind="ExternalInput").ap()
    db1 = nc.dram_tensor("b1", [1, DFF], F32, kind="ExternalInput").ap()
    dU2 = nc.dram_tensor("U2", [DFF, RANK], F32, kind="ExternalInput").ap()
    dV2 = nc.dram_tensor("V2", [RANK, D], F32, kind="ExternalInput").ap()
    db2 = nc.dram_tensor("b2", [1, D], F32, kind="ExternalInput").ap()
    dout = nc.dram_tensor("out", [n, D], F32, kind="ExternalOutput").ap()

    dident = nc.inline_tensor(C["ident"], "c_ident").ap()
    dmaskup = nc.inline_tensor(C["maskup"], "c_maskup").ap()
    dcausal = nc.inline_tensor(C["causal"], "c_causal").ap()
    dcs = nc.inline_tensor(C["cs"], "c_cs").ap()
    dones_row = nc.inline_tensor(C["ones_row"], "c_onesrow").ap()
    donesc = nc.inline_tensor(C["onesc_bf"], "c_onescol").ap()
    dzero_row = nc.inline_tensor(C["zero_row"], "c_zerorow").ap()
    done_one = nc.inline_tensor(C["one_one"], "c_oneone").ap()
    dmagic = nc.inline_tensor(C["magic"], "c_magic").ap()
    dzbd0 = nc.inline_tensor(C["zbd0"], "c_zbd0").ap()

    S = lambda name, shape, dt=BF16: nc.alloc_sbuf_tensor(name, shape, dt).ap()
    WQ = [S(f"WQ{c}", [TT, DQKV]) for c in range(8)]
    WO = [S(f"WO{c}", [TT, D]) for c in range(8)]
    U1w = [S(f"U1_{c}", [TT, RANK]) for c in range(8)]
    V1w = [S(f"V1_{r}", [TT, DFF]) for r in range(2)]
    U2w = [S(f"U2_{c}", [TT, RANK]) for c in range(32)]
    V2w = [S(f"V2_{r}", [TT, D]) for r in range(2)]
    bq_row = S("bq_row", [1, DQKV])
    b2bc = S("b2bc", [TT, D], F32)
    b1fm = S("b1fm", [TT, 32], F32)
    beta_bc = S("beta_bc", [TT, HEADS], F32)
    ombeta_bc = S("ombeta_bc", [TT, HEADS], F32)
    lngf = S("lngf", [TT, 8], F32)
    lnbf = S("lnbf", [TT, 8])
    ident = S("ident", [TT, TT])
    maskup = S("maskup", [TT, TT])
    causal = S("causal", [TT, TT])
    ones_row = S("ones_row", [1, TT])
    onesc = S("onesc", [TT, 1])
    zero_row = S("zero_row", [1, 512])
    one_one = S("one_one", [1, 1])
    magic = S("magic", [TT, 1], I32)
    zerocol = S("zerocol", [TT, 1], F32)

    zbd = S("zbd", [TT, HEADS], F32)
    zbd_bf = S("zbd_bf", [TT, HEADS])
    membf = S("membf", [TT, 1024])

    xin = [S(f"xin{i}", [TT, D], F32) for i in range(2)]
    csb = [S(f"cs{i}", [TT, 64], F32) for i in range(4)]
    xn2 = S("xn2", [TT, D])
    q_bf = S("q_bf", [TT, D])
    k_bf = S("k_bf", [TT, D])
    v_bf = S("v_bf", [TT, D])
    qr = S("qr", [TT, D])
    kr = S("kr", [TT, D])
    SQm = S("SQm", [TT, D])
    SKm = S("SKm", [TT, D])
    SKf = S("SKf", [TT, D])
    krz = S("krz", [TT, 2048])
    SQz = S("SQz", [TT, 2048])
    SKtz = S("SKtz", [TT, 2048])
    Pbuf = S("Pbuf", [TT, 2048])
    Pm = S("Pm", [TT, 2048])
    bvec = S("bvec", [TT, D], F32)
    g1 = S("g1", [TT, D])
    o_acc = S("o_acc", [TT, D], F32)
    x2 = [S(f"x2_{i}", [TT, D], F32) for i in range(ST_TILES)]
    P2 = S("P2", [TT, RANK])
    P2fw = S("P2fw", [TT, 2 * ST_TILES * TT])
    Hbuf = [S(f"Hbuf{i}", [TT, ST_TILES * TT]) for i in range(2)]
    Q2 = S("Q2", [TT, RANK])
    scr_bf = S("scr_bf", [TT, 1024])
    scr_f = S("scr_f", [TT, D], F32)
    sumv = S("sumv", [TT, 1], F32)
    sumsq = S("sumsq", [TT, 1], F32)
    stats = S("stats", [TT, 16], F32)
    negmu = S("negmu", [TT, 4], F32)
    rstd = S("rstd", [TT, 4], F32)
    den = S("den", [TT, HEADS], F32)
    denq = S("denq", [TT, HEADS], F32)
    rden = S("rden", [TT, HEADS], F32)
    rdq = S("rdq", [TT, HEADS], F32)
    negmax = S("negmax", [TT, HEADS], F32)
    lsum = S("lsum", [TT, HEADS], F32)
    rl = S("rl", [TT, HEADS], F32)

    canq = S("canq", [TT, D]) if "canq" in taps else None
    canqb = S("canqb", [TT, D]) if "canq" in taps else None
    cancs = S("cancs", [TT, 64], F32) if "canq" in taps else None
    xn2f, qrf = xn2, qr
    SKt = qr
    xn3 = SQm
    xn3f = xn3
    b_bf = v_bf
    g2 = k_bf
    o_bf, o_fm, ao, fout = q_bf, k_bf, bvec, o_acc
    PTb = Pbuf
    KTm = Pbuf
    SmU = S("SmU_dbg", [TT, 2048]) if "pm" in taps else Pm
    if "xn2" in taps:
        xn2f = S("xn2f_dbg", [TT, D])
    uvec = g1
    Q2f = S("Q2f_dbg", [TT, 2 * TT]) if "q2s0" in taps else Q2
    sq_scr = scr_bf[:, 0:1024]
    rowf = scr_f[0:1, 0:1024]
    rowbf = scr_bf[0:1, 0:1024]

    tap_srcs = dict(xn2=xn2, q=q_bf, k=k_bf, v=v_bf, qqr=qr, kkr=kr, sq=SQm,
                    canq=canq, canqb=canqb, cancs=cancs, bqr=qr, bpb=Pbuf,
                    negmax=negmax, fs0=o_acc, q2s0=Q2, p2f=P2fw, xn3t=xn3,
                    hs0=Hbuf[1],
                    sk=SKm, u=uvec, o=q_bf, aao=bvec, den=den, denq=denq,
                    mem=membf, lsum=lsum, x2_0=x2[0], b=bvec, ktm=KTm,
                    pm=Pm)
    dtap = {}
    for tname in taps:
        if tname == "x2late":
            continue
        src = tap_srcs[tname]
        dtap[tname] = nc.dram_tensor(f"tap_{tname}",
                                     [n, src.shape[-1]], src.dtype,
                                     kind="ExternalOutput").ap()

    PS_ST = nc.alloc_psum_tensor("ps_state", [TT, 1024], F32).ap()
    PS_A = nc.alloc_psum_tensor("ps_a", [TT, 1024], F32).ap()
    PS_B = nc.alloc_psum_tensor("ps_b", [TT, 1024], F32).ap()
    PS_C = nc.alloc_psum_tensor("ps_c", [TT, 1024], F32).ap()
    GA, GB, GC = PsumGen(), PsumGen(), PsumGen()
    GEN = {id(PS_A): GA, id(PS_B): GB, id(PS_C): GC}

    P = Plan(until_tag=until_tag)

    def pe(fns, waits=(), tag=""):
        def f(eng):
            last = None
            for fn in fns:
                last = fn(eng)
            return last
        return P.step("tensor", f, waits=waits, inc="pe", tag=tag)

    def vop(fn, waits=(), reads=(), writes=(), tag=""):
        return P.step("vector", fn, waits=waits, inc="dve", reads=reads,
                      writes=writes, tag=tag)

    def aop(fn, waits=(), reads=(), writes=(), tag=""):
        return P.step("scalar", fn, waits=waits, inc="act", reads=reads,
                      writes=writes, tag=tag)

    def gop(fn, waits=(), reads=(), writes=(), tag=""):
        return P.step("gpsimd", fn, waits=waits, inc="pool", reads=reads,
                      writes=writes, tag=tag)

    def dma(out, in_, waits=(), sem="dmaw", tag=""):
        def f(eng):
            return eng.dma_start(out=out, in_=in_)
        return P.step("sync", f, waits=waits, inc=sem, inc_amt=16, tag=tag)

    def hsl(t, h):
        return t[64 * (h % 2):64 * (h % 2) + 64,
                 TT * (h // 2):TT * (h // 2) + TT]

    def transpose_set(src, dst, ps, nchunks, waits, tag=""):
        g = GEN[id(ps)]
        g.new_gen()
        fns = []
        for c in range(nchunks):
            st = g.start(TT * c, TT * (c + 1))
            def f(eng, c=c, st=st):
                return eng.matmul(ps[:, TT * c:TT * (c + 1)],
                                  src[:, TT * c:TT * (c + 1)], ident,
                                  start=st, stop=True)
            fns.append(f)
        ev = pe(fns, waits=waits, tag=tag + ".t")
        def fe(eng):
            return eng.copy(out=dst[:, 0:TT * nchunks],
                            in_=ps[:, 0:TT * nchunks])
        return aop(fe, waits=[ev], writes=(dst.name,), tag=tag + ".ev")

    def transpose_zsplit(src, dst, ps, waits, tag=""):
        g = GEN[id(ps)]
        g.new_gen()
        fns = []
        for c in range(8):
            st = g.start(TT * c, TT * (c + 1))
            def f(eng, c=c, st=st):
                return eng.matmul(ps[:, TT * c:TT * (c + 1)],
                                  src[:, TT * c:TT * (c + 1)], ident,
                                  start=st, stop=True)
            fns.append(f)
        ev = pe(fns, waits=waits, tag=tag + ".t")
        top = dst.rearrange("p (c b k) -> p c b k", b=2, k=TT)[0:64, :, 0, :]
        bot = dst.rearrange("p (c b k) -> p c b k", b=2, k=TT)[64:128, :, 1, :]
        pst = ps.rearrange("p (c k) -> p c k", k=TT)
        e1 = aop(lambda eng: eng.copy(out=top, in_=pst[0:64]),
                 waits=[ev], writes=(dst.name,), tag=tag + ".e1")
        e2 = aop(lambda eng: eng.copy(out=bot, in_=pst[64:128]),
                 waits=[ev], writes=(dst.name,), tag=tag + ".e2")
        return e2

    def rsqrt_chain(src_col, dst_col, tag=""):
        s1, s2 = stats[:, 2:3], stats[:, 3:4]
        i1 = s1.bitcast(I32)
        fns = [
            lambda eng: eng.tensor_scalar(out=i1, in0=src_col.bitcast(I32),
                                          scalar1=0.5, scalar2=None,
                                          op0=ALU.mult),
            lambda eng: eng.tensor_tensor(out=i1, in0=magic, in1=i1,
                                          op=ALU.subtract),
            lambda eng: eng.tensor_tensor(out=s2, in0=s1, in1=s1, op=ALU.mult),
            lambda eng: eng.tensor_tensor(out=s2, in0=s2, in1=src_col,
                                          op=ALU.mult),
            lambda eng: eng.tensor_scalar(out=s2, in0=s2, scalar1=-0.5,
                                          scalar2=1.5, op0=ALU.mult,
                                          op1=ALU.add),
            lambda eng: eng.tensor_tensor(out=dst_col, in0=s1, in1=s2,
                                          op=ALU.mult),
        ]
        ev = None
        for i, fn in enumerate(fns):
            ev = P.step("gpsimd", fn, reads=("stats",),
                        writes=("stats", "rstd"),
                        inc=("pool" if i == len(fns) - 1 else None),
                        tag=tag + f".rs{i}")
        return ev

    def ln_stats_tail(site, tag):
        st = stats
        P.step("gpsimd", lambda eng: eng.tensor_scalar(
            out=negmu[:, site:site + 1], in0=sumv, scalar1=-1.0 / D,
            scalar2=None, op0=ALU.mult), reads=("sumv",), writes=("negmu",),
            tag=tag + ".mu")
        P.step("gpsimd", lambda eng: eng.tensor_tensor(
            out=st[:, 1:2], in0=negmu[:, site:site + 1],
            in1=negmu[:, site:site + 1], op=ALU.mult),
            reads=("negmu",), writes=("stats",), tag=tag + ".m2")
        P.step("gpsimd", lambda eng: eng.tensor_scalar(
            out=st[:, 0:1], in0=sumsq, scalar1=1.0 / D, scalar2=None,
            op0=ALU.mult), reads=("sumsq",), writes=("stats",), tag=tag + ".s2")
        P.step("gpsimd", lambda eng: eng.tensor_tensor(
            out=st[:, 0:1], in0=st[:, 0:1], in1=st[:, 1:2], op=ALU.subtract),
            reads=("stats",), writes=("stats",), tag=tag + ".var")
        P.step("gpsimd", lambda eng: eng.tensor_scalar(
            out=st[:, 1:2], in0=st[:, 0:1], scalar1=LN_EPS, scalar2=None,
            op0=ALU.add), reads=("stats",), writes=("stats",), tag=tag + ".ve")

    def ln_site(xten, xname, site, waits, fused, tag):
        ev_sq = aop(lambda eng: eng.activation(out=sq_scr, in_=xten,
                                               func=AF.Square,
                                               accum_out=sumsq),
                    waits=waits, reads=(xname,), writes=("sq_scr", "sumsq"),
                    tag=tag + ".sq")
        ev_sum = vop(lambda eng: eng.tensor_reduce(out=sumv, in_=xten,
                                                   axis=AX.X, op=ALU.add),
                     waits=waits, reads=(xname,), writes=("sumv",),
                     tag=tag + ".sum")
        P.step("gpsimd", lambda eng: eng.engine_nop(),
               waits=[ev_sq, ev_sum], tag=tag + ".w")
        ln_stats_tail(site, tag)
        st = stats
        if not fused:
            return rsqrt_chain(st[:, 1:2], rstd[:, site:site + 1], tag=tag)
        rsqrt_chain(st[:, 1:2], st[:, 4:5], tag=tag + ".r1")
        P.step("gpsimd", lambda eng: eng.tensor_tensor(
            out=st[:, 5:6], in0=st[:, 4:5], in1=st[:, 4:5], op=ALU.mult),
            reads=("stats",), writes=("stats",), tag=tag)
        P.step("gpsimd", lambda eng: eng.tensor_tensor(
            out=st[:, 5:6], in0=st[:, 5:6], in1=st[:, 0:1], op=ALU.mult),
            reads=("stats",), writes=("stats",), tag=tag)
        P.step("gpsimd", lambda eng: eng.tensor_scalar(
            out=st[:, 5:6], in0=st[:, 5:6], scalar1=LN_EPS, scalar2=None,
            op0=ALU.add), reads=("stats",), writes=("stats",), tag=tag)
        rsqrt_chain(st[:, 5:6], st[:, 6:7], tag=tag + ".r2")
        return P.step("gpsimd", lambda eng: eng.tensor_tensor(
            out=rstd[:, site:site + 1], in0=st[:, 4:5], in1=st[:, 6:7],
            op=ALU.mult), inc="pool", reads=("stats",), writes=("rstd",),
            tag=tag + ".r12")

    def bcast16(src16, cnt=16):
        return src16[:, 0:cnt].unsqueeze(2).broadcast_to([TT, cnt, DH])

    def tap(name, ev, t0):
        if name in dtap:
            src = tap_srcs[name]
            dma(dtap[name][t0:t0 + TT, :], src, waits=[ev], sem="dmao",
                tag=f"tap.{name}")

    ev_xd_pre, ev_cd_pre = {}, {}
    prev_xd, prev_cd = (), ()
    for it in range(min(2, nt)):
        ev_xd_pre[it] = dma(xin[it % 2], dx[it * TT:(it + 1) * TT, :],
                            sem="dmax", waits=prev_xd,
                            tag=f"prep.xd{it}")
        prev_xd = [ev_xd_pre[it]]
    for it in range(min(4, nt)):
        ev_cd_pre[it] = dma(csb[it % 4], dcs[it * TT:(it + 1) * TT, :],
                            sem="dmac", waits=prev_cd,
                            tag=f"prep.cd{it}")
        prev_cd = [ev_cd_pre[it]]
    for out, in_ in [(ident, dident), (maskup, dmaskup), (causal, dcausal),
                     (ones_row, dones_row), (onesc, donesc),
                     (zero_row, dzero_row), (one_one, done_one),
                     (magic, dmagic), (zbd, dzbd0)]:
        ev_consts = dma(out, in_, tag="prep.c")
    vop(lambda eng: eng.memset(zerocol, 0.0), writes=("zerocol",),
        tag="prep.zc")
    for tz, nm in ((krz, "krz"), (SQz, "SQz"), (SKtz, "SKtz")):
        gop(lambda eng, tz=tz: eng.memset(tz, 0.0), writes=(nm,),
            tag="prep.z" + nm)
    _zero_all = [xn2, q_bf, k_bf, v_bf, qr, kr, SQm, SKm, SKf, Pbuf, Pm,
                 bvec, g1, o_acc, sumv, sumsq, stats, negmu,
                 rstd, den, denq, rden, rdq, negmax, lsum, rl,
                 P2, P2fw, Hbuf[0], Hbuf[1], Q2] + x2
    for zi, zt in enumerate(_zero_all):
        vop(lambda eng, zt=zt: eng.memset(zt, 0.0), writes=("zz",),
            tag=f"prep.zz{zi}")
    ev_zbdbf = vop(lambda eng: eng.tensor_copy(out=zbd_bf, in_=zbd),
                   waits=[ev_consts], writes=("zbd_bf",), tag="prep.zbdbf")

    GST = PsumGen()
    _stz = [
        (lambda eng: eng.matmul(PS_ST[:, 0:512], ones_row, zero_row,
                                start=True, stop=True)),
        (lambda eng: eng.matmul(PS_ST[:, 512:1024], ones_row, zero_row,
                                start=True, stop=True))]
    for h in range(HEADS):
        ro = 64 * (h % 2)
        c0 = TT * (h // 2) + DH * (h % 2)
        def fz(eng, ro=ro, c0=c0):
            return eng.matmul(PS_ST[ro:ro + 64, c0:c0 + DH],
                              ones_row[0:1, 0:64], zero_row[0:1, 0:DH],
                              start=True, stop=True)
        _stz.append(fz)
    ev_stinit = pe(_stz, waits=[ev_consts], tag="prep.stzero")
    ev_membf = aop(lambda eng: eng.copy(out=membf, in_=PS_ST),
                   waits=[ev_stinit], writes=("membf",), tag="prep.stev")

    def row_to_fm(drow, total, dst_f32, dst_bf, chain, tag):
        col = 0
        last = chain
        GB.new_gen()
        while col < total:
            w = min(1024, total - col)
            evd = dma(rowf[0:1, 0:w], drow[0:1, col:col + w],
                      waits=[last], tag=tag + f".d{col}")
            evc = vop(lambda eng, w=w: eng.tensor_copy(
                out=rowbf[0:1, 0:w].bitcast(BF16), in_=rowf[0:1, 0:w]),
                waits=[evd], reads=("rowf",), writes=("rowbf",),
                tag=tag + f".c{col}")
            fns = []
            for c in range(w // TT):
                cc = col // TT + c
                st = GB.start(cc, cc + 1)
                def f(eng, c=c, cc=cc, st=st):
                    return eng.matmul(PS_B[:, cc:cc + 1],
                                      rowbf[0:1, TT * c:TT * (c + 1)].bitcast(BF16),
                                      one_one, start=st, stop=True)
                fns.append(f)
            last = pe(fns, waits=[evc, ev_consts], tag=tag + f".m{col}")
            col += w
        ncol = total // TT
        evs = last
        if dst_f32 is not None:
            evs = vop(lambda eng: eng.tensor_copy(out=dst_f32,
                                                  in_=PS_B[:, 0:ncol]),
                      waits=[last], writes=(dst_f32.name,), tag=tag + ".e")
        if dst_bf is not None:
            evs = aop(lambda eng: eng.copy(out=dst_bf, in_=PS_B[:, 0:ncol]),
                      waits=[last], writes=(dst_bf.name,), tag=tag + ".eb")
        return evs

    ev = row_to_fm(dlng, D, lngf, None, ev_consts, "prep.lng")
    ev = row_to_fm(dlnb, D, None, lnbf, ev, "prep.lnb")
    ev_b1 = row_to_fm(db1, DFF, b1fm, None, ev, "prep.b1")

    ev_bd = dma(stats[0:1, 0:16], dbeta, waits=[ev_b1], tag="prep.beta.d")
    ev_sig = aop(lambda eng: eng.activation(out=rowbf[0:1, 0:16].bitcast(BF16),
                                            in_=stats[0:1, 0:16],
                                            func=AF.Sigmoid),
                 waits=[ev_bd], writes=("rowbf",), tag="prep.beta.sig")
    GC.new_gen()
    ev_bmm = pe([lambda eng: eng.matmul(PS_C[:, 0:16], ones_row,
                                        rowbf[0:1, 0:16].bitcast(BF16),
                                        start=True, stop=True)],
                waits=[ev_sig], tag="prep.beta.mm")
    ev_beta = vop(lambda eng: eng.tensor_copy(out=beta_bc, in_=PS_C[:, 0:16]),
                  waits=[ev_bmm], writes=("beta_bc",), tag="prep.beta.e")
    ev_ombeta = vop(lambda eng: eng.tensor_scalar(
        out=ombeta_bc, in0=beta_bc, scalar1=-1.0, scalar2=1.0, op0=ALU.mult,
        op1=ALU.add), waits=[ev_beta], reads=("beta_bc",),
        writes=("ombeta_bc",), tag="prep.beta.om")

    ev_b2d = dma(rowf[0:1, 0:D], db2, waits=[ev_ombeta], tag="prep.b2.d")
    ev_b2c = vop(lambda eng: eng.tensor_copy(out=rowbf[0:1, 0:D].bitcast(BF16),
                                             in_=rowf[0:1, 0:D]),
                 waits=[ev_b2d], reads=("rowf",), writes=("rowbf",),
                 tag="prep.b2.c")
    GC.new_gen()
    ev_b2m = pe([lambda eng: eng.matmul(PS_C[:, 0:512], ones_row,
                                        rowbf[0:1, 0:512].bitcast(BF16),
                                        start=True, stop=True),
                 lambda eng: eng.matmul(PS_C[:, 512:1024], ones_row,
                                        rowbf[0:1, 512:1024].bitcast(BF16),
                                        start=True, stop=True)],
                waits=[ev_b2c, ev_beta], tag="prep.b2.mm")
    ev_b2 = vop(lambda eng: eng.tensor_copy(out=b2bc, in_=PS_C),
                waits=[ev_b2m], writes=("b2bc",), tag="prep.b2.e")

    GA.new_gen()
    bq_ps = {0: PS_A, 1: PS_C, 2: PS_B}
    ev_last = ev_b2
    ev_bq_mm = {0: None, 1: None, 2: None}
    for cch in range(8):
        for blk in range(3):
            evd = dma(scr_f, dwq[TT * cch:TT * (cch + 1),
                                 1024 * blk:1024 * (blk + 1)],
                      waits=[ev_last], tag=f"prep.wq{cch}.{blk}.d")
            evc = vop(lambda eng: eng.tensor_copy(out=scr_bf[:, 0:1024],
                                                  in_=scr_f),
                      waits=[evd], reads=("scr_f",), writes=("scr_bf",),
                      tag=f"prep.wq{cch}.{blk}.c")
            ps = bq_ps[blk]
            fns = []
            for g in range(2):
                def f(eng, g=g, ps=ps, cch=cch):
                    return eng.matmul(ps[0:1, 512 * g:512 * (g + 1)],
                                      lnbf[:, cch:cch + 1],
                                      scr_bf[:, 512 * g:512 * (g + 1)],
                                      start=(cch == 0), stop=(cch == 7))
                fns.append(f)
            ev_bq_mm[blk] = pe(fns, waits=[evc], tag=f"prep.bq{cch}.{blk}")
            dst = WQ[cch][:, 1024 * blk:1024 * (blk + 1)]
            if blk < 2:
                dst = dst.rearrange("p (h a j) -> p h a j", h=16, a=2,
                                    j=32).transpose([0, 1, 3, 2])
            def fw(eng, dst=dst, cch=cch):
                return eng.tensor_scalar(out=dst, in0=scr_f,
                                         scalar1=lngf[:, cch:cch + 1],
                                         scalar2=None, op0=ALU.mult)
            ev_last = vop(fw, waits=[ev_bq_mm[blk]], reads=("scr_f",),
                          writes=(f"WQ{cch}",), tag=f"prep.wqw{cch}.{blk}")
    ev_bq = ev_last
    for blk, ps in ((0, PS_A), (1, PS_C), (2, PS_B)):
        src = ps[0:1, 0:1024]
        dst = bq_row[0:1, 1024 * blk:1024 * (blk + 1)]
        if blk < 2:
            dst = dst.rearrange("p (h a j) -> p h a j", h=16, a=2,
                                j=32).transpose([0, 1, 3, 2])
        ev_bq = aop(lambda eng, dst=dst, src=src: eng.copy(out=dst, in_=src),
                    waits=[ev_last], writes=("bq_row",), tag=f"prep.bqe{blk}")

    def load_cast(pieces, piece_cols, chain, tag):
        last = chain
        for i, (dst, src) in enumerate(pieces):
            evd = dma(scr_f[:, 0:piece_cols], src, waits=[last],
                      tag=f"{tag}.d{i}")
            last = vop(lambda eng, dst=dst: eng.tensor_copy(
                out=dst, in_=scr_f[:, 0:piece_cols]),
                waits=[evd], reads=("scr_f",), writes=("wgt",),
                tag=f"{tag}.c{i}")
        return last

    if "xn2" in taps:
        dbq = nc.dram_tensor("tap_bq", [1, DQKV], BF16, kind="ExternalOutput").ap()
        dwq0 = nc.dram_tensor("tap_wq0", [TT, DQKV], BF16, kind="ExternalOutput").ap()
        dlngf = nc.dram_tensor("tap_lngf", [TT, 8], F32, kind="ExternalOutput").ap()
        dma(dbq, bq_row, waits=[ev_bq], sem="dmao", tag="prep.tapbq")
        dlnbf = nc.dram_tensor("tap_lnbf", [TT, 8], BF16, kind="ExternalOutput").ap()
        dma(dlnbf, lnbf, waits=[ev_bq], sem="dmao", tag="prep.taplnbf")
        dma(dwq0, WQ[0], waits=[ev_bq], sem="dmao", tag="prep.tapwq")
        dma(dlngf, lngf, waits=[ev_bq], sem="dmao", tag="prep.taplngf")
    ev_w = load_cast([(WO[c], dwo[TT * c:TT * (c + 1), :]) for c in range(8)],
                     D, ev_bq, "prep.wo")
    ev_w = load_cast([(U1w[c], dU1[TT * c:TT * (c + 1), :]) for c in range(8)],
                     RANK, ev_w, "prep.u1")
    pieces = []
    for r in range(2):
        for gp in range(4):
            pieces.append((V1w[r][:, 1024 * gp:1024 * (gp + 1)],
                           dV1[TT * r:TT * (r + 1),
                               1024 * gp:1024 * (gp + 1)]))
    ev_w = load_cast(pieces, 1024, ev_w, "prep.v1")
    ev_w = load_cast([(U2w[c], dU2[TT * c:TT * (c + 1), :])
                      for c in range(32)], RANK, ev_w, "prep.u2")
    ev_wlast = load_cast([(V2w[r], dV2[TT * r:TT * (r + 1), :])
                          for r in range(2)], D, ev_w, "prep.v2")

    P.barrier("prep")

    ev_x2_free = [None] * ST_TILES
    ev_state_delta = ev_stinit
    ev_zbd_prev = ev_zbdbf
    ev_x2_of, ev_kr_of = {}, {}
    st_ctx = {}

    for it in range(nt):
        ib = it % 2
        ic = it % 4
        t0 = it * TT
        tg = f"t{it}"
        if it < 2:
            ev_xd = ev_xd_pre[it]
        else:
            ev_xd = dma(xin[ib], dx[t0:t0 + TT, :], sem="dmax",
                        waits=list(prev_xd) + [ev_x2_of[it - 2]],
                        tag=tg + ".xd")
            prev_xd = [ev_xd]
        if it < 4:
            ev_cd = ev_cd_pre[it]
        else:
            ev_cd = dma(csb[ic], dcs[t0:t0 + TT, :], sem="dmac",
                        waits=list(prev_cd) + [ev_kr_of[it - 4]],
                        tag=tg + ".cd")
            prev_cd = [ev_cd]

        ev_rstd0 = ln_site(xin[ib], f"xin{ib}", 0, [ev_xd], True, tg + ".ln0")
        ev_xn2 = vop(lambda eng, ib=ib: eng.tensor_scalar(
            out=xn2, in0=xin[ib], scalar1=negmu[:, 0:1], scalar2=rstd[:, 0:1],
            op0=ALU.add, op1=ALU.mult),
            waits=[ev_rstd0], reads=(f"xin{ib}", "negmu", "rstd"),
            writes=("xn2",), tag=tg + ".xn2")

        ev_xn2f = transpose_set(xn2, xn2f, PS_A, 8, waits=[ev_xn2],
                                tag=tg + ".xn2T")

        qkv_ps = {0: PS_A, 1: PS_B, 2: PS_C}
        ev_qkv = {}
        for blk in range(3):
            ps = qkv_ps[blk]
            GEN[id(ps)].new_gen()
            fns = []
            for gidx in range(2):
                c0 = 512 * gidx
                def fb(eng, blk=blk, c0=c0, ps=ps):
                    return eng.matmul(
                        ps[:, c0:c0 + 512], ones_row,
                        bq_row[0:1, 1024 * blk + c0:1024 * blk + c0 + 512],
                        start=True, stop=False)
                fns.append(fb)
                for cch in range(8):
                    def fm_(eng, blk=blk, cch=cch, c0=c0, ps=ps):
                        return eng.matmul(
                            ps[:, c0:c0 + 512],
                            xn2f[:, TT * cch:TT * (cch + 1)],
                            WQ[cch][:, 1024 * blk + c0:1024 * blk + c0 + 512],
                            start=False, stop=(cch == 7))
                    fns.append(fm_)
            ev_qkv[blk] = pe(fns, waits=[ev_xn2f], tag=tg + f".qkv{blk}")
        ev_qe = aop(lambda eng: eng.copy(out=q_bf, in_=PS_A),
                    waits=[ev_qkv[0]], writes=("q_bf",), tag=tg + ".qe")
        ev_ke = aop(lambda eng: eng.copy(out=k_bf, in_=PS_B),
                    waits=[ev_qkv[1]], writes=("k_bf",), tag=tg + ".ke")
        ev_ve = aop(lambda eng: eng.copy(out=v_bf, in_=PS_C),
                    waits=[ev_qkv[2]], writes=("v_bf",), tag=tg + ".ve")

        cc_b = csb[ic][:, 0:32].unsqueeze(1).broadcast_to([TT, 16, 32])
        ss_b = csb[ic][:, 32:64].unsqueeze(1).broadcast_to([TT, 16, 32])
        def rope(src, dst, ev_src, ev_scr_free, tagr):
            v3 = src.rearrange("p (h a j) -> p (h a) j", a=2, j=32)
            d3 = dst.rearrange("p (h a j) -> p (h a) j", a=2, j=32)
            se, so = v3[:, 0::2, :], v3[:, 1::2, :]
            de, do = d3[:, 0::2, :], d3[:, 1::2, :]
            tA = scr_bf[:, 0:512].rearrange("p (h j) -> p h j", j=32)
            tB = scr_bf[:, 512:1024].rearrange("p (h j) -> p h j", j=32)
            dnop = lambda t: vop(lambda eng: eng.engine_nop(), tag=t)
            e1 = vop(lambda eng: eng.tensor_tensor(out=tA, in0=se, in1=cc_b,
                                                   op=ALU.mult),
                     waits=[ev_src, ev_cd, ev_scr_free], reads=(src.name,),
                     writes=("scr_bf",), tag=tagr + ".1")
            e2 = vop(lambda eng: eng.tensor_tensor(out=tB, in0=so, in1=ss_b,
                                                   op=ALU.mult),
                     waits=[ev_src, ev_cd, ev_scr_free], reads=(src.name,),
                     writes=("scr_bf",), tag=tagr + ".2")
            dnop(tagr + ".n2")
            e5 = vop(lambda eng: eng.tensor_tensor(out=de, in0=tA, in1=tB,
                                                   op=ALU.subtract),
                     waits=[e2], reads=("scr_bf",), writes=(dst.name,),
                     tag=tagr + ".5")
            dnop(tagr + ".n5")
            e3 = vop(lambda eng: eng.tensor_tensor(out=tA, in0=so, in1=cc_b,
                                                   op=ALU.mult),
                     reads=(src.name,), writes=("scr_bf",), tag=tagr + ".3")
            e4 = vop(lambda eng: eng.tensor_tensor(out=tB, in0=se, in1=ss_b,
                                                   op=ALU.mult),
                     waits=[e5], reads=(src.name,), writes=("scr_bf",),
                     tag=tagr + ".4")
            dnop(tagr + ".n4")
            e6 = vop(lambda eng: eng.tensor_tensor(out=do, in0=tA, in1=tB,
                                                   op=ALU.add),
                     waits=[e4], reads=("scr_bf",), writes=(dst.name,),
                     tag=tagr + ".6")
            dnop(tagr + ".n6")
            return e6
        ev_qr = rope(q_bf, qr, ev_qe, None, tg + ".ropeq")
        ev_tapqr = None
        if "bqr" in dtap and it == 0:
            def ftq(eng):
                return eng.dma_start(out=dtap["bqr"][0:TT, :], in_=qr)
            ev_tapqr = P.step("sync", ftq, waits=[ev_qr], inc="dmao",
                              inc_amt=16, tag=tg + ".bqr")
        ev_kr = rope(k_bf, kr, ev_ke, ev_qr, tg + ".ropek")
        ev_kr_of[it] = ev_kr
        if "canq" in dtap and it == 0:
            ecq = vop(lambda eng: eng.tensor_copy(out=canq, in_=qr),
                      waits=[ev_qr], reads=("qr",), writes=("canq",),
                      tag=tg + ".canq")
            ecs2 = aop(lambda eng: eng.copy(out=cancs, in_=csb[0]),
                       waits=[ev_cd], writes=("cancs",), tag=tg + ".cancs")
            ecqb = gop(lambda eng: eng.tensor_copy(out=canqb, in_=q_bf),
                       waits=[ev_qe], reads=("q_bf",), writes=("canqb",),
                       tag=tg + ".canqb")
            tap("canq", ecq, 0)
            tap("cancs", ecs2, 0)
            tap("canqb", ecqb, 0)

        escr = scr_bf[:, 0:1024]
        def elu(src, dst, ev_src, waits_m, tagr):
            e1 = aop(lambda eng: eng.activation(out=escr, in_=src,
                                                func=AF.Exp),
                     waits=[ev_src] + waits_m, reads=(src.name,),
                     writes=("scr_bf",), tag=tagr + ".e")
            e2 = vop(lambda eng: eng.tensor_scalar(out=escr, in0=escr,
                                                   scalar1=1.0, scalar2=None,
                                                   op0=ALU.min),
                     waits=[e1], reads=("scr_bf",), writes=("scr_bf",),
                     tag=tagr + ".m")
            e3 = vop(lambda eng: eng.scalar_tensor_tensor(
                out=dst, in0=src, scalar=0.0, in1=escr, op0=ALU.max,
                op1=ALU.add), waits=[e2], reads=(src.name, "scr_bf"),
                writes=(dst.name,), tag=tagr + ".s")
            return e3
        ev_SQ = elu(q_bf, SQm, ev_qr, [ev_kr], tg + ".eluq")
        ev_SK = elu(k_bf, SKm, ev_kr, [ev_SQ], tg + ".eluk")

        wq_settle, wk_settle = [], []
        if it < 2:
            wq_settle = [ev_kr]
            ev_kset = vop(lambda eng: eng.tensor_copy(out=kr[:, 0:64],
                                                      in_=kr[:, 0:64]),
                          waits=[ev_kr], reads=("kr",), writes=("kr",),
                          tag=tg + ".kset")
            wk_settle = [ev_kset]
        ev_qrf = transpose_set(qr, qrf, PS_B, 8,
                               waits=[ev_qr, ev_ke, ev_tapqr] + wq_settle,
                               tag=tg + ".qrT")
        ev_krz = transpose_zsplit(kr, krz, PS_C,
                                  waits=[ev_kr, ev_ve] + wk_settle,
                                  tag=tg + ".krT")
        ev_SQz = transpose_zsplit(SQm, SQz, PS_A, waits=[ev_SQ, ev_qe],
                                  tag=tg + ".sqT")
        ev_SKf = transpose_set(SKm, SKf, PS_B, 8, waits=[ev_SK, ev_qrf],
                               tag=tg + ".skT")

        ev_nms_r = [None, None]
        ev_nm_r = [None, None]
        for r, ps in ((0, PS_A), (1, PS_C)):
            g = GEN[id(ps)]
            g.new_gen()
            fns = []
            for hh in range(8):
                h = 8 * r + hh
                st = g.start(TT * hh, TT * (hh + 1))
                def f(eng, h=h, hh=hh, ps=ps, st=st):
                    return eng.matmul(ps[:, TT * hh:TT * (hh + 1)],
                                      qrf[:, TT * (h // 2):TT * (h // 2) + TT],
                                      krz[:, TT * h:TT * (h + 1)],
                                      start=st, stop=True)
                fns.append(f)
            w = [ev_qrf, ev_krz] + ([ev_SQz] if r == 0 else [])
            ev_mm = pe(fns, waits=w, tag=tg + f".sc{r}")
            ev_nm_r[r] = vop(lambda eng, ps=ps, r=r: eng.tensor_reduce(
                out=negmax[:, 8 * r:8 * r + 8],
                in_=ps.rearrange("p (h k) -> p h k", k=TT),
                axis=AX.X, op=ALU.max, negate=True),
                waits=[ev_mm], writes=("negmax",), tag=tg + f".nm{r}")
            ev_nms_r[r] = vop(lambda eng, r=r: eng.tensor_scalar(
                out=negmax[:, 8 * r:8 * r + 8],
                in0=negmax[:, 8 * r:8 * r + 8], scalar1=SCALE, scalar2=None,
                op0=ALU.mult), waits=[ev_nm_r[r]], reads=("negmax",),
                writes=("negmax",), tag=tg + f".nms{r}")
        ev_P = [None, None]
        ev_pmsk_r = [None, None]
        cb3 = causal.unsqueeze(1).broadcast_to([TT, 8, TT])
        for r, ps in ((0, PS_A), (1, PS_C)):
            evp = None
            for hh in range(8):
                h = 8 * r + hh
                def fe(eng, h=h, hh=hh, ps=ps):
                    return eng.activation(out=Pbuf[:, TT * h:TT * (h + 1)],
                                          in_=ps[:, TT * hh:TT * (hh + 1)],
                                          func=AF.Exp, bias=negmax[:, h:h + 1],
                                          scale=SCALE)
                if hh == 0:
                    ws = ([ev_nms_r[0], ev_nm_r[1]] if r == 0
                          else [ev_nms_r[1], ev_pmsk_r[0]])
                else:
                    ws = ()
                evp = aop(fe, waits=ws, reads=("negmax",), writes=("Pbuf",),
                          tag=tg + f".exp{h}")
            ev_P[r] = evp
            ev_pmsk_r[r] = vop(lambda eng, r=r: eng.tensor_tensor(
                out=Pm[:, 1024 * r:1024 * (r + 1)].rearrange(
                    "p (h k) -> p h k", k=TT),
                in0=Pbuf[:, 1024 * r:1024 * (r + 1)].rearrange(
                    "p (h k) -> p h k", k=TT), in1=cb3,
                op=ALU.mult), waits=[ev_P[r]], reads=("Pbuf",),
                writes=("Pm",), tag=tg + f".pmsk{r}")
        ev_tappb = None
        if "bpb" in dtap and it == 0:
            def ftp(eng):
                return eng.dma_start(out=dtap["bpb"][0:TT, :], in_=Pbuf)
            ev_tappb = P.step("sync", ftp, waits=[ev_P[0], ev_P[1]],
                              inc="dmao", inc_amt=16, tag=tg + ".bpb")
        ev_ttr = vop(lambda eng: eng.tensor_reduce(
            out=lsum, in_=Pm.rearrange("p (h k) -> p h k", k=TT), axis=AX.X,
            op=ALU.add), waits=[ev_pmsk_r[1]], reads=("Pm",),
            writes=("lsum",), tag=tg + ".lsum")
        ev_rl = vop(lambda eng: eng.reciprocal(out=rl, in_=lsum),
                    waits=[ev_ttr], reads=("lsum",), writes=("rl",),
                    tag=tg + ".rl")
        ev_rl2 = vop(lambda eng: eng.tensor_tensor(out=rl, in0=rl,
                                                   in1=ombeta_bc,
                                                   op=ALU.mult),
                     waits=[ev_rl], reads=("rl",), writes=("rl",),
                     tag=tg + ".rl2")

        ev_PT = [None, None]
        for r, ps in ((0, PS_A), (1, PS_C)):
            g = GEN[id(ps)]
            g.new_gen()
            fns = []
            for hh in range(8):
                h = 8 * r + hh
                st = g.start(TT * hh, TT * (hh + 1))
                def f(eng, h=h, hh=hh, ps=ps, st=st):
                    return eng.matmul(ps[:, TT * hh:TT * (hh + 1)],
                                      Pm[:, TT * h:TT * (h + 1)], ident,
                                      start=st, stop=True)
                fns.append(f)
            ev_mm = pe(fns, waits=[ev_ttr], tag=tg + f".pT{r}")
            def fe(eng, r=r, ps=ps):
                return eng.copy(out=PTb[:, 1024 * r:1024 * (r + 1)],
                                in_=ps[:, 0:1024])
            ev_PT[r] = aop(fe, waits=[ev_mm, ev_tappb], writes=("Pbuf",),
                           tag=tg + f".pTe{r}")

        GB.new_gen()
        fns = []
        for h in range(HEADS):
            st = GB.start(DH * h, DH * (h + 1))
            def f(eng, h=h, st=st):
                return eng.matmul(PS_B[:, DH * h:DH * (h + 1)],
                                  PTb[:, TT * h:TT * (h + 1)],
                                  v_bf[:, DH * h:DH * (h + 1)],
                                  start=st, stop=True)
            fns.append(f)
        ev_sdpa = pe(fns, waits=[ev_PT[0], ev_PT[1], ev_ve, ev_SKf],
                     tag=tg + ".sdpa")
        ev_omix1 = vop(lambda eng: eng.tensor_tensor(
            out=scr_f.rearrange("p (h d) -> p h d", d=DH),
            in0=PS_B.rearrange("p (h d) -> p h d", d=DH), in1=bcast16(rl),
            op=ALU.mult), waits=[ev_sdpa, ev_rl2], reads=("rl",),
            writes=("scr_f",), tag=tg + ".omix1")

        GA.new_gen()
        ev_zp = pe([
            lambda eng: eng.matmul(PS_A[:, 0:512], maskup, SKm[:, 0:512],
                                   start=True, stop=True),
            lambda eng: eng.matmul(PS_A[:, 512:1024], maskup, SKm[:, 512:1024],
                                   start=True, stop=True)],
            waits=[ev_SK, ev_PT[0]], tag=tg + ".zp")
        GC.new_gen()
        fns = []
        for c in range(8):
            def f(eng, c=c):
                return eng.matmul(PS_C[:, 992 + 2 * c:994 + 2 * c],
                                  SKf[:, TT * c:TT * (c + 1)],
                                  zbd_bf[:, 2 * c:2 * c + 2],
                                  start=True, stop=True)
            fns.append(f)
        for h in range(HEADS):
            def f(eng, h=h):
                return eng.matmul(PS_C[:, 976 + h:977 + h],
                                  SQz[:, TT * h:TT * (h + 1)],
                                  zbd_bf[:, h:h + 1],
                                  start=True, stop=True)
            fns.append(f)
        ev_z0 = pe(fns, waits=[ev_SKf, ev_SQz, ev_zbd_prev, ev_PT[1]],
                   tag=tg + ".z0")
        ev_dp = vop(lambda eng: eng.tensor_tensor(
            out=scr_bf[:, 0:1024], in0=SKm, in1=PS_A, op=ALU.mult),
            waits=[ev_zp, ev_kr, ev_qr], reads=("SKm",), writes=("scr_bf",),
            tag=tg + ".dp")
        ev_dr = vop(lambda eng: eng.tensor_reduce(
            out=den, in_=scr_bf[:, 0:1024].rearrange("p (h d) -> p h d",
                                                     d=DH),
            axis=AX.X, op=ALU.add), waits=[ev_dp], reads=("scr_bf",),
            writes=("den",), tag=tg + ".dr")
        ev_den = vop(lambda eng: eng.tensor_tensor(
            out=den, in0=den, in1=PS_C[:, 992:1008], op=ALU.add),
            waits=[ev_dr, ev_z0], reads=("den",), writes=("den",),
            tag=tg + ".den")
        ev_qp = vop(lambda eng: eng.tensor_tensor(
            out=scr_bf[:, 0:1024], in0=SQm, in1=PS_A, op=ALU.mult),
            waits=[ev_zp, ev_dr], reads=("SQm",), writes=("scr_bf",),
            tag=tg + ".qp")
        ev_qred = vop(lambda eng: eng.tensor_reduce(
            out=denq, in_=scr_bf[:, 0:1024].rearrange(
                "p (h d) -> p h d", d=DH), axis=AX.X, op=ALU.add),
            waits=[ev_qp], reads=("scr_bf",), writes=("denq",),
            tag=tg + ".qred")
        ev_denq = vop(lambda eng: eng.scalar_tensor_tensor(
            out=denq, in0=PS_C[:, 976:992], scalar=1e-6, in1=denq,
            op0=ALU.add, op1=ALU.add), waits=[ev_qred, ev_z0],
            reads=("denq",), writes=("denq",), tag=tg + ".denq")
        ev_rden = vop(lambda eng: eng.reciprocal(out=rden, in_=den),
                      waits=[ev_den], reads=("den",), writes=("rden",),
                      tag=tg + ".rden")
        ev_rdq0 = vop(lambda eng: eng.reciprocal(out=rdq, in_=denq),
                      waits=[ev_denq], reads=("denq",), writes=("rdq",),
                      tag=tg + ".rdq0")
        ev_rdq = vop(lambda eng: eng.tensor_tensor(out=rdq, in0=rdq,
                                                   in1=beta_bc, op=ALU.mult),
                     waits=[ev_rdq0], reads=("rdq",), writes=("rdq",),
                     tag=tg + ".rdq")
        ev_skt = vop(lambda eng: eng.tensor_tensor(
            out=SKt.rearrange("p (h d) -> p h d", d=DH),
            in0=SKm.rearrange("p (h d) -> p h d", d=DH),
            in1=bcast16(rden), op=ALU.mult),
            waits=[ev_rden], reads=("SKm", "rden"), writes=("SKt",),
            tag=tg + ".skt")
        ev_sktz = transpose_zsplit(SKt, SKtz, PS_A,
                                   waits=[ev_skt, ev_dp, ev_qp],
                                   tag=tg + ".sktT")

        mupb = maskup.unsqueeze(1).broadcast_to([TT, 8, TT])
        ev_KT = [None, None]
        for r, ps in ((0, PS_B), (1, PS_C)):
            g = GEN[id(ps)]
            g.new_gen()
            fns = []
            for hh in range(8):
                h = 8 * r + hh
                st = g.start(TT * hh, TT * (hh + 1))
                def f(eng, h=h, hh=hh, ps=ps, st=st):
                    return eng.matmul(ps[:, TT * hh:TT * (hh + 1)],
                                      SKf[:, TT * (h // 2):TT * (h // 2) + TT],
                                      SKtz[:, TT * h:TT * (h + 1)],
                                      start=st, stop=True)
                fns.append(f)
            w = [ev_sktz, ev_omix1 if r == 0 else ev_denq]
            if r == 1:
                w.append(ev_den)
            ev_mm = pe(fns, waits=w, tag=tg + f".kt{r}")
            def fe(eng, r=r, ps=ps):
                return eng.tensor_tensor(
                    out=KTm[:, 1024 * r:1024 * (r + 1)].rearrange(
                        "p (h k) -> p h k", k=TT),
                    in0=ps.rearrange("p (h k) -> p h k", k=TT),
                    in1=mupb, op=ALU.mult)
            ev_KT[r] = vop(fe, waits=[ev_mm], writes=("KTm",),
                           tag=tg + f".kte{r}")
        ev_Sq = [None, None]
        for r, ps in ((0, PS_B), (1, PS_C)):
            g = GEN[id(ps)]
            g.new_gen()
            fns = []
            for hh in range(8):
                h = 8 * r + hh
                st = g.start(TT * hh, TT * (hh + 1))
                def f(eng, h=h, hh=hh, ps=ps, st=st):
                    return eng.matmul(ps[:, TT * hh:TT * (hh + 1)],
                                      SKf[:, TT * (h // 2):TT * (h // 2) + TT],
                                      SQz[:, TT * h:TT * (h + 1)],
                                      start=st, stop=True)
                fns.append(f)
            ev_mm = pe(fns, waits=[ev_KT[r]], tag=tg + f".sqm{r}")
            def fe(eng, r=r, ps=ps):
                return eng.tensor_tensor(
                    out=SmU[:, 1024 * r:1024 * (r + 1)].rearrange(
                        "p (h k) -> p h k", k=TT),
                    in0=ps.rearrange("p (h k) -> p h k", k=TT),
                    in1=mupb, op=ALU.mult)
            ev_Sq[r] = vop(fe, waits=[ev_mm], writes=("SmU",),
                           tag=tg + f".sqe{r}")

        GA.new_gen()
        fns = []
        for h in range(HEADS):
            st = GA.start(DH * h, DH * (h + 1))
            mcol = TT * (h // 2) + DH * (h % 2)
            def f(eng, h=h, st=st, mcol=mcol):
                return eng.matmul(PS_A[:, DH * h:DH * (h + 1)],
                                  SKtz[:, TT * h:TT * (h + 1)],
                                  membf[:, mcol:mcol + DH],
                                  start=st, stop=True)
            fns.append(f)
        ev_pv = pe(fns, waits=[ev_sktz, ev_membf], tag=tg + ".pv")
        ev_b = vop(lambda eng: eng.scalar_tensor_tensor(
            out=bvec, in0=PS_A, scalar=-1.0, in1=v_bf, op0=ALU.mult,
            op1=ALU.add), waits=[ev_pv, ev_ve], writes=("bvec",),
            tag=tg + ".b")
        ev_bbf = gop(lambda eng: eng.tensor_copy(out=b_bf, in_=bvec),
                     waits=[ev_b], reads=("bvec",), writes=("b_bf",),
                     tag=tg + ".bbf")

        prev_reader = {id(PS_B): ev_Sq[0], id(PS_C): ev_Sq[1]}
        g_in, ev_in = b_bf, ev_bbf
        ev_u = None
        ev_h3_mm = None
        for i3, (ps, dst) in enumerate(((PS_B, g1), (PS_C, g2),
                                        (PS_B, uvec))):
            g = GEN[id(ps)]
            g.new_gen()
            fns = []
            for h in range(HEADS):
                st = g.start(DH * h, DH * (h + 1))
                def f(eng, h=h, ps=ps, g_in=g_in, st=st):
                    return eng.matmul(ps[:, DH * h:DH * (h + 1)],
                                      KTm[:, TT * h:TT * (h + 1)],
                                      g_in[:, DH * h:DH * (h + 1)],
                                      start=st, stop=True)
                fns.append(f)
            ev_mm = pe(fns, waits=[ev_in, prev_reader[id(ps)], ev_KT[1]],
                       tag=tg + f".h{i3}")
            ev_h3_mm = ev_mm
            def fe(eng, ps=ps, dst=dst):
                return eng.scalar_tensor_tensor(out=dst, in0=ps, scalar=-1.0,
                                                in1=bvec, op0=ALU.mult,
                                                op1=ALU.add)
            evg = vop(fe, waits=[ev_mm], reads=("bvec",), writes=(dst.name,),
                      tag=tg + f".g{i3}")
            prev_reader[id(ps)] = evg
            g_in, ev_in = dst, evg
        ev_u = ev_in

        GC.new_gen()
        fns = []
        for h in range(HEADS):
            GC.start(DH * h, DH * (h + 1))
            mcol = TT * (h // 2) + DH * (h % 2)
            def f(eng, h=h, mcol=mcol):
                return eng.matmul(PS_C[:, DH * h:DH * (h + 1)],
                                  SQz[:, TT * h:TT * (h + 1)],
                                  membf[:, mcol:mcol + DH],
                                  start=True, stop=False)
            fns.append(f)
            def f2(eng, h=h):
                return eng.matmul(PS_C[:, DH * h:DH * (h + 1)],
                                  SmU[:, TT * h:TT * (h + 1)],
                                  uvec[:, DH * h:DH * (h + 1)],
                                  start=False, stop=True)
            fns.append(f2)
        ev_num = pe(fns, waits=[ev_u, ev_membf, prev_reader[id(PS_C)]],
                    tag=tg + ".num")
        ev_omix2 = vop(lambda eng: eng.tensor_tensor(
            out=o_acc.rearrange("p (h d) -> p h d", d=DH),
            in0=PS_C.rearrange("p (h d) -> p h d", d=DH), in1=bcast16(rdq),
            op=ALU.mult), waits=[ev_num, ev_rdq], reads=("rdq",),
            writes=("o_acc",), tag=tg + ".omix2")
        ev_o = gop(lambda eng: eng.tensor_tensor(out=o_bf, in0=o_acc,
                                                 in1=scr_f, op=ALU.add),
                   waits=[ev_omix2, ev_omix1], reads=("o_acc", "scr_f"),
                   writes=("q_bf",), tag=tg + ".o")

        fns = []
        for h in range(HEADS):
            def f(eng, h=h):
                ro = 64 * (h % 2)
                c0 = TT * (h // 2) + DH * (h % 2)
                return eng.matmul(PS_ST[ro:ro + 64, c0:c0 + DH],
                                  SKm[:, DH * h:DH * (h + 1)],
                                  uvec[:, DH * h:DH * (h + 1)],
                                  start=False, stop=True)
            fns.append(f)
        for h in range(HEADS):
            def f(eng, h=h):
                ro = 64 * (h % 2)
                return eng.matmul(PS_B[ro:ro + 64,
                                       1016 + (h // 2):1017 + (h // 2)],
                                  SKm[:, DH * h:DH * (h + 1)], onesc,
                                  start=True, stop=True)
            fns.append(f)
        ev_dmem = pe(fns, waits=[ev_u, ev_state_delta, ev_membf,
                                 prev_reader[id(PS_B)]],
                     tag=tg + ".dmem")
        ev_state_delta = ev_dmem
        zview_e = zbd[0:64, :].rearrange("p (c a) -> p a c", a=2)[:, 0, :]
        zview_o = zbd[64:128, :].rearrange("p (c a) -> p a c", a=2)[:, 1, :]
        ev_z1 = vop(lambda eng: eng.tensor_tensor(
            out=zview_e, in0=zview_e, in1=PS_B[0:64, 1016:1024], op=ALU.add),
            waits=[ev_dmem], reads=("zbd",), writes=("zbd",), tag=tg + ".z1")
        ev_z2 = vop(lambda eng: eng.tensor_tensor(
            out=zview_o, in0=zview_o, in1=PS_B[64:128, 1016:1024],
            op=ALU.add), waits=[ev_dmem], reads=("zbd",), writes=("zbd",),
            tag=tg + ".z2")
        ev_zbd_prev = vop(lambda eng: eng.tensor_copy(out=zbd_bf, in_=zbd),
                          waits=[ev_z1, ev_z2], reads=("zbd",),
                          writes=("zbd_bf",), tag=tg + ".zbf")
        ev_membf = aop(lambda eng: eng.copy(out=membf, in_=PS_ST),
                       waits=[ev_dmem], writes=("membf",), tag=tg + ".stev")

        ev_ofm = transpose_set(o_bf, o_fm, PS_B, 8,
                               waits=[ev_o, ev_z1, ev_z2], tag=tg + ".oT")
        GC.new_gen()
        fns = []
        for gidx in range(2):
            c0 = 512 * gidx
            for cch in range(8):
                def f(eng, cch=cch, c0=c0):
                    return eng.matmul(PS_C[:, c0:c0 + 512],
                                      o_fm[:, TT * cch:TT * (cch + 1)],
                                      WO[cch][:, c0:c0 + 512],
                                      start=(cch == 0), stop=(cch == 7))
                fns.append(f)
        ev_womm = pe(fns, waits=[ev_ofm, ev_omix2], tag=tg + ".wo")
        ev_ao = vop(lambda eng: eng.scalar_tensor_tensor(
            out=ao, in0=PS_C, scalar=0.0,
            in1=zerocol.broadcast_to([TT, D]), op0=ALU.add, op1=ALU.add,
            accum_out=sumv), waits=[ev_womm, ev_bbf],
            writes=("bvec", "sumv"), tag=tg + ".ao")
        ev_aosq = aop(lambda eng: eng.activation(out=sq_scr, in_=ao,
                                                 func=AF.Square,
                                                 accum_out=sumsq),
                      waits=[ev_ao], reads=("bvec",),
                      writes=("sq_scr", "sumsq"), tag=tg + ".aosq")
        P.step("gpsimd", lambda eng: eng.engine_nop(),
               waits=[ev_ao, ev_aosq], tag=tg + ".lnw")
        ln_stats_tail(1, tg + ".ln1")
        ev_rstd1 = rsqrt_chain(stats[:, 1:2], rstd[:, 1:2], tag=tg + ".ln1r")
        ev_t1 = vop(lambda eng: eng.tensor_scalar(
            out=scr_f, in0=ao, scalar1=negmu[:, 1:2], scalar2=rstd[:, 1:2],
            op0=ALU.add, op1=ALU.mult),
            waits=[ev_rstd1, ev_o], reads=("bvec", "negmu", "rstd"),
            writes=("scr_f",), tag=tg + ".lnap1")
        jslot = it % ST_TILES
        ev_x2 = vop(lambda eng, ib=ib, jslot=jslot: eng.tensor_tensor(
            out=x2[jslot], in0=scr_f, in1=xin[ib], op=ALU.add),
            waits=[ev_t1, ev_x2_free[jslot]], reads=("scr_f", f"xin{ib}"),
            writes=(f"x2_{jslot}",), tag=tg + ".x2")
        ev_x2_of[it] = ev_x2

        ev_rstd2 = ln_site(x2[jslot], f"x2_{jslot}", 2,
                           [ev_x2, ev_aosq], False, tg + ".ln2")
        ev_xn3 = vop(lambda eng, jslot=jslot: eng.tensor_scalar(
            out=xn3, in0=x2[jslot], scalar1=negmu[:, 2:3],
            scalar2=rstd[:, 2:3], op0=ALU.add, op1=ALU.mult),
            waits=[ev_rstd2], reads=(f"x2_{jslot}", "negmu", "rstd"),
            writes=("xn3",), tag=tg + ".xn3")
        ev_xn3f = transpose_set(xn3, xn3f, PS_B, 8, waits=[ev_xn3, ev_ofm],
                                tag=tg + ".xn3T")
        GC.new_gen()
        fns = []
        for cch in range(8):
            def f(eng, cch=cch):
                return eng.matmul(PS_C[:, 0:256],
                                  xn3f[:, TT * cch:TT * (cch + 1)],
                                  U1w[cch], start=(cch == 0), stop=(cch == 7))
            fns.append(f)
        ev_u1mm = pe(fns, waits=[ev_xn3f, ev_ao], tag=tg + ".u1")
        ev_p2 = aop(lambda eng: eng.copy(out=P2, in_=PS_C[:, 0:256]),
                    waits=[ev_u1mm], writes=("P2",), tag=tg + ".p2")
        fns = []
        for rc in range(2):
            def f(eng, rc=rc):
                return eng.matmul(PS_C[:, 256 + TT * rc:256 + TT * (rc + 1)],
                                  P2[:, TT * rc:TT * (rc + 1)], ident,
                                  start=True, stop=True)
            fns.append(f)
        ev_p2t = pe(fns, waits=[ev_p2], tag=tg + ".p2T")
        p2dst = P2fw.rearrange("p (r t k) -> p r t k", r=2, t=ST_TILES)
        def fe(eng, jslot=jslot):
            return eng.copy(out=p2dst[:, :, jslot, :],
                            in_=PS_C[:, 256:512].rearrange(
                                "p (r k) -> p r k", r=2))
        ev_p2fw = aop(fe, waits=[ev_p2t], writes=("P2fw",), tag=tg + ".p2fw")

        tap("xn2", ev_xn2, t0)
        tap("q", ev_qe, t0)
        tap("k", ev_ke, t0)
        tap("v", ev_ve, t0)
        tap("qqr", ev_qr, t0)
        tap("kkr", ev_kr, t0)
        tap("sq", ev_SQ, t0)
        tap("sk", ev_SK, t0)
        tap("u", ev_u, t0)
        tap("o", ev_o, t0)
        tap("aao", ev_ao, t0)
        tap("den", ev_den, t0)
        tap("denq", ev_denq, t0)
        tap("mem", ev_membf, t0)
        tap("lsum", ev_ttr, t0)
        tap("b", ev_b, t0)
        tap("ktm", ev_KT[1], t0)
        if jslot == 0:
            tap("x2_0", ev_x2, t0)
        tap("pm", ev_ttr, t0)
        tap("negmax", ev_ttr, t0)

        tap("xn3t", ev_xn3, t0)
        st_ctx[jslot] = dict(ev_p2fw=ev_p2fw, it=it)
        if jslot != ST_TILES - 1 and it != nt - 1:
            P.barrier(tg)
            continue

        nslots = jslot + 1
        ev_allp2 = [st_ctx[j]["ev_p2fw"] for j in range(nslots)]
        for _ev in ev_allp2:
            tap("p2f", _ev, 0)
        GA.new_gen()
        GB.new_gen()
        ev_gel = [None, None]
        ev_q2mm = None
        for fc in range(32):
            pp = fc % 2
            hb = Hbuf[pp]
            c0 = 512 * pp
            fns = []
            for r2 in range(2):
                def f(eng, r2=r2, fc=fc, c0=c0):
                    return eng.matmul(PS_A[:, c0:c0 + TT * nslots],
                                      V1w[r2][:, TT * fc:TT * (fc + 1)],
                                      P2fw[:, 512 * r2:512 * r2 + TT * nslots],
                                      start=(r2 == 0), stop=(r2 == 1))
                fns.append(f)
            if fc < 2:
                w = list(ev_allp2) + [ev_b, ev_pv]
            else:
                w = [ev_gel[pp]]
            ev_v1mm = pe(fns, waits=w, tag=tg + f".v1_{fc}")
            def fg(eng, fc=fc, hb=hb, c0=c0):
                return eng.activation(out=hb[:, 0:TT * nslots],
                                      in_=PS_A[:, c0:c0 + TT * nslots],
                                      func=AF.Gelu_apprx_tanh,
                                      bias=b1fm[:, fc:fc + 1])
            ev_g = aop(fg, waits=[ev_v1mm], writes=(hb.name,),
                       tag=tg + f".gel{fc}")
            fns = []
            if fc == 0:
                for j in range(nslots):
                    GB.start(RANK * j, RANK * (j + 1))
                    def fz(eng, j=j):
                        return eng.matmul(PS_B[:, RANK * j:RANK * (j + 1)],
                                          ones_row,
                                          zero_row[0:1, 0:RANK],
                                          start=True, stop=False)
                    fns.append(fz)
            for j in range(nslots):
                def f(eng, j=j, fc=fc, hb=hb):
                    return eng.matmul(PS_B[:, RANK * j:RANK * (j + 1)],
                                      hb[:, TT * j:TT * (j + 1)], U2w[fc],
                                      start=False, stop=(fc == 31))
                fns.append(f)
            w = [ev_g]
            if fc == 0:
                w += [ev_dmem, ev_z1, ev_z2, ev_xn3f]
            ev_q2mm = pe(fns, waits=w, tag=tg + f".u2_{fc}")
            ev_gel[pp] = ev_q2mm

        ev_prev_tail = None
        for j in reversed(range(nslots)):
            jt = st_ctx[j]["it"]
            jt0 = jt * TT
            ttg = f"t{jt}.ffn"
            ev_q2 = aop(lambda eng, j=j: eng.copy(
                out=Q2, in_=PS_B[:, RANK * j:RANK * (j + 1)]),
                waits=[ev_q2mm, ev_prev_tail], writes=("Q2",),
                tag=ttg + ".q2")
            GC.new_gen()
            fns = []
            for rc in range(2):
                st = GC.start(TT * rc, TT * (rc + 1))
                def f(eng, rc=rc, st=st):
                    return eng.matmul(PS_C[:, TT * rc:TT * (rc + 1)],
                                      Q2[:, TT * rc:TT * (rc + 1)], ident,
                                      start=st, stop=True)
                fns.append(f)
            ev_q2t = pe(fns, waits=[ev_q2, ev_p2fw], tag=ttg + ".q2T")
            ev_q2f = aop(lambda eng: eng.copy(out=Q2f, in_=PS_C[:, 0:256]),
                         waits=[ev_q2t], writes=("Q2f",), tag=ttg + ".q2f")
            GC.new_gen()
            fns = []
            for gidx in range(2):
                c0 = 512 * gidx
                for rc in range(2):
                    def f(eng, rc=rc, c0=c0):
                        return eng.matmul(PS_C[:, c0:c0 + 512],
                                          Q2f[:, TT * rc:TT * (rc + 1)],
                                          V2w[rc][:, c0:c0 + 512],
                                          start=(rc == 0), stop=(rc == 1))
                    fns.append(f)
            ev_v2 = pe(fns, waits=[ev_q2f], tag=ttg + ".v2")
            ev_f = vop(lambda eng: eng.scalar_tensor_tensor(
                out=fout, in0=PS_C, scalar=0.0, in1=b2bc, op0=ALU.add,
                op1=ALU.add, accum_out=sumv),
                waits=[ev_v2], writes=("o_acc", "sumv"), tag=ttg + ".f")
            tap("fs0", ev_f, jt0)
            tap("q2s0", ev_q2, jt0)
            ev_fsq = aop(lambda eng: eng.activation(out=sq_scr, in_=fout,
                                                    func=AF.Square,
                                                    accum_out=sumsq),
                         waits=[ev_f], reads=("o_acc",),
                         writes=("sq_scr", "sumsq"), tag=ttg + ".fsq")
            P.step("gpsimd", lambda eng: eng.engine_nop(),
                   waits=[ev_f, ev_fsq], tag=ttg + ".w")
            ln_stats_tail(3, ttg + ".ln3")
            ev_rstd3 = rsqrt_chain(stats[:, 1:2], rstd[:, 3:4],
                                   tag=ttg + ".ln3r")
            ev_t3 = vop(lambda eng: eng.tensor_scalar(
                out=scr_f, in0=fout, scalar1=negmu[:, 3:4],
                scalar2=rstd[:, 3:4], op0=ALU.add, op1=ALU.mult),
                waits=[ev_rstd3], reads=("o_acc", "negmu", "rstd"),
                writes=("scr_f",), tag=ttg + ".lap")
            ev_y = vop(lambda eng, j=j: eng.tensor_tensor(
                out=x2[j], in0=scr_f, in1=x2[j], op=ALU.add),
                waits=[ev_t3], reads=("scr_f",), writes=(f"x2_{j}",),
                tag=ttg + ".y")
            ev_od = dma(dout[jt0:jt0 + TT, :], x2[j], waits=[ev_y],
                        sem="dmao", tag=ttg + ".od")
            ev_x2_free[j] = ev_od
            ev_prev_tail = ev_t3
        st_ctx = {}
        P.barrier(tg + ".st")

    if "x2late" in taps:
        late = dict(x2late=x2[0], aol=bvec, scrfl=scr_f, negmul=negmu,
                    rstdl=rstd, sumvl=sumv, sumsql=sumsq, obl=q_bf,
                    oaccl=o_acc, lsuml=lsum, rll=rl, xinl=xin[0],
                    pml=Pm, pbl=Pbuf, negmaxl=negmax, qrl=qr, krzl=krz,
                    **({"canql": canq, "canqbl": canqb, "cancsl": cancs}
                       if canq is not None else {}))
        w = [(s, P.cnt[s]) for s in ("pe", "dve", "act", "pool")]
        for nm, buf in late.items():
            dl = nc.dram_tensor(f"tap_{nm}", list(buf.shape), buf.dtype,
                                kind="ExternalOutput").ap()
            def fxl(eng, dl=dl, buf=buf):
                return eng.dma_start(out=dl, in_=buf)
            P.step("sync", fxl, inc="dmao", inc_amt=16, tag=f"late.{nm}",
                   force=True, waits=w)
            w = []
    if P.frozen:
        def ffz(eng):
            return eng.dma_start(out=dout[0:TT, :], in_=xin[0])
        P.step("sync", ffz, inc="dmao", inc_amt=16, tag="final.frz",
               force=True)
    P.step("sync", lambda eng: eng.nop(),
           waits=[("dmao", P.cnt["dmao"])], tag="final", force=True)

    with (nc.semaphore("s_dmaw") as s_dmaw, nc.semaphore("s_dmax") as s_dmax,
          nc.semaphore("s_dmac") as s_dmac,
          nc.semaphore("s_dmao") as s_dmao, nc.semaphore("s_pe") as s_pe,
          nc.semaphore("s_dve") as s_dve, nc.semaphore("s_act") as s_act,
          nc.semaphore("s_pool") as s_pool,
          nc.semaphore("s_esync") as s_esync, nc.Block() as block):
        P.sems = dict(dmaw=s_dmaw, dmax=s_dmax, dmac=s_dmac, dmao=s_dmao,
                      pe=s_pe, dve=s_dve, act=s_act, pool=s_pool,
                      esync=s_esync)

        @block.sync
        def _(eng):
            P.run_engine(eng, "sync")

        @block.tensor
        def _(eng):
            P.run_engine(eng, "tensor")

        @block.vector
        def _(eng):
            P.run_engine(eng, "vector")

        @block.scalar
        def _(eng):
            P.run_engine(eng, "scalar")

        @block.gpsimd
        def _(eng):
            P.run_engine(eng, "gpsimd")

    return nc


def kernel(L0_attn_x, L0_attn_w_qkv, L0_attn_w_out, L0_attn_ln_g, L0_attn_ln_b,
           L0_attn_mem_beta, L0_ffn_U1, L0_ffn_V1, L0_ffn_b1, L0_ffn_U2,
           L0_ffn_V2, L0_ffn_b2):
    x = np.ascontiguousarray(np.asarray(L0_attn_x, dtype=np.float32))
    B = x.shape[0]
    nc = build(NT_FULL)
    common = dict(
        w_qkv=np.ascontiguousarray(L0_attn_w_qkv, dtype=np.float32),
        w_out=np.ascontiguousarray(L0_attn_w_out, dtype=np.float32),
        ln_g=np.asarray(L0_attn_ln_g, dtype=np.float32).reshape(1, D),
        ln_b=np.asarray(L0_attn_ln_b, dtype=np.float32).reshape(1, D),
        mem_beta=np.asarray(L0_attn_mem_beta,
                            dtype=np.float32).reshape(1, HEADS),
        U1=np.ascontiguousarray(L0_ffn_U1, dtype=np.float32),
        V1=np.ascontiguousarray(L0_ffn_V1, dtype=np.float32),
        b1=np.asarray(L0_ffn_b1, dtype=np.float32).reshape(1, DFF),
        U2=np.ascontiguousarray(L0_ffn_U2, dtype=np.float32),
        V2=np.ascontiguousarray(L0_ffn_V2, dtype=np.float32),
        b2=np.asarray(L0_ffn_b2, dtype=np.float32).reshape(1, D),
    )
    in_maps = [dict(common, x=np.ascontiguousarray(x[b])) for b in range(B)]
    res = run_bass_kernel_spmd(nc, in_maps, core_ids=list(range(B)))
    return np.stack([res.results[b]["out"] for b in range(B)], axis=0)


# version 05
# speedup vs baseline: 1.0344x; 1.0187x over previous
import numpy as np
import ml_dtypes
import concourse.bass as bass
import concourse.mybir as mybir
from concourse.bass_utils import run_bass_kernel_spmd

F32 = mybir.dt.float32
BF16 = mybir.dt.bfloat16
I32 = mybir.dt.int32
AF = mybir.ActivationFunctionType
ALU = mybir.AluOpType
AX = mybir.AxisListType
bfnp = ml_dtypes.bfloat16

HEADS, DH, SEG, TT = 16, 64, 32, 128
SCALE = DH ** -0.5
D, DQKV, DFF, RANK = 1024, 3072, 4096, 256
N_TOK = 8192
NT_FULL = N_TOK // TT
ST_TILES = 4
LN_EPS = 1e-5
MAGIC = 0x5F3759DF
ENGS = ("sync", "tensor", "vector", "scalar", "gpsimd")
ESEM = dict(sync="esync", tensor="pe", vector="dve", scalar="act",
            gpsimd="pool")


def _np_consts(n):
    ident = np.eye(TT, dtype=np.float32).astype(bfnp)
    segid = np.arange(TT) // SEG
    maskup = (segid[:, None] < segid[None, :]).astype(np.float32).astype(bfnp)
    causal = ((segid[:, None] == segid[None, :]) &
              (np.arange(TT)[:, None] >= np.arange(TT)[None, :]))
    causal = causal.astype(np.float32).astype(bfnp)
    j = np.arange(0, DH, 2, dtype=np.float32)
    freqs = (10000.0 ** (-j / DH)).astype(np.float32)
    ang = np.arange(n, dtype=np.float32)[:, None] * freqs[None, :]
    cs = np.concatenate([np.cos(ang), np.sin(ang)], axis=1).astype(np.float32)
    ones_row = np.ones((1, TT), dtype=np.float32).astype(bfnp)
    onesc_bf = np.ones((TT, 1), dtype=np.float32).astype(bfnp)
    zero_row = np.zeros((1, 512), dtype=np.float32).astype(bfnp)
    one_one = np.ones((1, 1), dtype=np.float32).astype(bfnp)
    magic_t = np.full((TT, 1), MAGIC, dtype=np.int32)
    zbd0 = np.zeros((TT, 16), dtype=np.float32)
    for c in range(8):
        zbd0[0:64, 2 * c] = 1e-6
        zbd0[64:128, 2 * c + 1] = 1e-6
    return dict(ident=ident, maskup=maskup, causal=causal, cs=cs,
                ones_row=ones_row, onesc_bf=onesc_bf, zero_row=zero_row,
                one_one=one_one, magic=magic_t, zbd0=zbd0)


class Plan:
    def __init__(self, until_tag=None):
        self.until = until_tag
        self.frozen = False
        self.steps = []
        self.cnt = {k: 0 for k in
                    ("dmaw", "dmax", "dmac", "dmao", "pe", "dve", "act",
                     "pool", "esync")}
        self.sems = {}
        self.pending_barrier = {}

    def step(self, engine, fn, waits=(), inc=None, inc_amt=1, reads=(),
             writes=(), tag="", force=False):
        if self.until is not None and self.until in tag:
            self.frozen = True
        if self.frozen and not force:
            if inc is not None:
                return (inc, self.cnt[inc])
            return None
        ev = None
        if inc is not None:
            self.cnt[inc] += inc_amt
            ev = (inc, self.cnt[inc])
        w = [x for x in waits if x is not None]
        if engine in self.pending_barrier:
            w = self.pending_barrier.pop(engine) + w
        self.steps.append(dict(engine=engine, fn=fn, waits=w, inc=inc,
                               inc_amt=inc_amt, reads=set(reads),
                               writes=set(writes), tag=tag))
        return ev

    def barrier(self, tag=""):
        prev = None
        for e in ENGS:
            sem = ESEM[e]
            def f(eng, e=e):
                return (eng.engine_nop() if e in ("vector", "gpsimd")
                        else eng.nop())
            prev = self.step(e, f, waits=[prev] if prev else (),
                             inc=sem, tag=tag + f".bar.{e}")
        for e in ENGS[:-1]:
            self.pending_barrier[e] = [prev]

    def run_engine(self, eng, engine_name):
        last_writes = set()
        for st in self.steps:
            if st["engine"] != engine_name:
                continue
            waited = False
            for semname, val in st["waits"]:
                eng.wait_ge(self.sems[semname], val)
                waited = True
            if waited:
                last_writes = set()
            if st["reads"] & last_writes or st["writes"] & last_writes:
                if engine_name in ("vector", "gpsimd"):
                    eng.engine_nop()
                elif engine_name == "scalar":
                    eng.drain()
            inst = st["fn"](eng)
            if st["inc"] is not None:
                assert inst is not None, st["tag"]
                inst.then_inc(self.sems[st["inc"]], st["inc_amt"])
            last_writes = st["writes"]


class PsumGen:

    def __init__(self):
        self.regions = set()

    def new_gen(self):
        self.regions = set()

    def start(self, col_lo, col_hi):
        key = (col_lo, col_hi)
        st = key not in self.regions
        self.regions.add(key)
        return st


def build(nt=NT_FULL, taps=(), until_tag=None):
    nc = bass.Bass("TRN2", target_bir_lowering=False, debug=False)
    n = nt * TT
    C = _np_consts(n)

    dx = nc.dram_tensor("x", [n, D], F32, kind="ExternalInput").ap()
    dwq = nc.dram_tensor("w_qkv", [D, DQKV], F32, kind="ExternalInput").ap()
    dwo = nc.dram_tensor("w_out", [D, D], F32, kind="ExternalInput").ap()
    dlng = nc.dram_tensor("ln_g", [1, D], F32, kind="ExternalInput").ap()
    dlnb = nc.dram_tensor("ln_b", [1, D], F32, kind="ExternalInput").ap()
    dbeta = nc.dram_tensor("mem_beta", [1, HEADS], F32, kind="ExternalInput").ap()
    dU1 = nc.dram_tensor("U1", [D, RANK], F32, kind="ExternalInput").ap()
    dV1 = nc.dram_tensor("V1", [RANK, DFF], F32, kind="ExternalInput").ap()
    db1 = nc.dram_tensor("b1", [1, DFF], F32, kind="ExternalInput").ap()
    dU2 = nc.dram_tensor("U2", [DFF, RANK], F32, kind="ExternalInput").ap()
    dV2 = nc.dram_tensor("V2", [RANK, D], F32, kind="ExternalInput").ap()
    db2 = nc.dram_tensor("b2", [1, D], F32, kind="ExternalInput").ap()
    dout = nc.dram_tensor("out", [n, D], F32, kind="ExternalOutput").ap()

    dident = nc.inline_tensor(C["ident"], "c_ident").ap()
    dmaskup = nc.inline_tensor(C["maskup"], "c_maskup").ap()
    dcausal = nc.inline_tensor(C["causal"], "c_causal").ap()
    dcs = nc.inline_tensor(C["cs"], "c_cs").ap()
    dones_row = nc.inline_tensor(C["ones_row"], "c_onesrow").ap()
    donesc = nc.inline_tensor(C["onesc_bf"], "c_onescol").ap()
    dzero_row = nc.inline_tensor(C["zero_row"], "c_zerorow").ap()
    done_one = nc.inline_tensor(C["one_one"], "c_oneone").ap()
    dmagic = nc.inline_tensor(C["magic"], "c_magic").ap()
    dzbd0 = nc.inline_tensor(C["zbd0"], "c_zbd0").ap()

    S = lambda name, shape, dt=BF16: nc.alloc_sbuf_tensor(name, shape, dt).ap()
    WQ = [S(f"WQ{c}", [TT, DQKV]) for c in range(8)]
    WO = [S(f"WO{c}", [TT, D]) for c in range(8)]
    U1w = [S(f"U1_{c}", [TT, RANK]) for c in range(8)]
    V1w = [S(f"V1_{r}", [TT, DFF]) for r in range(2)]
    U2w = [S(f"U2_{c}", [TT, RANK]) for c in range(32)]
    V2w = [S(f"V2_{r}", [TT, D]) for r in range(2)]
    bq_row = S("bq_row", [1, DQKV])
    b2bc = S("b2bc", [TT, D], F32)
    b1fm = S("b1fm", [TT, 32], F32)
    beta_bc = S("beta_bc", [TT, HEADS], F32)
    ombeta_bc = S("ombeta_bc", [TT, HEADS], F32)
    lngf = S("lngf", [TT, 8], F32)
    lnbf = S("lnbf", [TT, 8])
    ident = S("ident", [TT, TT])
    maskup = S("maskup", [TT, TT])
    causal = S("causal", [TT, TT])
    ones_row = S("ones_row", [1, TT])
    onesc = S("onesc", [TT, 1])
    zero_row = S("zero_row", [1, 512])
    one_one = S("one_one", [1, 1])
    magic = S("magic", [TT, 1], I32)
    zerocol = S("zerocol", [TT, 1], F32)

    zbd = S("zbd", [TT, HEADS], F32)
    zbd_bf = S("zbd_bf", [TT, HEADS])
    membf = S("membf", [TT, 1024])

    xin = [S(f"xin{i}", [TT, D], F32) for i in range(2)]
    csb = [S(f"cs{i}", [TT, 64], F32) for i in range(4)]
    xn2 = S("xn2", [TT, D])
    q_bf = S("q_bf", [TT, D])
    k_bf = S("k_bf", [TT, D])
    v_bf = S("v_bf", [TT, D])
    qr = S("qr", [TT, D])
    kr = S("kr", [TT, D])
    SQm = S("SQm", [TT, D])
    SKm = S("SKm", [TT, D])
    SKf = S("SKf", [TT, D])
    krz = S("krz", [TT, 2048])
    SQz = S("SQz", [TT, 2048])
    SKtz = S("SKtz", [TT, 2048])
    Pbuf = S("Pbuf", [TT, 2048])
    Pm = S("Pm", [TT, 2048])
    bvec = S("bvec", [TT, D], F32)
    g1 = S("g1", [TT, D])
    o_acc = S("o_acc", [TT, D], F32)
    x2 = [S(f"x2_{i}", [TT, D], F32) for i in range(ST_TILES)]
    P2 = S("P2", [TT, RANK])
    P2fw = S("P2fw", [TT, 2 * ST_TILES * TT])
    Hbuf = [S(f"Hbuf{i}", [TT, ST_TILES * TT]) for i in range(2)]
    Q2 = S("Q2", [TT, RANK])
    scr_bf = S("scr_bf", [TT, 1024])
    scr_f = S("scr_f", [TT, D], F32)
    sumv = S("sumv", [TT, 1], F32)
    sumsq = S("sumsq", [TT, 1], F32)
    stats = S("stats", [TT, 16], F32)
    negmu = S("negmu", [TT, 4], F32)
    rstd = S("rstd", [TT, 4], F32)
    den = S("den", [TT, HEADS], F32)
    denq = S("denq", [TT, HEADS], F32)
    rden = S("rden", [TT, HEADS], F32)
    rdq = S("rdq", [TT, HEADS], F32)
    negmax = S("negmax", [TT, HEADS], F32)
    lsum = S("lsum", [TT, HEADS], F32)
    rl = S("rl", [TT, HEADS], F32)

    canq = S("canq", [TT, D]) if "canq" in taps else None
    canqb = S("canqb", [TT, D]) if "canq" in taps else None
    cancs = S("cancs", [TT, 64], F32) if "canq" in taps else None
    xn2f, qrf = xn2, qr
    SKt = qr
    xn3 = SQm
    xn3f = xn3
    b_bf = v_bf
    g2 = k_bf
    o_bf, o_fm, ao, fout = q_bf, k_bf, bvec, o_acc
    PTb = Pbuf
    KTm = Pbuf
    SmU = S("SmU_dbg", [TT, 2048]) if "pm" in taps else Pm
    if "xn2" in taps:
        xn2f = S("xn2f_dbg", [TT, D])
    uvec = g1
    Q2f = S("Q2f_dbg", [TT, 2 * TT]) if "q2s0" in taps else Q2
    sq_scr = scr_bf[:, 0:1024]
    rowf = scr_f[0:1, 0:1024]
    rowbf = scr_bf[0:1, 0:1024]

    tap_srcs = dict(xn2=xn2, q=q_bf, k=k_bf, v=v_bf, qqr=qr, kkr=kr, sq=SQm,
                    canq=canq, canqb=canqb, cancs=cancs, bqr=qr, bpb=Pbuf,
                    negmax=negmax, fs0=o_acc, q2s0=Q2, p2f=P2fw, xn3t=xn3,
                    hs0=Hbuf[1],
                    sk=SKm, u=uvec, o=q_bf, aao=bvec, den=den, denq=denq,
                    mem=membf, lsum=lsum, x2_0=x2[0], b=bvec, ktm=KTm,
                    pm=Pm)
    dtap = {}
    for tname in taps:
        if tname == "x2late":
            continue
        src = tap_srcs[tname]
        dtap[tname] = nc.dram_tensor(f"tap_{tname}",
                                     [n, src.shape[-1]], src.dtype,
                                     kind="ExternalOutput").ap()

    PS_ST = nc.alloc_psum_tensor("ps_state", [TT, 1024], F32).ap()
    PS_A = nc.alloc_psum_tensor("ps_a", [TT, 1024], F32).ap()
    PS_B = nc.alloc_psum_tensor("ps_b", [TT, 1024], F32).ap()
    PS_C = nc.alloc_psum_tensor("ps_c", [TT, 1024], F32).ap()
    GA, GB, GC = PsumGen(), PsumGen(), PsumGen()
    GEN = {id(PS_A): GA, id(PS_B): GB, id(PS_C): GC}

    P = Plan(until_tag=until_tag)

    def pe(fns, waits=(), tag=""):
        def f(eng):
            last = None
            for fn in fns:
                last = fn(eng)
            return last
        return P.step("tensor", f, waits=waits, inc="pe", tag=tag)

    def vop(fn, waits=(), reads=(), writes=(), tag=""):
        return P.step("vector", fn, waits=waits, inc="dve", reads=reads,
                      writes=writes, tag=tag)

    def aop(fn, waits=(), reads=(), writes=(), tag=""):
        return P.step("scalar", fn, waits=waits, inc="act", reads=reads,
                      writes=writes, tag=tag)

    def gop(fn, waits=(), reads=(), writes=(), tag=""):
        return P.step("gpsimd", fn, waits=waits, inc="pool", reads=reads,
                      writes=writes, tag=tag)

    def dma(out, in_, waits=(), sem="dmaw", tag=""):
        def f(eng):
            return eng.dma_start(out=out, in_=in_)
        return P.step("sync", f, waits=waits, inc=sem, inc_amt=16, tag=tag)

    def hsl(t, h):
        return t[64 * (h % 2):64 * (h % 2) + 64,
                 TT * (h // 2):TT * (h // 2) + TT]

    def transpose_set(src, dst, ps, nchunks, waits, tag=""):
        g = GEN[id(ps)]
        g.new_gen()
        fns = []
        for c in range(nchunks):
            st = g.start(TT * c, TT * (c + 1))
            def f(eng, c=c, st=st):
                return eng.matmul(ps[:, TT * c:TT * (c + 1)],
                                  src[:, TT * c:TT * (c + 1)], ident,
                                  start=st, stop=True)
            fns.append(f)
        ev = pe(fns, waits=waits, tag=tag + ".t")
        def fe(eng):
            return eng.copy(out=dst[:, 0:TT * nchunks],
                            in_=ps[:, 0:TT * nchunks])
        return aop(fe, waits=[ev], writes=(dst.name,), tag=tag + ".ev")

    def transpose_zsplit(src, dst, ps, waits, tag=""):
        g = GEN[id(ps)]
        g.new_gen()
        fns = []
        for c in range(8):
            st = g.start(TT * c, TT * (c + 1))
            def f(eng, c=c, st=st):
                return eng.matmul(ps[:, TT * c:TT * (c + 1)],
                                  src[:, TT * c:TT * (c + 1)], ident,
                                  start=st, stop=True)
            fns.append(f)
        ev = pe(fns, waits=waits, tag=tag + ".t")
        top = dst.rearrange("p (c b k) -> p c b k", b=2, k=TT)[0:64, :, 0, :]
        bot = dst.rearrange("p (c b k) -> p c b k", b=2, k=TT)[64:128, :, 1, :]
        pst = ps.rearrange("p (c k) -> p c k", k=TT)
        e1 = aop(lambda eng: eng.copy(out=top, in_=pst[0:64]),
                 waits=[ev], writes=(dst.name,), tag=tag + ".e1")
        e2 = aop(lambda eng: eng.copy(out=bot, in_=pst[64:128]),
                 waits=[ev], writes=(dst.name,), tag=tag + ".e2")
        return e2

    def rsqrt_chain(src_col, dst_col, tag=""):
        s1, s2 = stats[:, 2:3], stats[:, 3:4]
        i1 = s1.bitcast(I32)
        fns = [
            lambda eng: eng.tensor_scalar(out=i1, in0=src_col.bitcast(I32),
                                          scalar1=0.5, scalar2=None,
                                          op0=ALU.mult),
            lambda eng: eng.tensor_tensor(out=i1, in0=magic, in1=i1,
                                          op=ALU.subtract),
            lambda eng: eng.tensor_tensor(out=s2, in0=s1, in1=s1, op=ALU.mult),
            lambda eng: eng.tensor_tensor(out=s2, in0=s2, in1=src_col,
                                          op=ALU.mult),
            lambda eng: eng.tensor_scalar(out=s2, in0=s2, scalar1=-0.5,
                                          scalar2=1.5, op0=ALU.mult,
                                          op1=ALU.add),
            lambda eng: eng.tensor_tensor(out=dst_col, in0=s1, in1=s2,
                                          op=ALU.mult),
        ]
        ev = None
        for i, fn in enumerate(fns):
            ev = P.step("gpsimd", fn, reads=("stats",),
                        writes=("stats", "rstd"),
                        inc=("pool" if i == len(fns) - 1 else None),
                        tag=tag + f".rs{i}")
        return ev

    def ln_stats_tail(site, tag):
        st = stats
        P.step("gpsimd", lambda eng: eng.tensor_scalar(
            out=negmu[:, site:site + 1], in0=sumv, scalar1=-1.0 / D,
            scalar2=None, op0=ALU.mult), reads=("sumv",), writes=("negmu",),
            tag=tag + ".mu")
        P.step("gpsimd", lambda eng: eng.tensor_tensor(
            out=st[:, 1:2], in0=negmu[:, site:site + 1],
            in1=negmu[:, site:site + 1], op=ALU.mult),
            reads=("negmu",), writes=("stats",), tag=tag + ".m2")
        P.step("gpsimd", lambda eng: eng.tensor_scalar(
            out=st[:, 0:1], in0=sumsq, scalar1=1.0 / D, scalar2=None,
            op0=ALU.mult), reads=("sumsq",), writes=("stats",), tag=tag + ".s2")
        P.step("gpsimd", lambda eng: eng.tensor_tensor(
            out=st[:, 0:1], in0=st[:, 0:1], in1=st[:, 1:2], op=ALU.subtract),
            reads=("stats",), writes=("stats",), tag=tag + ".var")
        P.step("gpsimd", lambda eng: eng.tensor_scalar(
            out=st[:, 1:2], in0=st[:, 0:1], scalar1=LN_EPS, scalar2=None,
            op0=ALU.add), reads=("stats",), writes=("stats",), tag=tag + ".ve")

    def ln_site(xten, xname, site, waits, fused, tag):
        ev_sq = aop(lambda eng: eng.activation(out=sq_scr, in_=xten,
                                               func=AF.Square,
                                               accum_out=sumsq),
                    waits=waits, reads=(xname,), writes=("sq_scr", "sumsq"),
                    tag=tag + ".sq")
        ev_sum = vop(lambda eng: eng.tensor_reduce(out=sumv, in_=xten,
                                                   axis=AX.X, op=ALU.add),
                     waits=waits, reads=(xname,), writes=("sumv",),
                     tag=tag + ".sum")
        P.step("gpsimd", lambda eng: eng.engine_nop(),
               waits=[ev_sq, ev_sum], tag=tag + ".w")
        ln_stats_tail(site, tag)
        st = stats
        if not fused:
            return rsqrt_chain(st[:, 1:2], rstd[:, site:site + 1], tag=tag)
        rsqrt_chain(st[:, 1:2], st[:, 4:5], tag=tag + ".r1")
        P.step("gpsimd", lambda eng: eng.tensor_tensor(
            out=st[:, 5:6], in0=st[:, 4:5], in1=st[:, 4:5], op=ALU.mult),
            reads=("stats",), writes=("stats",), tag=tag)
        P.step("gpsimd", lambda eng: eng.tensor_tensor(
            out=st[:, 5:6], in0=st[:, 5:6], in1=st[:, 0:1], op=ALU.mult),
            reads=("stats",), writes=("stats",), tag=tag)
        P.step("gpsimd", lambda eng: eng.tensor_scalar(
            out=st[:, 5:6], in0=st[:, 5:6], scalar1=LN_EPS, scalar2=None,
            op0=ALU.add), reads=("stats",), writes=("stats",), tag=tag)
        rsqrt_chain(st[:, 5:6], st[:, 6:7], tag=tag + ".r2")
        return P.step("gpsimd", lambda eng: eng.tensor_tensor(
            out=rstd[:, site:site + 1], in0=st[:, 4:5], in1=st[:, 6:7],
            op=ALU.mult), inc="pool", reads=("stats",), writes=("rstd",),
            tag=tag + ".r12")

    def bcast16(src16, cnt=16):
        return src16[:, 0:cnt].unsqueeze(2).broadcast_to([TT, cnt, DH])

    def tap(name, ev, t0):
        if name in dtap:
            src = tap_srcs[name]
            dma(dtap[name][t0:t0 + TT, :], src, waits=[ev], sem="dmao",
                tag=f"tap.{name}")

    ev_xd_pre, ev_cd_pre = {}, {}
    prev_xd, prev_cd = (), ()
    for it in range(min(2, nt)):
        ev_xd_pre[it] = dma(xin[it % 2], dx[it * TT:(it + 1) * TT, :],
                            sem="dmax", waits=prev_xd,
                            tag=f"prep.xd{it}")
        prev_xd = [ev_xd_pre[it]]
    for it in range(min(4, nt)):
        ev_cd_pre[it] = dma(csb[it % 4], dcs[it * TT:(it + 1) * TT, :],
                            sem="dmac", waits=prev_cd,
                            tag=f"prep.cd{it}")
        prev_cd = [ev_cd_pre[it]]
    for out, in_ in [(ident, dident), (maskup, dmaskup), (causal, dcausal),
                     (ones_row, dones_row), (onesc, donesc),
                     (zero_row, dzero_row), (one_one, done_one),
                     (magic, dmagic), (zbd, dzbd0)]:
        ev_consts = dma(out, in_, tag="prep.c")
    vop(lambda eng: eng.memset(zerocol, 0.0), writes=("zerocol",),
        tag="prep.zc")
    for tz, nm in ((krz, "krz"), (SQz, "SQz"), (SKtz, "SKtz")):
        gop(lambda eng, tz=tz: eng.memset(tz, 0.0), writes=(nm,),
            tag="prep.z" + nm)
    _zero_all = [xn2, q_bf, k_bf, v_bf, qr, kr, SQm, SKm, SKf, Pbuf, Pm,
                 bvec, g1, o_acc, sumv, sumsq, stats, negmu,
                 rstd, den, denq, rden, rdq, negmax, lsum, rl,
                 P2, P2fw, Hbuf[0], Hbuf[1], Q2] + x2
    for zi, zt in enumerate(_zero_all):
        vop(lambda eng, zt=zt: eng.memset(zt, 0.0), writes=("zz",),
            tag=f"prep.zz{zi}")
    ev_zbdbf = vop(lambda eng: eng.tensor_copy(out=zbd_bf, in_=zbd),
                   waits=[ev_consts], writes=("zbd_bf",), tag="prep.zbdbf")

    GST = PsumGen()
    _stz = [
        (lambda eng: eng.matmul(PS_ST[:, 0:512], ones_row, zero_row,
                                start=True, stop=True)),
        (lambda eng: eng.matmul(PS_ST[:, 512:1024], ones_row, zero_row,
                                start=True, stop=True))]
    for h in range(HEADS):
        ro = 64 * (h % 2)
        c0 = TT * (h // 2) + DH * (h % 2)
        def fz(eng, ro=ro, c0=c0):
            return eng.matmul(PS_ST[ro:ro + 64, c0:c0 + DH],
                              ones_row[0:1, 0:64], zero_row[0:1, 0:DH],
                              start=True, stop=True)
        _stz.append(fz)
    ev_stinit = pe(_stz, waits=[ev_consts], tag="prep.stzero")
    ev_membf = aop(lambda eng: eng.copy(out=membf, in_=PS_ST),
                   waits=[ev_stinit], writes=("membf",), tag="prep.stev")

    def row_to_fm(drow, total, dst_f32, dst_bf, chain, tag):
        col = 0
        last = chain
        GB.new_gen()
        while col < total:
            w = min(1024, total - col)
            evd = dma(rowf[0:1, 0:w], drow[0:1, col:col + w],
                      waits=[last], tag=tag + f".d{col}")
            evc = vop(lambda eng, w=w: eng.tensor_copy(
                out=rowbf[0:1, 0:w].bitcast(BF16), in_=rowf[0:1, 0:w]),
                waits=[evd], reads=("rowf",), writes=("rowbf",),
                tag=tag + f".c{col}")
            fns = []
            for c in range(w // TT):
                cc = col // TT + c
                st = GB.start(cc, cc + 1)
                def f(eng, c=c, cc=cc, st=st):
                    return eng.matmul(PS_B[:, cc:cc + 1],
                                      rowbf[0:1, TT * c:TT * (c + 1)].bitcast(BF16),
                                      one_one, start=st, stop=True)
                fns.append(f)
            last = pe(fns, waits=[evc, ev_consts], tag=tag + f".m{col}")
            col += w
        ncol = total // TT
        evs = last
        if dst_f32 is not None:
            evs = vop(lambda eng: eng.tensor_copy(out=dst_f32,
                                                  in_=PS_B[:, 0:ncol]),
                      waits=[last], writes=(dst_f32.name,), tag=tag + ".e")
        if dst_bf is not None:
            evs = aop(lambda eng: eng.copy(out=dst_bf, in_=PS_B[:, 0:ncol]),
                      waits=[last], writes=(dst_bf.name,), tag=tag + ".eb")
        return evs

    ev = row_to_fm(dlng, D, lngf, None, ev_consts, "prep.lng")
    ev = row_to_fm(dlnb, D, None, lnbf, ev, "prep.lnb")
    ev_b1 = row_to_fm(db1, DFF, b1fm, None, ev, "prep.b1")

    ev_bd = dma(stats[0:1, 0:16], dbeta, waits=[ev_b1], tag="prep.beta.d")
    ev_sig = aop(lambda eng: eng.activation(out=rowbf[0:1, 0:16].bitcast(BF16),
                                            in_=stats[0:1, 0:16],
                                            func=AF.Sigmoid),
                 waits=[ev_bd], writes=("rowbf",), tag="prep.beta.sig")
    GC.new_gen()
    ev_bmm = pe([lambda eng: eng.matmul(PS_C[:, 0:16], ones_row,
                                        rowbf[0:1, 0:16].bitcast(BF16),
                                        start=True, stop=True)],
                waits=[ev_sig], tag="prep.beta.mm")
    ev_beta = vop(lambda eng: eng.tensor_copy(out=beta_bc, in_=PS_C[:, 0:16]),
                  waits=[ev_bmm], writes=("beta_bc",), tag="prep.beta.e")
    ev_ombeta = vop(lambda eng: eng.tensor_scalar(
        out=ombeta_bc, in0=beta_bc, scalar1=-1.0, scalar2=1.0, op0=ALU.mult,
        op1=ALU.add), waits=[ev_beta], reads=("beta_bc",),
        writes=("ombeta_bc",), tag="prep.beta.om")

    ev_b2d = dma(rowf[0:1, 0:D], db2, waits=[ev_ombeta], tag="prep.b2.d")
    ev_b2c = vop(lambda eng: eng.tensor_copy(out=rowbf[0:1, 0:D].bitcast(BF16),
                                             in_=rowf[0:1, 0:D]),
                 waits=[ev_b2d], reads=("rowf",), writes=("rowbf",),
                 tag="prep.b2.c")
    GC.new_gen()
    ev_b2m = pe([lambda eng: eng.matmul(PS_C[:, 0:512], ones_row,
                                        rowbf[0:1, 0:512].bitcast(BF16),
                                        start=True, stop=True),
                 lambda eng: eng.matmul(PS_C[:, 512:1024], ones_row,
                                        rowbf[0:1, 512:1024].bitcast(BF16),
                                        start=True, stop=True)],
                waits=[ev_b2c, ev_beta], tag="prep.b2.mm")
    ev_b2 = vop(lambda eng: eng.tensor_copy(out=b2bc, in_=PS_C),
                waits=[ev_b2m], writes=("b2bc",), tag="prep.b2.e")

    GA.new_gen()
    bq_ps = {0: PS_A, 1: PS_C, 2: PS_B}
    ev_last = ev_b2
    ev_bq_mm = {0: None, 1: None, 2: None}
    for cch in range(8):
        for blk in range(3):
            evd = dma(scr_f, dwq[TT * cch:TT * (cch + 1),
                                 1024 * blk:1024 * (blk + 1)],
                      waits=[ev_last], tag=f"prep.wq{cch}.{blk}.d")
            evc = vop(lambda eng: eng.tensor_copy(out=scr_bf[:, 0:1024],
                                                  in_=scr_f),
                      waits=[evd], reads=("scr_f",), writes=("scr_bf",),
                      tag=f"prep.wq{cch}.{blk}.c")
            ps = bq_ps[blk]
            fns = []
            for g in range(2):
                def f(eng, g=g, ps=ps, cch=cch):
                    return eng.matmul(ps[0:1, 512 * g:512 * (g + 1)],
                                      lnbf[:, cch:cch + 1],
                                      scr_bf[:, 512 * g:512 * (g + 1)],
                                      start=(cch == 0), stop=(cch == 7))
                fns.append(f)
            ev_bq_mm[blk] = pe(fns, waits=[evc], tag=f"prep.bq{cch}.{blk}")
            dst = WQ[cch][:, 1024 * blk:1024 * (blk + 1)]
            if blk < 2:
                dst = dst.rearrange("p (h a j) -> p h a j", h=16, a=2,
                                    j=32).transpose([0, 1, 3, 2])
            def fw(eng, dst=dst, cch=cch):
                return eng.tensor_scalar(out=dst, in0=scr_f,
                                         scalar1=lngf[:, cch:cch + 1],
                                         scalar2=None, op0=ALU.mult)
            ev_last = vop(fw, waits=[ev_bq_mm[blk]], reads=("scr_f",),
                          writes=(f"WQ{cch}",), tag=f"prep.wqw{cch}.{blk}")
    ev_bq = ev_last
    for blk, ps in ((0, PS_A), (1, PS_C), (2, PS_B)):
        src = ps[0:1, 0:1024]
        dst = bq_row[0:1, 1024 * blk:1024 * (blk + 1)]
        if blk < 2:
            dst = dst.rearrange("p (h a j) -> p h a j", h=16, a=2,
                                j=32).transpose([0, 1, 3, 2])
        ev_bq = aop(lambda eng, dst=dst, src=src: eng.copy(out=dst, in_=src),
                    waits=[ev_last], writes=("bq_row",), tag=f"prep.bqe{blk}")

    def load_cast(pieces, piece_cols, chain, tag):
        last = chain
        for i, (dst, src) in enumerate(pieces):
            evd = dma(scr_f[:, 0:piece_cols], src, waits=[last],
                      tag=f"{tag}.d{i}")
            last = vop(lambda eng, dst=dst: eng.tensor_copy(
                out=dst, in_=scr_f[:, 0:piece_cols]),
                waits=[evd], reads=("scr_f",), writes=("wgt",),
                tag=f"{tag}.c{i}")
        return last

    if "xn2" in taps:
        dbq = nc.dram_tensor("tap_bq", [1, DQKV], BF16, kind="ExternalOutput").ap()
        dwq0 = nc.dram_tensor("tap_wq0", [TT, DQKV], BF16, kind="ExternalOutput").ap()
        dlngf = nc.dram_tensor("tap_lngf", [TT, 8], F32, kind="ExternalOutput").ap()
        dma(dbq, bq_row, waits=[ev_bq], sem="dmao", tag="prep.tapbq")
        dlnbf = nc.dram_tensor("tap_lnbf", [TT, 8], BF16, kind="ExternalOutput").ap()
        dma(dlnbf, lnbf, waits=[ev_bq], sem="dmao", tag="prep.taplnbf")
        dma(dwq0, WQ[0], waits=[ev_bq], sem="dmao", tag="prep.tapwq")
        dma(dlngf, lngf, waits=[ev_bq], sem="dmao", tag="prep.taplngf")
    ev_w = load_cast([(WO[c], dwo[TT * c:TT * (c + 1), :]) for c in range(8)],
                     D, ev_bq, "prep.wo")
    ev_w = load_cast([(U1w[c], dU1[TT * c:TT * (c + 1), :]) for c in range(8)],
                     RANK, ev_w, "prep.u1")
    pieces = []
    for r in range(2):
        for gp in range(4):
            pieces.append((V1w[r][:, 1024 * gp:1024 * (gp + 1)],
                           dV1[TT * r:TT * (r + 1),
                               1024 * gp:1024 * (gp + 1)]))
    ev_w = load_cast(pieces, 1024, ev_w, "prep.v1")
    ev_w = load_cast([(U2w[c], dU2[TT * c:TT * (c + 1), :])
                      for c in range(32)], RANK, ev_w, "prep.u2")
    ev_wlast = load_cast([(V2w[r], dV2[TT * r:TT * (r + 1), :])
                          for r in range(2)], D, ev_w, "prep.v2")

    P.barrier("prep")

    ev_x2_free = [None] * ST_TILES
    ev_state_delta = ev_stinit
    ev_zbd_prev = ev_zbdbf
    ev_x2_of, ev_kr_of = {}, {}
    st_ctx = {}

    for it in range(nt):
        ib = it % 2
        ic = it % 4
        t0 = it * TT
        tg = f"t{it}"
        if it % ST_TILES != 0:
            bw_pe = ("pe", P.cnt["pe"])
            bw_dve = ("dve", P.cnt["dve"])
            bw_act = ("act", P.cnt["act"])
            bw_pool = ("pool", P.cnt["pool"])
        else:
            bw_pe = bw_dve = bw_act = bw_pool = None
        if it < 2:
            ev_xd = ev_xd_pre[it]
        else:
            ev_xd = dma(xin[ib], dx[t0:t0 + TT, :], sem="dmax",
                        waits=list(prev_xd) + [ev_x2_of[it - 2]],
                        tag=tg + ".xd")
            prev_xd = [ev_xd]
        if it < 4:
            ev_cd = ev_cd_pre[it]
        else:
            ev_cd = dma(csb[ic], dcs[t0:t0 + TT, :], sem="dmac",
                        waits=list(prev_cd) + [ev_kr_of[it - 4]],
                        tag=tg + ".cd")
            prev_cd = [ev_cd]

        ev_rstd0 = ln_site(xin[ib], f"xin{ib}", 0,
                           [ev_xd, bw_dve, bw_pool], True, tg + ".ln0")
        ev_xn2 = vop(lambda eng, ib=ib: eng.tensor_scalar(
            out=xn2, in0=xin[ib], scalar1=negmu[:, 0:1], scalar2=rstd[:, 0:1],
            op0=ALU.add, op1=ALU.mult),
            waits=[ev_rstd0, bw_pe], reads=(f"xin{ib}", "negmu", "rstd"),
            writes=("xn2",), tag=tg + ".xn2")

        ev_xn2f = transpose_set(xn2, xn2f, PS_A, 8,
                                waits=[ev_xn2, bw_act, bw_dve],
                                tag=tg + ".xn2T")

        qkv_ps = {0: PS_A, 1: PS_B, 2: PS_C}
        ev_qkv = {}
        for blk in range(3):
            ps = qkv_ps[blk]
            GEN[id(ps)].new_gen()
            fns = []
            for gidx in range(2):
                c0 = 512 * gidx
                def fb(eng, blk=blk, c0=c0, ps=ps):
                    return eng.matmul(
                        ps[:, c0:c0 + 512], ones_row,
                        bq_row[0:1, 1024 * blk + c0:1024 * blk + c0 + 512],
                        start=True, stop=False)
                fns.append(fb)
                for cch in range(8):
                    def fm_(eng, blk=blk, cch=cch, c0=c0, ps=ps):
                        return eng.matmul(
                            ps[:, c0:c0 + 512],
                            xn2f[:, TT * cch:TT * (cch + 1)],
                            WQ[cch][:, 1024 * blk + c0:1024 * blk + c0 + 512],
                            start=False, stop=(cch == 7))
                    fns.append(fm_)
            ev_qkv[blk] = pe(fns, waits=[ev_xn2f], tag=tg + f".qkv{blk}")
        ev_qe = aop(lambda eng: eng.copy(out=q_bf, in_=PS_A),
                    waits=[ev_qkv[0]], writes=("q_bf",), tag=tg + ".qe")
        ev_ke = aop(lambda eng: eng.copy(out=k_bf, in_=PS_B),
                    waits=[ev_qkv[1]], writes=("k_bf",), tag=tg + ".ke")
        ev_ve = aop(lambda eng: eng.copy(out=v_bf, in_=PS_C),
                    waits=[ev_qkv[2]], writes=("v_bf",), tag=tg + ".ve")

        cc_b = csb[ic][:, 0:32].unsqueeze(1).broadcast_to([TT, 16, 32])
        ss_b = csb[ic][:, 32:64].unsqueeze(1).broadcast_to([TT, 16, 32])
        def rope(src, dst, ev_src, ev_scr_free, tagr):
            v3 = src.rearrange("p (h a j) -> p (h a) j", a=2, j=32)
            d3 = dst.rearrange("p (h a j) -> p (h a) j", a=2, j=32)
            se, so = v3[:, 0::2, :], v3[:, 1::2, :]
            de, do = d3[:, 0::2, :], d3[:, 1::2, :]
            tA = scr_bf[:, 0:512].rearrange("p (h j) -> p h j", j=32)
            tB = scr_bf[:, 512:1024].rearrange("p (h j) -> p h j", j=32)
            dnop = lambda t: vop(lambda eng: eng.engine_nop(), tag=t)
            e1 = vop(lambda eng: eng.tensor_tensor(out=tA, in0=se, in1=cc_b,
                                                   op=ALU.mult),
                     waits=[ev_src, ev_cd, ev_scr_free], reads=(src.name,),
                     writes=("scr_bf",), tag=tagr + ".1")
            e2 = vop(lambda eng: eng.tensor_tensor(out=tB, in0=so, in1=ss_b,
                                                   op=ALU.mult),
                     waits=[ev_src, ev_cd, ev_scr_free], reads=(src.name,),
                     writes=("scr_bf",), tag=tagr + ".2")
            dnop(tagr + ".n2")
            e5 = vop(lambda eng: eng.tensor_tensor(out=de, in0=tA, in1=tB,
                                                   op=ALU.subtract),
                     waits=[e2], reads=("scr_bf",), writes=(dst.name,),
                     tag=tagr + ".5")
            dnop(tagr + ".n5")
            e3 = vop(lambda eng: eng.tensor_tensor(out=tA, in0=so, in1=cc_b,
                                                   op=ALU.mult),
                     reads=(src.name,), writes=("scr_bf",), tag=tagr + ".3")
            e4 = vop(lambda eng: eng.tensor_tensor(out=tB, in0=se, in1=ss_b,
                                                   op=ALU.mult),
                     waits=[e5], reads=(src.name,), writes=("scr_bf",),
                     tag=tagr + ".4")
            dnop(tagr + ".n4")
            e6 = vop(lambda eng: eng.tensor_tensor(out=do, in0=tA, in1=tB,
                                                   op=ALU.add),
                     waits=[e4], reads=("scr_bf",), writes=(dst.name,),
                     tag=tagr + ".6")
            dnop(tagr + ".n6")
            return e6
        ev_qr = rope(q_bf, qr, ev_qe, None, tg + ".ropeq")
        ev_tapqr = None
        if "bqr" in dtap and it == 0:
            def ftq(eng):
                return eng.dma_start(out=dtap["bqr"][0:TT, :], in_=qr)
            ev_tapqr = P.step("sync", ftq, waits=[ev_qr], inc="dmao",
                              inc_amt=16, tag=tg + ".bqr")
        ev_kr = rope(k_bf, kr, ev_ke, ev_qr, tg + ".ropek")
        ev_kr_of[it] = ev_kr
        if "canq" in dtap and it == 0:
            ecq = vop(lambda eng: eng.tensor_copy(out=canq, in_=qr),
                      waits=[ev_qr], reads=("qr",), writes=("canq",),
                      tag=tg + ".canq")
            ecs2 = aop(lambda eng: eng.copy(out=cancs, in_=csb[0]),
                       waits=[ev_cd], writes=("cancs",), tag=tg + ".cancs")
            ecqb = gop(lambda eng: eng.tensor_copy(out=canqb, in_=q_bf),
                       waits=[ev_qe], reads=("q_bf",), writes=("canqb",),
                       tag=tg + ".canqb")
            tap("canq", ecq, 0)
            tap("cancs", ecs2, 0)
            tap("canqb", ecqb, 0)

        escr = scr_bf[:, 0:1024]
        def elu(src, dst, ev_src, waits_m, tagr):
            e1 = aop(lambda eng: eng.activation(out=escr, in_=src,
                                                func=AF.Exp),
                     waits=[ev_src] + waits_m, reads=(src.name,),
                     writes=("scr_bf",), tag=tagr + ".e")
            e2 = vop(lambda eng: eng.tensor_scalar(out=escr, in0=escr,
                                                   scalar1=1.0, scalar2=None,
                                                   op0=ALU.min),
                     waits=[e1], reads=("scr_bf",), writes=("scr_bf",),
                     tag=tagr + ".m")
            e3 = vop(lambda eng: eng.scalar_tensor_tensor(
                out=dst, in0=src, scalar=0.0, in1=escr, op0=ALU.max,
                op1=ALU.add), waits=[e2], reads=(src.name, "scr_bf"),
                writes=(dst.name,), tag=tagr + ".s")
            return e3
        ev_SQ = elu(q_bf, SQm, ev_qr, [ev_kr], tg + ".eluq")
        ev_SK = elu(k_bf, SKm, ev_kr, [ev_SQ], tg + ".eluk")

        wq_settle, wk_settle = [], []
        if it < 2:
            wq_settle = [ev_kr]
            ev_kset = vop(lambda eng: eng.tensor_copy(out=kr[:, 0:64],
                                                      in_=kr[:, 0:64]),
                          waits=[ev_kr], reads=("kr",), writes=("kr",),
                          tag=tg + ".kset")
            wk_settle = [ev_kset]
        ev_qrf = transpose_set(qr, qrf, PS_B, 8,
                               waits=[ev_qr, ev_ke, ev_tapqr] + wq_settle,
                               tag=tg + ".qrT")
        ev_krz = transpose_zsplit(kr, krz, PS_C,
                                  waits=[ev_kr, ev_ve] + wk_settle,
                                  tag=tg + ".krT")
        ev_SQz = transpose_zsplit(SQm, SQz, PS_A, waits=[ev_SQ, ev_qe],
                                  tag=tg + ".sqT")
        ev_SKf = transpose_set(SKm, SKf, PS_B, 8, waits=[ev_SK, ev_qrf],
                               tag=tg + ".skT")

        ev_nms_r = [None, None]
        ev_nm_r = [None, None]
        for r, ps in ((0, PS_A), (1, PS_C)):
            g = GEN[id(ps)]
            g.new_gen()
            fns = []
            for hh in range(8):
                h = 8 * r + hh
                st = g.start(TT * hh, TT * (hh + 1))
                def f(eng, h=h, hh=hh, ps=ps, st=st):
                    return eng.matmul(ps[:, TT * hh:TT * (hh + 1)],
                                      qrf[:, TT * (h // 2):TT * (h // 2) + TT],
                                      krz[:, TT * h:TT * (h + 1)],
                                      start=st, stop=True)
                fns.append(f)
            w = [ev_qrf, ev_krz] + ([ev_SQz] if r == 0 else [])
            ev_mm = pe(fns, waits=w, tag=tg + f".sc{r}")
            ev_nm_r[r] = vop(lambda eng, ps=ps, r=r: eng.tensor_reduce(
                out=negmax[:, 8 * r:8 * r + 8],
                in_=ps.rearrange("p (h k) -> p h k", k=TT),
                axis=AX.X, op=ALU.max, negate=True),
                waits=[ev_mm], writes=("negmax",), tag=tg + f".nm{r}")
            ev_nms_r[r] = vop(lambda eng, r=r: eng.tensor_scalar(
                out=negmax[:, 8 * r:8 * r + 8],
                in0=negmax[:, 8 * r:8 * r + 8], scalar1=SCALE, scalar2=None,
                op0=ALU.mult), waits=[ev_nm_r[r]], reads=("negmax",),
                writes=("negmax",), tag=tg + f".nms{r}")
        ev_P = [None, None]
        ev_pmsk_r = [None, None]
        cb3 = causal.unsqueeze(1).broadcast_to([TT, 8, TT])
        for r, ps in ((0, PS_A), (1, PS_C)):
            evp = None
            for hh in range(8):
                h = 8 * r + hh
                def fe(eng, h=h, hh=hh, ps=ps):
                    return eng.activation(out=Pbuf[:, TT * h:TT * (h + 1)],
                                          in_=ps[:, TT * hh:TT * (hh + 1)],
                                          func=AF.Exp, bias=negmax[:, h:h + 1],
                                          scale=SCALE)
                if hh == 0:
                    ws = ([ev_nms_r[0], ev_nm_r[1]] if r == 0
                          else [ev_nms_r[1], ev_pmsk_r[0]])
                else:
                    ws = ()
                evp = aop(fe, waits=ws, reads=("negmax",), writes=("Pbuf",),
                          tag=tg + f".exp{h}")
            ev_P[r] = evp
            ev_pmsk_r[r] = vop(lambda eng, r=r: eng.tensor_tensor(
                out=Pm[:, 1024 * r:1024 * (r + 1)].rearrange(
                    "p (h k) -> p h k", k=TT),
                in0=Pbuf[:, 1024 * r:1024 * (r + 1)].rearrange(
                    "p (h k) -> p h k", k=TT), in1=cb3,
                op=ALU.mult), waits=[ev_P[r]], reads=("Pbuf",),
                writes=("Pm",), tag=tg + f".pmsk{r}")
        ev_tappb = None
        if "bpb" in dtap and it == 0:
            def ftp(eng):
                return eng.dma_start(out=dtap["bpb"][0:TT, :], in_=Pbuf)
            ev_tappb = P.step("sync", ftp, waits=[ev_P[0], ev_P[1]],
                              inc="dmao", inc_amt=16, tag=tg + ".bpb")
        ev_ttr = vop(lambda eng: eng.tensor_reduce(
            out=lsum, in_=Pm.rearrange("p (h k) -> p h k", k=TT), axis=AX.X,
            op=ALU.add), waits=[ev_pmsk_r[1]], reads=("Pm",),
            writes=("lsum",), tag=tg + ".lsum")
        ev_rl = vop(lambda eng: eng.reciprocal(out=rl, in_=lsum),
                    waits=[ev_ttr], reads=("lsum",), writes=("rl",),
                    tag=tg + ".rl")
        ev_rl2 = vop(lambda eng: eng.tensor_tensor(out=rl, in0=rl,
                                                   in1=ombeta_bc,
                                                   op=ALU.mult),
                     waits=[ev_rl], reads=("rl",), writes=("rl",),
                     tag=tg + ".rl2")

        ev_PT = [None, None]
        for r, ps in ((0, PS_A), (1, PS_C)):
            g = GEN[id(ps)]
            g.new_gen()
            fns = []
            for hh in range(8):
                h = 8 * r + hh
                st = g.start(TT * hh, TT * (hh + 1))
                def f(eng, h=h, hh=hh, ps=ps, st=st):
                    return eng.matmul(ps[:, TT * hh:TT * (hh + 1)],
                                      Pm[:, TT * h:TT * (h + 1)], ident,
                                      start=st, stop=True)
                fns.append(f)
            ev_mm = pe(fns, waits=[ev_ttr], tag=tg + f".pT{r}")
            def fe(eng, r=r, ps=ps):
                return eng.copy(out=PTb[:, 1024 * r:1024 * (r + 1)],
                                in_=ps[:, 0:1024])
            ev_PT[r] = aop(fe, waits=[ev_mm, ev_tappb], writes=("Pbuf",),
                           tag=tg + f".pTe{r}")

        GB.new_gen()
        fns = []
        for h in range(HEADS):
            st = GB.start(DH * h, DH * (h + 1))
            def f(eng, h=h, st=st):
                return eng.matmul(PS_B[:, DH * h:DH * (h + 1)],
                                  PTb[:, TT * h:TT * (h + 1)],
                                  v_bf[:, DH * h:DH * (h + 1)],
                                  start=st, stop=True)
            fns.append(f)
        ev_sdpa = pe(fns, waits=[ev_PT[0], ev_PT[1], ev_ve, ev_SKf],
                     tag=tg + ".sdpa")
        ev_omix1 = vop(lambda eng: eng.tensor_tensor(
            out=scr_f.rearrange("p (h d) -> p h d", d=DH),
            in0=PS_B.rearrange("p (h d) -> p h d", d=DH), in1=bcast16(rl),
            op=ALU.mult), waits=[ev_sdpa, ev_rl2], reads=("rl",),
            writes=("scr_f",), tag=tg + ".omix1")

        GA.new_gen()
        ev_zp = pe([
            lambda eng: eng.matmul(PS_A[:, 0:512], maskup, SKm[:, 0:512],
                                   start=True, stop=True),
            lambda eng: eng.matmul(PS_A[:, 512:1024], maskup, SKm[:, 512:1024],
                                   start=True, stop=True)],
            waits=[ev_SK, ev_PT[0]], tag=tg + ".zp")
        GC.new_gen()
        fns = []
        for c in range(8):
            def f(eng, c=c):
                return eng.matmul(PS_C[:, 992 + 2 * c:994 + 2 * c],
                                  SKf[:, TT * c:TT * (c + 1)],
                                  zbd_bf[:, 2 * c:2 * c + 2],
                                  start=True, stop=True)
            fns.append(f)
        for h in range(HEADS):
            def f(eng, h=h):
                return eng.matmul(PS_C[:, 976 + h:977 + h],
                                  SQz[:, TT * h:TT * (h + 1)],
                                  zbd_bf[:, h:h + 1],
                                  start=True, stop=True)
            fns.append(f)
        ev_z0 = pe(fns, waits=[ev_SKf, ev_SQz, ev_zbd_prev, ev_PT[1]],
                   tag=tg + ".z0")
        ev_dp = vop(lambda eng: eng.tensor_tensor(
            out=scr_bf[:, 0:1024], in0=SKm, in1=PS_A, op=ALU.mult),
            waits=[ev_zp, ev_kr, ev_qr], reads=("SKm",), writes=("scr_bf",),
            tag=tg + ".dp")
        ev_dr = vop(lambda eng: eng.tensor_reduce(
            out=den, in_=scr_bf[:, 0:1024].rearrange("p (h d) -> p h d",
                                                     d=DH),
            axis=AX.X, op=ALU.add), waits=[ev_dp], reads=("scr_bf",),
            writes=("den",), tag=tg + ".dr")
        ev_den = vop(lambda eng: eng.tensor_tensor(
            out=den, in0=den, in1=PS_C[:, 992:1008], op=ALU.add),
            waits=[ev_dr, ev_z0], reads=("den",), writes=("den",),
            tag=tg + ".den")
        ev_qp = vop(lambda eng: eng.tensor_tensor(
            out=scr_bf[:, 0:1024], in0=SQm, in1=PS_A, op=ALU.mult),
            waits=[ev_zp, ev_dr], reads=("SQm",), writes=("scr_bf",),
            tag=tg + ".qp")
        ev_qred = vop(lambda eng: eng.tensor_reduce(
            out=denq, in_=scr_bf[:, 0:1024].rearrange(
                "p (h d) -> p h d", d=DH), axis=AX.X, op=ALU.add),
            waits=[ev_qp], reads=("scr_bf",), writes=("denq",),
            tag=tg + ".qred")
        ev_denq = vop(lambda eng: eng.scalar_tensor_tensor(
            out=denq, in0=PS_C[:, 976:992], scalar=1e-6, in1=denq,
            op0=ALU.add, op1=ALU.add), waits=[ev_qred, ev_z0],
            reads=("denq",), writes=("denq",), tag=tg + ".denq")
        ev_rden = vop(lambda eng: eng.reciprocal(out=rden, in_=den),
                      waits=[ev_den], reads=("den",), writes=("rden",),
                      tag=tg + ".rden")
        ev_rdq0 = vop(lambda eng: eng.reciprocal(out=rdq, in_=denq),
                      waits=[ev_denq], reads=("denq",), writes=("rdq",),
                      tag=tg + ".rdq0")
        ev_rdq = vop(lambda eng: eng.tensor_tensor(out=rdq, in0=rdq,
                                                   in1=beta_bc, op=ALU.mult),
                     waits=[ev_rdq0], reads=("rdq",), writes=("rdq",),
                     tag=tg + ".rdq")
        ev_skt = vop(lambda eng: eng.tensor_tensor(
            out=SKt.rearrange("p (h d) -> p h d", d=DH),
            in0=SKm.rearrange("p (h d) -> p h d", d=DH),
            in1=bcast16(rden), op=ALU.mult),
            waits=[ev_rden], reads=("SKm", "rden"), writes=("SKt",),
            tag=tg + ".skt")
        ev_sktz = transpose_zsplit(SKt, SKtz, PS_A,
                                   waits=[ev_skt, ev_dp, ev_qp],
                                   tag=tg + ".sktT")

        mupb = maskup.unsqueeze(1).broadcast_to([TT, 8, TT])
        ev_KT = [None, None]
        for r, ps in ((0, PS_B), (1, PS_C)):
            g = GEN[id(ps)]
            g.new_gen()
            fns = []
            for hh in range(8):
                h = 8 * r + hh
                st = g.start(TT * hh, TT * (hh + 1))
                def f(eng, h=h, hh=hh, ps=ps, st=st):
                    return eng.matmul(ps[:, TT * hh:TT * (hh + 1)],
                                      SKf[:, TT * (h // 2):TT * (h // 2) + TT],
                                      SKtz[:, TT * h:TT * (h + 1)],
                                      start=st, stop=True)
                fns.append(f)
            w = [ev_sktz, ev_omix1 if r == 0 else ev_denq]
            if r == 1:
                w.append(ev_den)
            ev_mm = pe(fns, waits=w, tag=tg + f".kt{r}")
            def fe(eng, r=r, ps=ps):
                return eng.tensor_tensor(
                    out=KTm[:, 1024 * r:1024 * (r + 1)].rearrange(
                        "p (h k) -> p h k", k=TT),
                    in0=ps.rearrange("p (h k) -> p h k", k=TT),
                    in1=mupb, op=ALU.mult)
            ev_KT[r] = vop(fe, waits=[ev_mm], writes=("KTm",),
                           tag=tg + f".kte{r}")
        ev_Sq = [None, None]
        for r, ps in ((0, PS_B), (1, PS_C)):
            g = GEN[id(ps)]
            g.new_gen()
            fns = []
            for hh in range(8):
                h = 8 * r + hh
                st = g.start(TT * hh, TT * (hh + 1))
                def f(eng, h=h, hh=hh, ps=ps, st=st):
                    return eng.matmul(ps[:, TT * hh:TT * (hh + 1)],
                                      SKf[:, TT * (h // 2):TT * (h // 2) + TT],
                                      SQz[:, TT * h:TT * (h + 1)],
                                      start=st, stop=True)
                fns.append(f)
            ev_mm = pe(fns, waits=[ev_KT[r]], tag=tg + f".sqm{r}")
            def fe(eng, r=r, ps=ps):
                return eng.tensor_tensor(
                    out=SmU[:, 1024 * r:1024 * (r + 1)].rearrange(
                        "p (h k) -> p h k", k=TT),
                    in0=ps.rearrange("p (h k) -> p h k", k=TT),
                    in1=mupb, op=ALU.mult)
            ev_Sq[r] = vop(fe, waits=[ev_mm], writes=("SmU",),
                           tag=tg + f".sqe{r}")

        GA.new_gen()
        fns = []
        for h in range(HEADS):
            st = GA.start(DH * h, DH * (h + 1))
            mcol = TT * (h // 2) + DH * (h % 2)
            def f(eng, h=h, st=st, mcol=mcol):
                return eng.matmul(PS_A[:, DH * h:DH * (h + 1)],
                                  SKtz[:, TT * h:TT * (h + 1)],
                                  membf[:, mcol:mcol + DH],
                                  start=st, stop=True)
            fns.append(f)
        ev_pv = pe(fns, waits=[ev_sktz, ev_membf], tag=tg + ".pv")
        ev_b = vop(lambda eng: eng.scalar_tensor_tensor(
            out=bvec, in0=PS_A, scalar=-1.0, in1=v_bf, op0=ALU.mult,
            op1=ALU.add), waits=[ev_pv, ev_ve], writes=("bvec",),
            tag=tg + ".b")
        ev_bbf = gop(lambda eng: eng.tensor_copy(out=b_bf, in_=bvec),
                     waits=[ev_b], reads=("bvec",), writes=("b_bf",),
                     tag=tg + ".bbf")

        prev_reader = {id(PS_B): ev_Sq[0], id(PS_C): ev_Sq[1]}
        g_in, ev_in = b_bf, ev_bbf
        ev_u = None
        ev_h3_mm = None
        for i3, (ps, dst) in enumerate(((PS_B, g1), (PS_C, g2),
                                        (PS_B, uvec))):
            g = GEN[id(ps)]
            g.new_gen()
            fns = []
            for h in range(HEADS):
                st = g.start(DH * h, DH * (h + 1))
                def f(eng, h=h, ps=ps, g_in=g_in, st=st):
                    return eng.matmul(ps[:, DH * h:DH * (h + 1)],
                                      KTm[:, TT * h:TT * (h + 1)],
                                      g_in[:, DH * h:DH * (h + 1)],
                                      start=st, stop=True)
                fns.append(f)
            ev_mm = pe(fns, waits=[ev_in, prev_reader[id(ps)], ev_KT[1]],
                       tag=tg + f".h{i3}")
            ev_h3_mm = ev_mm
            def fe(eng, ps=ps, dst=dst):
                return eng.scalar_tensor_tensor(out=dst, in0=ps, scalar=-1.0,
                                                in1=bvec, op0=ALU.mult,
                                                op1=ALU.add)
            evg = vop(fe, waits=[ev_mm], reads=("bvec",), writes=(dst.name,),
                      tag=tg + f".g{i3}")
            prev_reader[id(ps)] = evg
            g_in, ev_in = dst, evg
        ev_u = ev_in

        GC.new_gen()
        fns = []
        for h in range(HEADS):
            GC.start(DH * h, DH * (h + 1))
            mcol = TT * (h // 2) + DH * (h % 2)
            def f(eng, h=h, mcol=mcol):
                return eng.matmul(PS_C[:, DH * h:DH * (h + 1)],
                                  SQz[:, TT * h:TT * (h + 1)],
                                  membf[:, mcol:mcol + DH],
                                  start=True, stop=False)
            fns.append(f)
            def f2(eng, h=h):
                return eng.matmul(PS_C[:, DH * h:DH * (h + 1)],
                                  SmU[:, TT * h:TT * (h + 1)],
                                  uvec[:, DH * h:DH * (h + 1)],
                                  start=False, stop=True)
            fns.append(f2)
        ev_num = pe(fns, waits=[ev_u, ev_membf, prev_reader[id(PS_C)]],
                    tag=tg + ".num")
        ev_omix2 = vop(lambda eng: eng.tensor_tensor(
            out=o_acc.rearrange("p (h d) -> p h d", d=DH),
            in0=PS_C.rearrange("p (h d) -> p h d", d=DH), in1=bcast16(rdq),
            op=ALU.mult), waits=[ev_num, ev_rdq], reads=("rdq",),
            writes=("o_acc",), tag=tg + ".omix2")
        ev_o = gop(lambda eng: eng.tensor_tensor(out=o_bf, in0=o_acc,
                                                 in1=scr_f, op=ALU.add),
                   waits=[ev_omix2, ev_omix1], reads=("o_acc", "scr_f"),
                   writes=("q_bf",), tag=tg + ".o")

        fns = []
        for h in range(HEADS):
            def f(eng, h=h):
                ro = 64 * (h % 2)
                c0 = TT * (h // 2) + DH * (h % 2)
                return eng.matmul(PS_ST[ro:ro + 64, c0:c0 + DH],
                                  SKm[:, DH * h:DH * (h + 1)],
                                  uvec[:, DH * h:DH * (h + 1)],
                                  start=False, stop=True)
            fns.append(f)
        for h in range(HEADS):
            def f(eng, h=h):
                ro = 64 * (h % 2)
                return eng.matmul(PS_B[ro:ro + 64,
                                       1016 + (h // 2):1017 + (h // 2)],
                                  SKm[:, DH * h:DH * (h + 1)], onesc,
                                  start=True, stop=True)
            fns.append(f)
        ev_dmem = pe(fns, waits=[ev_u, ev_state_delta, ev_membf,
                                 prev_reader[id(PS_B)]],
                     tag=tg + ".dmem")
        ev_state_delta = ev_dmem
        zview_e = zbd[0:64, :].rearrange("p (c a) -> p a c", a=2)[:, 0, :]
        zview_o = zbd[64:128, :].rearrange("p (c a) -> p a c", a=2)[:, 1, :]
        ev_z1 = vop(lambda eng: eng.tensor_tensor(
            out=zview_e, in0=zview_e, in1=PS_B[0:64, 1016:1024], op=ALU.add),
            waits=[ev_dmem], reads=("zbd",), writes=("zbd",), tag=tg + ".z1")
        ev_z2 = vop(lambda eng: eng.tensor_tensor(
            out=zview_o, in0=zview_o, in1=PS_B[64:128, 1016:1024],
            op=ALU.add), waits=[ev_dmem], reads=("zbd",), writes=("zbd",),
            tag=tg + ".z2")
        ev_zbd_prev = vop(lambda eng: eng.tensor_copy(out=zbd_bf, in_=zbd),
                          waits=[ev_z1, ev_z2], reads=("zbd",),
                          writes=("zbd_bf",), tag=tg + ".zbf")
        ev_membf = aop(lambda eng: eng.copy(out=membf, in_=PS_ST),
                       waits=[ev_dmem], writes=("membf",), tag=tg + ".stev")

        ev_ofm = transpose_set(o_bf, o_fm, PS_B, 8,
                               waits=[ev_o, ev_z1, ev_z2], tag=tg + ".oT")
        GC.new_gen()
        fns = []
        for gidx in range(2):
            c0 = 512 * gidx
            for cch in range(8):
                def f(eng, cch=cch, c0=c0):
                    return eng.matmul(PS_C[:, c0:c0 + 512],
                                      o_fm[:, TT * cch:TT * (cch + 1)],
                                      WO[cch][:, c0:c0 + 512],
                                      start=(cch == 0), stop=(cch == 7))
                fns.append(f)
        ev_womm = pe(fns, waits=[ev_ofm, ev_omix2], tag=tg + ".wo")
        ev_ao = vop(lambda eng: eng.scalar_tensor_tensor(
            out=ao, in0=PS_C, scalar=0.0,
            in1=zerocol.broadcast_to([TT, D]), op0=ALU.add, op1=ALU.add,
            accum_out=sumv), waits=[ev_womm, ev_bbf],
            writes=("bvec", "sumv"), tag=tg + ".ao")
        ev_aosq = aop(lambda eng: eng.activation(out=sq_scr, in_=ao,
                                                 func=AF.Square,
                                                 accum_out=sumsq),
                      waits=[ev_ao], reads=("bvec",),
                      writes=("sq_scr", "sumsq"), tag=tg + ".aosq")
        P.step("gpsimd", lambda eng: eng.engine_nop(),
               waits=[ev_ao, ev_aosq], tag=tg + ".lnw")
        ln_stats_tail(1, tg + ".ln1")
        ev_rstd1 = rsqrt_chain(stats[:, 1:2], rstd[:, 1:2], tag=tg + ".ln1r")
        ev_t1 = vop(lambda eng: eng.tensor_scalar(
            out=scr_f, in0=ao, scalar1=negmu[:, 1:2], scalar2=rstd[:, 1:2],
            op0=ALU.add, op1=ALU.mult),
            waits=[ev_rstd1, ev_o], reads=("bvec", "negmu", "rstd"),
            writes=("scr_f",), tag=tg + ".lnap1")
        jslot = it % ST_TILES
        ev_x2 = vop(lambda eng, ib=ib, jslot=jslot: eng.tensor_tensor(
            out=x2[jslot], in0=scr_f, in1=xin[ib], op=ALU.add),
            waits=[ev_t1, ev_x2_free[jslot]], reads=("scr_f", f"xin{ib}"),
            writes=(f"x2_{jslot}",), tag=tg + ".x2")
        ev_x2_of[it] = ev_x2

        ev_rstd2 = ln_site(x2[jslot], f"x2_{jslot}", 2,
                           [ev_x2, ev_aosq], False, tg + ".ln2")
        ev_xn3 = vop(lambda eng, jslot=jslot: eng.tensor_scalar(
            out=xn3, in0=x2[jslot], scalar1=negmu[:, 2:3],
            scalar2=rstd[:, 2:3], op0=ALU.add, op1=ALU.mult),
            waits=[ev_rstd2], reads=(f"x2_{jslot}", "negmu", "rstd"),
            writes=("xn3",), tag=tg + ".xn3")
        ev_xn3f = transpose_set(xn3, xn3f, PS_B, 8, waits=[ev_xn3, ev_ofm],
                                tag=tg + ".xn3T")
        GC.new_gen()
        fns = []
        for cch in range(8):
            def f(eng, cch=cch):
                return eng.matmul(PS_C[:, 0:256],
                                  xn3f[:, TT * cch:TT * (cch + 1)],
                                  U1w[cch], start=(cch == 0), stop=(cch == 7))
            fns.append(f)
        ev_u1mm = pe(fns, waits=[ev_xn3f, ev_ao], tag=tg + ".u1")
        ev_p2 = aop(lambda eng: eng.copy(out=P2, in_=PS_C[:, 0:256]),
                    waits=[ev_u1mm], writes=("P2",), tag=tg + ".p2")
        fns = []
        for rc in range(2):
            def f(eng, rc=rc):
                return eng.matmul(PS_C[:, 256 + TT * rc:256 + TT * (rc + 1)],
                                  P2[:, TT * rc:TT * (rc + 1)], ident,
                                  start=True, stop=True)
            fns.append(f)
        ev_p2t = pe(fns, waits=[ev_p2], tag=tg + ".p2T")
        p2dst = P2fw.rearrange("p (r t k) -> p r t k", r=2, t=ST_TILES)
        def fe(eng, jslot=jslot):
            return eng.copy(out=p2dst[:, :, jslot, :],
                            in_=PS_C[:, 256:512].rearrange(
                                "p (r k) -> p r k", r=2))
        ev_p2fw = aop(fe, waits=[ev_p2t], writes=("P2fw",), tag=tg + ".p2fw")

        tap("xn2", ev_xn2, t0)
        tap("q", ev_qe, t0)
        tap("k", ev_ke, t0)
        tap("v", ev_ve, t0)
        tap("qqr", ev_qr, t0)
        tap("kkr", ev_kr, t0)
        tap("sq", ev_SQ, t0)
        tap("sk", ev_SK, t0)
        tap("u", ev_u, t0)
        tap("o", ev_o, t0)
        tap("aao", ev_ao, t0)
        tap("den", ev_den, t0)
        tap("denq", ev_denq, t0)
        tap("mem", ev_membf, t0)
        tap("lsum", ev_ttr, t0)
        tap("b", ev_b, t0)
        tap("ktm", ev_KT[1], t0)
        if jslot == 0:
            tap("x2_0", ev_x2, t0)
        tap("pm", ev_ttr, t0)
        tap("negmax", ev_ttr, t0)

        tap("xn3t", ev_xn3, t0)
        st_ctx[jslot] = dict(ev_p2fw=ev_p2fw, it=it)
        if jslot != ST_TILES - 1 and it != nt - 1:
            continue

        nslots = jslot + 1
        ev_allp2 = [st_ctx[j]["ev_p2fw"] for j in range(nslots)]
        for _ev in ev_allp2:
            tap("p2f", _ev, 0)
        GA.new_gen()
        GB.new_gen()
        ev_gel = [None, None]
        ev_q2mm = None
        for fc in range(32):
            pp = fc % 2
            hb = Hbuf[pp]
            c0 = 512 * pp
            fns = []
            for r2 in range(2):
                def f(eng, r2=r2, fc=fc, c0=c0):
                    return eng.matmul(PS_A[:, c0:c0 + TT * nslots],
                                      V1w[r2][:, TT * fc:TT * (fc + 1)],
                                      P2fw[:, 512 * r2:512 * r2 + TT * nslots],
                                      start=(r2 == 0), stop=(r2 == 1))
                fns.append(f)
            if fc < 2:
                w = list(ev_allp2) + [ev_b, ev_pv]
            else:
                w = [ev_gel[pp]]
            ev_v1mm = pe(fns, waits=w, tag=tg + f".v1_{fc}")
            def fg(eng, fc=fc, hb=hb, c0=c0):
                return eng.activation(out=hb[:, 0:TT * nslots],
                                      in_=PS_A[:, c0:c0 + TT * nslots],
                                      func=AF.Gelu_apprx_tanh,
                                      bias=b1fm[:, fc:fc + 1])
            ev_g = aop(fg, waits=[ev_v1mm], writes=(hb.name,),
                       tag=tg + f".gel{fc}")
            fns = []
            if fc == 0:
                for j in range(nslots):
                    GB.start(RANK * j, RANK * (j + 1))
                    def fz(eng, j=j):
                        return eng.matmul(PS_B[:, RANK * j:RANK * (j + 1)],
                                          ones_row,
                                          zero_row[0:1, 0:RANK],
                                          start=True, stop=False)
                    fns.append(fz)
            for j in range(nslots):
                def f(eng, j=j, fc=fc, hb=hb):
                    return eng.matmul(PS_B[:, RANK * j:RANK * (j + 1)],
                                      hb[:, TT * j:TT * (j + 1)], U2w[fc],
                                      start=False, stop=(fc == 31))
                fns.append(f)
            w = [ev_g]
            if fc == 0:
                w += [ev_dmem, ev_z1, ev_z2, ev_xn3f]
            ev_q2mm = pe(fns, waits=w, tag=tg + f".u2_{fc}")
            ev_gel[pp] = ev_q2mm

        ev_prev_tail = None
        for j in reversed(range(nslots)):
            jt = st_ctx[j]["it"]
            jt0 = jt * TT
            ttg = f"t{jt}.ffn"
            ev_q2 = aop(lambda eng, j=j: eng.copy(
                out=Q2, in_=PS_B[:, RANK * j:RANK * (j + 1)]),
                waits=[ev_q2mm, ev_prev_tail], writes=("Q2",),
                tag=ttg + ".q2")
            GC.new_gen()
            fns = []
            for rc in range(2):
                st = GC.start(TT * rc, TT * (rc + 1))
                def f(eng, rc=rc, st=st):
                    return eng.matmul(PS_C[:, TT * rc:TT * (rc + 1)],
                                      Q2[:, TT * rc:TT * (rc + 1)], ident,
                                      start=st, stop=True)
                fns.append(f)
            ev_q2t = pe(fns, waits=[ev_q2, ev_p2fw], tag=ttg + ".q2T")
            ev_q2f = aop(lambda eng: eng.copy(out=Q2f, in_=PS_C[:, 0:256]),
                         waits=[ev_q2t], writes=("Q2f",), tag=ttg + ".q2f")
            GC.new_gen()
            fns = []
            for gidx in range(2):
                c0 = 512 * gidx
                for rc in range(2):
                    def f(eng, rc=rc, c0=c0):
                        return eng.matmul(PS_C[:, c0:c0 + 512],
                                          Q2f[:, TT * rc:TT * (rc + 1)],
                                          V2w[rc][:, c0:c0 + 512],
                                          start=(rc == 0), stop=(rc == 1))
                    fns.append(f)
            ev_v2 = pe(fns, waits=[ev_q2f], tag=ttg + ".v2")
            ev_f = vop(lambda eng: eng.scalar_tensor_tensor(
                out=fout, in0=PS_C, scalar=0.0, in1=b2bc, op0=ALU.add,
                op1=ALU.add, accum_out=sumv),
                waits=[ev_v2], writes=("o_acc", "sumv"), tag=ttg + ".f")
            tap("fs0", ev_f, jt0)
            tap("q2s0", ev_q2, jt0)
            ev_fsq = aop(lambda eng: eng.activation(out=sq_scr, in_=fout,
                                                    func=AF.Square,
                                                    accum_out=sumsq),
                         waits=[ev_f], reads=("o_acc",),
                         writes=("sq_scr", "sumsq"), tag=ttg + ".fsq")
            P.step("gpsimd", lambda eng: eng.engine_nop(),
                   waits=[ev_f, ev_fsq], tag=ttg + ".w")
            ln_stats_tail(3, ttg + ".ln3")
            ev_rstd3 = rsqrt_chain(stats[:, 1:2], rstd[:, 3:4],
                                   tag=ttg + ".ln3r")
            ev_t3 = vop(lambda eng: eng.tensor_scalar(
                out=scr_f, in0=fout, scalar1=negmu[:, 3:4],
                scalar2=rstd[:, 3:4], op0=ALU.add, op1=ALU.mult),
                waits=[ev_rstd3], reads=("o_acc", "negmu", "rstd"),
                writes=("scr_f",), tag=ttg + ".lap")
            ev_y = vop(lambda eng, j=j: eng.tensor_tensor(
                out=x2[j], in0=scr_f, in1=x2[j], op=ALU.add),
                waits=[ev_t3], reads=("scr_f",), writes=(f"x2_{j}",),
                tag=ttg + ".y")
            ev_od = dma(dout[jt0:jt0 + TT, :], x2[j], waits=[ev_y],
                        sem="dmao", tag=ttg + ".od")
            ev_x2_free[j] = ev_od
            ev_prev_tail = ev_t3
        st_ctx = {}
        P.barrier(tg + ".st")

    if "x2late" in taps:
        late = dict(x2late=x2[0], aol=bvec, scrfl=scr_f, negmul=negmu,
                    rstdl=rstd, sumvl=sumv, sumsql=sumsq, obl=q_bf,
                    oaccl=o_acc, lsuml=lsum, rll=rl, xinl=xin[0],
                    pml=Pm, pbl=Pbuf, negmaxl=negmax, qrl=qr, krzl=krz,
                    **({"canql": canq, "canqbl": canqb, "cancsl": cancs}
                       if canq is not None else {}))
        w = [(s, P.cnt[s]) for s in ("pe", "dve", "act", "pool")]
        for nm, buf in late.items():
            dl = nc.dram_tensor(f"tap_{nm}", list(buf.shape), buf.dtype,
                                kind="ExternalOutput").ap()
            def fxl(eng, dl=dl, buf=buf):
                return eng.dma_start(out=dl, in_=buf)
            P.step("sync", fxl, inc="dmao", inc_amt=16, tag=f"late.{nm}",
                   force=True, waits=w)
            w = []
    if P.frozen:
        def ffz(eng):
            return eng.dma_start(out=dout[0:TT, :], in_=xin[0])
        P.step("sync", ffz, inc="dmao", inc_amt=16, tag="final.frz",
               force=True)
    P.step("sync", lambda eng: eng.nop(),
           waits=[("dmao", P.cnt["dmao"])], tag="final", force=True)

    with (nc.semaphore("s_dmaw") as s_dmaw, nc.semaphore("s_dmax") as s_dmax,
          nc.semaphore("s_dmac") as s_dmac,
          nc.semaphore("s_dmao") as s_dmao, nc.semaphore("s_pe") as s_pe,
          nc.semaphore("s_dve") as s_dve, nc.semaphore("s_act") as s_act,
          nc.semaphore("s_pool") as s_pool,
          nc.semaphore("s_esync") as s_esync, nc.Block() as block):
        P.sems = dict(dmaw=s_dmaw, dmax=s_dmax, dmac=s_dmac, dmao=s_dmao,
                      pe=s_pe, dve=s_dve, act=s_act, pool=s_pool,
                      esync=s_esync)

        @block.sync
        def _(eng):
            P.run_engine(eng, "sync")

        @block.tensor
        def _(eng):
            P.run_engine(eng, "tensor")

        @block.vector
        def _(eng):
            P.run_engine(eng, "vector")

        @block.scalar
        def _(eng):
            P.run_engine(eng, "scalar")

        @block.gpsimd
        def _(eng):
            P.run_engine(eng, "gpsimd")

    return nc


def kernel(L0_attn_x, L0_attn_w_qkv, L0_attn_w_out, L0_attn_ln_g, L0_attn_ln_b,
           L0_attn_mem_beta, L0_ffn_U1, L0_ffn_V1, L0_ffn_b1, L0_ffn_U2,
           L0_ffn_V2, L0_ffn_b2):
    x = np.ascontiguousarray(np.asarray(L0_attn_x, dtype=np.float32))
    B = x.shape[0]
    nc = build(NT_FULL)
    common = dict(
        w_qkv=np.ascontiguousarray(L0_attn_w_qkv, dtype=np.float32),
        w_out=np.ascontiguousarray(L0_attn_w_out, dtype=np.float32),
        ln_g=np.asarray(L0_attn_ln_g, dtype=np.float32).reshape(1, D),
        ln_b=np.asarray(L0_attn_ln_b, dtype=np.float32).reshape(1, D),
        mem_beta=np.asarray(L0_attn_mem_beta,
                            dtype=np.float32).reshape(1, HEADS),
        U1=np.ascontiguousarray(L0_ffn_U1, dtype=np.float32),
        V1=np.ascontiguousarray(L0_ffn_V1, dtype=np.float32),
        b1=np.asarray(L0_ffn_b1, dtype=np.float32).reshape(1, DFF),
        U2=np.ascontiguousarray(L0_ffn_U2, dtype=np.float32),
        V2=np.ascontiguousarray(L0_ffn_V2, dtype=np.float32),
        b2=np.asarray(L0_ffn_b2, dtype=np.float32).reshape(1, D),
    )
    in_maps = [dict(common, x=np.ascontiguousarray(x[b])) for b in range(B)]
    res = run_bass_kernel_spmd(nc, in_maps, core_ids=list(range(B)))
    return np.stack([res.results[b]["out"] for b in range(B)], axis=0)


# version 06 (final)
# speedup vs baseline: 1.0405x; 1.0058x over previous
import numpy as np
import ml_dtypes
import concourse.bass as bass
import concourse.mybir as mybir
from concourse.bass_utils import run_bass_kernel_spmd

F32 = mybir.dt.float32
BF16 = mybir.dt.bfloat16
I32 = mybir.dt.int32
AF = mybir.ActivationFunctionType
ALU = mybir.AluOpType
AX = mybir.AxisListType
bfnp = ml_dtypes.bfloat16

HEADS, DH, SEG, TT = 16, 64, 32, 128
SCALE = DH ** -0.5
D, DQKV, DFF, RANK = 1024, 3072, 4096, 256
N_TOK = 8192
NT_FULL = N_TOK // TT
ST_TILES = 4
LN_EPS = 1e-5
MAGIC = 0x5F3759DF
ENGS = ("sync", "tensor", "vector", "scalar", "gpsimd")
ESEM = dict(sync="esync", tensor="pe", vector="dve", scalar="act",
            gpsimd="pool")


def _np_consts(n):
    ident = np.eye(TT, dtype=np.float32).astype(bfnp)
    segid = np.arange(TT) // SEG
    maskup = (segid[:, None] < segid[None, :]).astype(np.float32).astype(bfnp)
    causal = ((segid[:, None] == segid[None, :]) &
              (np.arange(TT)[:, None] >= np.arange(TT)[None, :]))
    causal = causal.astype(np.float32).astype(bfnp)
    j = np.arange(0, DH, 2, dtype=np.float32)
    freqs = (10000.0 ** (-j / DH)).astype(np.float32)
    ang = np.arange(n, dtype=np.float32)[:, None] * freqs[None, :]
    cs = np.concatenate([np.cos(ang), np.sin(ang)], axis=1).astype(np.float32)
    ones_row = np.ones((1, TT), dtype=np.float32).astype(bfnp)
    onesc_bf = np.ones((TT, 1), dtype=np.float32).astype(bfnp)
    zero_row = np.zeros((1, 512), dtype=np.float32).astype(bfnp)
    one_one = np.ones((1, 1), dtype=np.float32).astype(bfnp)
    magic_t = np.full((TT, 1), MAGIC, dtype=np.int32)
    zbd0 = np.zeros((TT, 16), dtype=np.float32)
    for c in range(8):
        zbd0[0:64, 2 * c] = 1e-6
        zbd0[64:128, 2 * c + 1] = 1e-6
    return dict(ident=ident, maskup=maskup, causal=causal, cs=cs,
                ones_row=ones_row, onesc_bf=onesc_bf, zero_row=zero_row,
                one_one=one_one, magic=magic_t, zbd0=zbd0)


class Plan:
    def __init__(self, until_tag=None):
        self.until = until_tag
        self.frozen = False
        self.steps = []
        self.cnt = {k: 0 for k in
                    ("dmaw", "dmax", "dmac", "dmao", "pe", "dve", "act",
                     "pool", "esync")}
        self.sems = {}
        self.pending_barrier = {}

    def step(self, engine, fn, waits=(), inc=None, inc_amt=1, reads=(),
             writes=(), tag="", force=False):
        if self.until is not None and self.until in tag:
            self.frozen = True
        if self.frozen and not force:
            if inc is not None:
                return (inc, self.cnt[inc])
            return None
        ev = None
        if inc is not None:
            self.cnt[inc] += inc_amt
            ev = (inc, self.cnt[inc])
        w = [x for x in waits if x is not None]
        if engine in self.pending_barrier:
            w = self.pending_barrier.pop(engine) + w
        self.steps.append(dict(engine=engine, fn=fn, waits=w, inc=inc,
                               inc_amt=inc_amt, reads=set(reads),
                               writes=set(writes), tag=tag))
        return ev

    def barrier(self, tag=""):
        prev = None
        for e in ENGS:
            sem = ESEM[e]
            def f(eng, e=e):
                return (eng.engine_nop() if e in ("vector", "gpsimd")
                        else eng.nop())
            prev = self.step(e, f, waits=[prev] if prev else (),
                             inc=sem, tag=tag + f".bar.{e}")
        for e in ENGS[:-1]:
            self.pending_barrier[e] = [prev]

    def run_engine(self, eng, engine_name):
        last_writes = set()
        for st in self.steps:
            if st["engine"] != engine_name:
                continue
            waited = False
            for semname, val in st["waits"]:
                eng.wait_ge(self.sems[semname], val)
                waited = True
            if waited:
                last_writes = set()
            if st["reads"] & last_writes or st["writes"] & last_writes:
                if engine_name in ("vector", "gpsimd"):
                    eng.engine_nop()
                elif engine_name == "scalar":
                    eng.drain()
            inst = st["fn"](eng)
            if st["inc"] is not None:
                assert inst is not None, st["tag"]
                inst.then_inc(self.sems[st["inc"]], st["inc_amt"])
            last_writes = st["writes"]


class PsumGen:

    def __init__(self):
        self.regions = set()

    def new_gen(self):
        self.regions = set()

    def start(self, col_lo, col_hi):
        key = (col_lo, col_hi)
        st = key not in self.regions
        self.regions.add(key)
        return st


def build(nt=NT_FULL, taps=(), until_tag=None):
    nc = bass.Bass("TRN2", target_bir_lowering=False, debug=False)
    n = nt * TT
    C = _np_consts(n)

    dx = nc.dram_tensor("x", [n, D], F32, kind="ExternalInput").ap()
    dwq = nc.dram_tensor("w_qkv", [D, DQKV], F32, kind="ExternalInput").ap()
    dwo = nc.dram_tensor("w_out", [D, D], F32, kind="ExternalInput").ap()
    dlng = nc.dram_tensor("ln_g", [1, D], F32, kind="ExternalInput").ap()
    dlnb = nc.dram_tensor("ln_b", [1, D], F32, kind="ExternalInput").ap()
    dbeta = nc.dram_tensor("mem_beta", [1, HEADS], F32, kind="ExternalInput").ap()
    dU1 = nc.dram_tensor("U1", [D, RANK], F32, kind="ExternalInput").ap()
    dV1 = nc.dram_tensor("V1", [RANK, DFF], F32, kind="ExternalInput").ap()
    db1 = nc.dram_tensor("b1", [1, DFF], F32, kind="ExternalInput").ap()
    dU2 = nc.dram_tensor("U2", [DFF, RANK], F32, kind="ExternalInput").ap()
    dV2 = nc.dram_tensor("V2", [RANK, D], F32, kind="ExternalInput").ap()
    db2 = nc.dram_tensor("b2", [1, D], F32, kind="ExternalInput").ap()
    dout = nc.dram_tensor("out", [n, D], F32, kind="ExternalOutput").ap()

    dident = nc.inline_tensor(C["ident"], "c_ident").ap()
    dmaskup = nc.inline_tensor(C["maskup"], "c_maskup").ap()
    dcausal = nc.inline_tensor(C["causal"], "c_causal").ap()
    dcs = nc.inline_tensor(C["cs"], "c_cs").ap()
    dones_row = nc.inline_tensor(C["ones_row"], "c_onesrow").ap()
    donesc = nc.inline_tensor(C["onesc_bf"], "c_onescol").ap()
    dzero_row = nc.inline_tensor(C["zero_row"], "c_zerorow").ap()
    done_one = nc.inline_tensor(C["one_one"], "c_oneone").ap()
    dmagic = nc.inline_tensor(C["magic"], "c_magic").ap()
    dzbd0 = nc.inline_tensor(C["zbd0"], "c_zbd0").ap()

    S = lambda name, shape, dt=BF16: nc.alloc_sbuf_tensor(name, shape, dt).ap()
    WQ = [S(f"WQ{c}", [TT, DQKV]) for c in range(8)]
    WO = [S(f"WO{c}", [TT, D]) for c in range(8)]
    U1w = [S(f"U1_{c}", [TT, RANK]) for c in range(8)]
    V1w = [S(f"V1_{r}", [TT, DFF]) for r in range(2)]
    U2w = [S(f"U2_{c}", [TT, RANK]) for c in range(32)]
    V2w = [S(f"V2_{r}", [TT, D]) for r in range(2)]
    bq_row = S("bq_row", [1, DQKV])
    b2bc = S("b2bc", [TT, D], F32)
    b1fm = S("b1fm", [TT, 32], F32)
    beta_bc = S("beta_bc", [TT, HEADS], F32)
    ombeta_bc = S("ombeta_bc", [TT, HEADS], F32)
    lngf = S("lngf", [TT, 8], F32)
    lnbf = S("lnbf", [TT, 8])
    ident = S("ident", [TT, TT])
    maskup = S("maskup", [TT, TT])
    causal = S("causal", [TT, TT])
    ones_row = S("ones_row", [1, TT])
    onesc = S("onesc", [TT, 1])
    zero_row = S("zero_row", [1, 512])
    one_one = S("one_one", [1, 1])
    magic = S("magic", [TT, 1], I32)
    zerocol = S("zerocol", [TT, 1], F32)

    zbd = S("zbd", [TT, HEADS], F32)
    zbd_bf = S("zbd_bf", [TT, HEADS])
    membf = S("membf", [TT, 1024])

    xin = [S(f"xin{i}", [TT, D], F32) for i in range(2)]
    csb = [S(f"cs{i}", [TT, 64], F32) for i in range(4)]
    xn2 = S("xn2", [TT, D])
    q_bf = S("q_bf", [TT, D])
    k_bf = S("k_bf", [TT, D])
    v_bf = S("v_bf", [TT, D])
    qr = S("qr", [TT, D])
    kr = S("kr", [TT, D])
    SQm = S("SQm", [TT, D])
    SKm = S("SKm", [TT, D])
    SKf = S("SKf", [TT, D])
    krz = S("krz", [TT, 2048])
    SQz = S("SQz", [TT, 2048])
    SKtz = S("SKtz", [TT, 2048])
    Pbuf = S("Pbuf", [TT, 2048])
    Pm = S("Pm", [TT, 2048])
    bvec = S("bvec", [TT, D], F32)
    g1 = S("g1", [TT, D])
    o_acc = S("o_acc", [TT, D], F32)
    x2 = [S(f"x2_{i}", [TT, D], F32) for i in range(ST_TILES)]
    P2 = S("P2", [TT, RANK])
    P2fw = S("P2fw", [TT, 2 * ST_TILES * TT])
    Hbuf = [S(f"Hbuf{i}", [TT, ST_TILES * TT]) for i in range(2)]
    Q2 = S("Q2", [TT, RANK])
    scr_bf = S("scr_bf", [TT, 1024])
    scr_f = S("scr_f", [TT, D], F32)
    sumv = S("sumv", [TT, 1], F32)
    sumsq = S("sumsq", [TT, 1], F32)
    stats = S("stats", [TT, 16], F32)
    negmu = S("negmu", [TT, 4], F32)
    rstd = S("rstd", [TT, 4], F32)
    den = S("den", [TT, HEADS], F32)
    denq = S("denq", [TT, HEADS], F32)
    rden = S("rden", [TT, HEADS], F32)
    rdq = S("rdq", [TT, HEADS], F32)
    negmax = S("negmax", [TT, HEADS], F32)
    lsum = S("lsum", [TT, HEADS], F32)
    rl = S("rl", [TT, HEADS], F32)

    canq = S("canq", [TT, D]) if "canq" in taps else None
    canqb = S("canqb", [TT, D]) if "canq" in taps else None
    cancs = S("cancs", [TT, 64], F32) if "canq" in taps else None
    xn2f, qrf = xn2, qr
    SKt = qr
    xn3 = SQm
    xn3f = xn3
    b_bf = v_bf
    g2 = k_bf
    o_bf, o_fm, ao, fout = q_bf, k_bf, bvec, o_acc
    PTb = Pbuf
    KTm = Pbuf
    SmU = S("SmU_dbg", [TT, 2048]) if "pm" in taps else Pm
    if "xn2" in taps:
        xn2f = S("xn2f_dbg", [TT, D])
    uvec = g1
    Q2f = S("Q2f_dbg", [TT, 2 * TT]) if "q2s0" in taps else Q2
    sq_scr = scr_bf[:, 0:1024]
    rowf = scr_f[0:1, 0:1024]
    rowbf = scr_bf[0:1, 0:1024]

    tap_srcs = dict(xn2=xn2, q=q_bf, k=k_bf, v=v_bf, qqr=qr, kkr=kr, sq=SQm,
                    canq=canq, canqb=canqb, cancs=cancs, bqr=qr, bpb=Pbuf,
                    negmax=negmax, fs0=o_acc, q2s0=Q2, p2f=P2fw, xn3t=xn3,
                    hs0=Hbuf[1],
                    sk=SKm, u=uvec, o=q_bf, aao=bvec, den=den, denq=denq,
                    mem=membf, lsum=lsum, x2_0=x2[0], b=bvec, ktm=KTm,
                    pm=Pm)
    dtap = {}
    for tname in taps:
        if tname == "x2late":
            continue
        src = tap_srcs[tname]
        dtap[tname] = nc.dram_tensor(f"tap_{tname}",
                                     [n, src.shape[-1]], src.dtype,
                                     kind="ExternalOutput").ap()

    PS_ST = nc.alloc_psum_tensor("ps_state", [TT, 1024], F32).ap()
    PS_A = nc.alloc_psum_tensor("ps_a", [TT, 1024], F32).ap()
    PS_B = nc.alloc_psum_tensor("ps_b", [TT, 1024], F32).ap()
    PS_C = nc.alloc_psum_tensor("ps_c", [TT, 1024], F32).ap()
    GA, GB, GC = PsumGen(), PsumGen(), PsumGen()
    GEN = {id(PS_A): GA, id(PS_B): GB, id(PS_C): GC}

    P = Plan(until_tag=until_tag)

    def pe(fns, waits=(), tag=""):
        def f(eng):
            last = None
            for fn in fns:
                last = fn(eng)
            return last
        return P.step("tensor", f, waits=waits, inc="pe", tag=tag)

    def vop(fn, waits=(), reads=(), writes=(), tag=""):
        return P.step("vector", fn, waits=waits, inc="dve", reads=reads,
                      writes=writes, tag=tag)

    def aop(fn, waits=(), reads=(), writes=(), tag=""):
        return P.step("scalar", fn, waits=waits, inc="act", reads=reads,
                      writes=writes, tag=tag)

    def gop(fn, waits=(), reads=(), writes=(), tag=""):
        return P.step("gpsimd", fn, waits=waits, inc="pool", reads=reads,
                      writes=writes, tag=tag)

    def dma(out, in_, waits=(), sem="dmaw", tag=""):
        def f(eng):
            return eng.dma_start(out=out, in_=in_)
        return P.step("sync", f, waits=waits, inc=sem, inc_amt=16, tag=tag)

    def hsl(t, h):
        return t[64 * (h % 2):64 * (h % 2) + 64,
                 TT * (h // 2):TT * (h // 2) + TT]

    def transpose_set(src, dst, ps, nchunks, waits, tag=""):
        g = GEN[id(ps)]
        g.new_gen()
        fns = []
        for c in range(nchunks):
            st = g.start(TT * c, TT * (c + 1))
            def f(eng, c=c, st=st):
                return eng.matmul(ps[:, TT * c:TT * (c + 1)],
                                  src[:, TT * c:TT * (c + 1)], ident,
                                  start=st, stop=True)
            fns.append(f)
        ev = pe(fns, waits=waits, tag=tag + ".t")
        def fe(eng):
            return eng.copy(out=dst[:, 0:TT * nchunks],
                            in_=ps[:, 0:TT * nchunks])
        return aop(fe, waits=[ev], writes=(dst.name,), tag=tag + ".ev")

    def transpose_zsplit(src, dst, ps, waits, tag=""):
        g = GEN[id(ps)]
        g.new_gen()
        fns = []
        for c in range(8):
            st = g.start(TT * c, TT * (c + 1))
            def f(eng, c=c, st=st):
                return eng.matmul(ps[:, TT * c:TT * (c + 1)],
                                  src[:, TT * c:TT * (c + 1)], ident,
                                  start=st, stop=True)
            fns.append(f)
        ev = pe(fns, waits=waits, tag=tag + ".t")
        top = dst.rearrange("p (c b k) -> p c b k", b=2, k=TT)[0:64, :, 0, :]
        bot = dst.rearrange("p (c b k) -> p c b k", b=2, k=TT)[64:128, :, 1, :]
        pst = ps.rearrange("p (c k) -> p c k", k=TT)
        e1 = aop(lambda eng: eng.copy(out=top, in_=pst[0:64]),
                 waits=[ev], writes=(dst.name,), tag=tag + ".e1")
        e2 = aop(lambda eng: eng.copy(out=bot, in_=pst[64:128]),
                 waits=[ev], writes=(dst.name,), tag=tag + ".e2")
        return e2

    def rsqrt_chain(src_col, dst_col, tag=""):
        s1, s2 = stats[:, 2:3], stats[:, 3:4]
        i1 = s1.bitcast(I32)
        fns = [
            lambda eng: eng.tensor_scalar(out=i1, in0=src_col.bitcast(I32),
                                          scalar1=0.5, scalar2=None,
                                          op0=ALU.mult),
            lambda eng: eng.tensor_tensor(out=i1, in0=magic, in1=i1,
                                          op=ALU.subtract),
            lambda eng: eng.tensor_tensor(out=s2, in0=s1, in1=s1, op=ALU.mult),
            lambda eng: eng.tensor_tensor(out=s2, in0=s2, in1=src_col,
                                          op=ALU.mult),
            lambda eng: eng.tensor_scalar(out=s2, in0=s2, scalar1=-0.5,
                                          scalar2=1.5, op0=ALU.mult,
                                          op1=ALU.add),
            lambda eng: eng.tensor_tensor(out=dst_col, in0=s1, in1=s2,
                                          op=ALU.mult),
        ]
        ev = None
        for i, fn in enumerate(fns):
            ev = P.step("gpsimd", fn, reads=("stats",),
                        writes=("stats", "rstd"),
                        inc=("pool" if i == len(fns) - 1 else None),
                        tag=tag + f".rs{i}")
        return ev

    def ln_stats_tail(site, tag):
        st = stats
        P.step("gpsimd", lambda eng: eng.tensor_scalar(
            out=negmu[:, site:site + 1], in0=sumv, scalar1=-1.0 / D,
            scalar2=None, op0=ALU.mult), reads=("sumv",), writes=("negmu",),
            tag=tag + ".mu")
        P.step("gpsimd", lambda eng: eng.tensor_tensor(
            out=st[:, 1:2], in0=negmu[:, site:site + 1],
            in1=negmu[:, site:site + 1], op=ALU.mult),
            reads=("negmu",), writes=("stats",), tag=tag + ".m2")
        P.step("gpsimd", lambda eng: eng.tensor_scalar(
            out=st[:, 0:1], in0=sumsq, scalar1=1.0 / D, scalar2=None,
            op0=ALU.mult), reads=("sumsq",), writes=("stats",), tag=tag + ".s2")
        P.step("gpsimd", lambda eng: eng.tensor_tensor(
            out=st[:, 0:1], in0=st[:, 0:1], in1=st[:, 1:2], op=ALU.subtract),
            reads=("stats",), writes=("stats",), tag=tag + ".var")
        P.step("gpsimd", lambda eng: eng.tensor_scalar(
            out=st[:, 1:2], in0=st[:, 0:1], scalar1=LN_EPS, scalar2=None,
            op0=ALU.add), reads=("stats",), writes=("stats",), tag=tag + ".ve")

    def ln_site(xten, xname, site, waits, fused, tag):
        ev_sq = aop(lambda eng: eng.activation(out=sq_scr, in_=xten,
                                               func=AF.Square,
                                               accum_out=sumsq),
                    waits=waits, reads=(xname,), writes=("sq_scr", "sumsq"),
                    tag=tag + ".sq")
        ev_sum = vop(lambda eng: eng.tensor_reduce(out=sumv, in_=xten,
                                                   axis=AX.X, op=ALU.add),
                     waits=waits, reads=(xname,), writes=("sumv",),
                     tag=tag + ".sum")
        P.step("gpsimd", lambda eng: eng.engine_nop(),
               waits=[ev_sq, ev_sum], tag=tag + ".w")
        ln_stats_tail(site, tag)
        st = stats
        if not fused:
            return rsqrt_chain(st[:, 1:2], rstd[:, site:site + 1], tag=tag)
        rsqrt_chain(st[:, 1:2], st[:, 4:5], tag=tag + ".r1")
        P.step("gpsimd", lambda eng: eng.tensor_tensor(
            out=st[:, 5:6], in0=st[:, 4:5], in1=st[:, 4:5], op=ALU.mult),
            reads=("stats",), writes=("stats",), tag=tag)
        P.step("gpsimd", lambda eng: eng.tensor_tensor(
            out=st[:, 5:6], in0=st[:, 5:6], in1=st[:, 0:1], op=ALU.mult),
            reads=("stats",), writes=("stats",), tag=tag)
        P.step("gpsimd", lambda eng: eng.tensor_scalar(
            out=st[:, 5:6], in0=st[:, 5:6], scalar1=LN_EPS, scalar2=None,
            op0=ALU.add), reads=("stats",), writes=("stats",), tag=tag)
        rsqrt_chain(st[:, 5:6], st[:, 6:7], tag=tag + ".r2")
        return P.step("gpsimd", lambda eng: eng.tensor_tensor(
            out=rstd[:, site:site + 1], in0=st[:, 4:5], in1=st[:, 6:7],
            op=ALU.mult), inc="pool", reads=("stats",), writes=("rstd",),
            tag=tag + ".r12")

    def bcast16(src16, cnt=16):
        return src16[:, 0:cnt].unsqueeze(2).broadcast_to([TT, cnt, DH])

    def tap(name, ev, t0):
        if name in dtap:
            src = tap_srcs[name]
            dma(dtap[name][t0:t0 + TT, :], src, waits=[ev], sem="dmao",
                tag=f"tap.{name}")

    ev_xd_pre, ev_cd_pre = {}, {}
    prev_xd, prev_cd = (), ()
    for it in range(min(2, nt)):
        ev_xd_pre[it] = dma(xin[it % 2], dx[it * TT:(it + 1) * TT, :],
                            sem="dmax", waits=prev_xd,
                            tag=f"prep.xd{it}")
        prev_xd = [ev_xd_pre[it]]
    for it in range(min(4, nt)):
        ev_cd_pre[it] = dma(csb[it % 4], dcs[it * TT:(it + 1) * TT, :],
                            sem="dmac", waits=prev_cd,
                            tag=f"prep.cd{it}")
        prev_cd = [ev_cd_pre[it]]
    for out, in_ in [(ident, dident), (maskup, dmaskup), (causal, dcausal),
                     (ones_row, dones_row), (onesc, donesc),
                     (zero_row, dzero_row), (one_one, done_one),
                     (magic, dmagic), (zbd, dzbd0)]:
        ev_consts = dma(out, in_, tag="prep.c")
    vop(lambda eng: eng.memset(zerocol, 0.0), writes=("zerocol",),
        tag="prep.zc")
    for tz, nm in ((krz, "krz"), (SQz, "SQz"), (SKtz, "SKtz")):
        gop(lambda eng, tz=tz: eng.memset(tz, 0.0), writes=(nm,),
            tag="prep.z" + nm)
    _zero_all = [xn2, q_bf, k_bf, v_bf, qr, kr, SQm, SKm, SKf, Pbuf, Pm,
                 bvec, g1, o_acc, sumv, sumsq, stats, negmu,
                 rstd, den, denq, rden, rdq, negmax, lsum, rl,
                 P2, P2fw, Hbuf[0], Hbuf[1], Q2] + x2
    for zi, zt in enumerate(_zero_all):
        vop(lambda eng, zt=zt: eng.memset(zt, 0.0), writes=("zz",),
            tag=f"prep.zz{zi}")
    ev_zbdbf = vop(lambda eng: eng.tensor_copy(out=zbd_bf, in_=zbd),
                   waits=[ev_consts], writes=("zbd_bf",), tag="prep.zbdbf")

    GST = PsumGen()
    _stz = [
        (lambda eng: eng.matmul(PS_ST[:, 0:512], ones_row, zero_row,
                                start=True, stop=True)),
        (lambda eng: eng.matmul(PS_ST[:, 512:1024], ones_row, zero_row,
                                start=True, stop=True))]
    for h in range(HEADS):
        ro = 64 * (h % 2)
        c0 = TT * (h // 2) + DH * (h % 2)
        def fz(eng, ro=ro, c0=c0):
            return eng.matmul(PS_ST[ro:ro + 64, c0:c0 + DH],
                              ones_row[0:1, 0:64], zero_row[0:1, 0:DH],
                              start=True, stop=True)
        _stz.append(fz)
    ev_stinit = pe(_stz, waits=[ev_consts], tag="prep.stzero")
    ev_membf = aop(lambda eng: eng.copy(out=membf, in_=PS_ST),
                   waits=[ev_stinit], writes=("membf",), tag="prep.stev")

    def row_to_fm(drow, total, dst_f32, dst_bf, chain, tag):
        col = 0
        last = chain
        GB.new_gen()
        while col < total:
            w = min(1024, total - col)
            evd = dma(rowf[0:1, 0:w], drow[0:1, col:col + w],
                      waits=[last], tag=tag + f".d{col}")
            evc = vop(lambda eng, w=w: eng.tensor_copy(
                out=rowbf[0:1, 0:w].bitcast(BF16), in_=rowf[0:1, 0:w]),
                waits=[evd], reads=("rowf",), writes=("rowbf",),
                tag=tag + f".c{col}")
            fns = []
            for c in range(w // TT):
                cc = col // TT + c
                st = GB.start(cc, cc + 1)
                def f(eng, c=c, cc=cc, st=st):
                    return eng.matmul(PS_B[:, cc:cc + 1],
                                      rowbf[0:1, TT * c:TT * (c + 1)].bitcast(BF16),
                                      one_one, start=st, stop=True)
                fns.append(f)
            last = pe(fns, waits=[evc, ev_consts], tag=tag + f".m{col}")
            col += w
        ncol = total // TT
        evs = last
        if dst_f32 is not None:
            evs = vop(lambda eng: eng.tensor_copy(out=dst_f32,
                                                  in_=PS_B[:, 0:ncol]),
                      waits=[last], writes=(dst_f32.name,), tag=tag + ".e")
        if dst_bf is not None:
            evs = aop(lambda eng: eng.copy(out=dst_bf, in_=PS_B[:, 0:ncol]),
                      waits=[last], writes=(dst_bf.name,), tag=tag + ".eb")
        return evs

    ev = row_to_fm(dlng, D, lngf, None, ev_consts, "prep.lng")
    ev = row_to_fm(dlnb, D, None, lnbf, ev, "prep.lnb")
    ev_b1 = row_to_fm(db1, DFF, b1fm, None, ev, "prep.b1")

    ev_bd = dma(stats[0:1, 0:16], dbeta, waits=[ev_b1], tag="prep.beta.d")
    ev_sig = aop(lambda eng: eng.activation(out=rowbf[0:1, 0:16].bitcast(BF16),
                                            in_=stats[0:1, 0:16],
                                            func=AF.Sigmoid),
                 waits=[ev_bd], writes=("rowbf",), tag="prep.beta.sig")
    GC.new_gen()
    ev_bmm = pe([lambda eng: eng.matmul(PS_C[:, 0:16], ones_row,
                                        rowbf[0:1, 0:16].bitcast(BF16),
                                        start=True, stop=True)],
                waits=[ev_sig], tag="prep.beta.mm")
    ev_beta = vop(lambda eng: eng.tensor_copy(out=beta_bc, in_=PS_C[:, 0:16]),
                  waits=[ev_bmm], writes=("beta_bc",), tag="prep.beta.e")
    ev_ombeta = vop(lambda eng: eng.tensor_scalar(
        out=ombeta_bc, in0=beta_bc, scalar1=-1.0, scalar2=1.0, op0=ALU.mult,
        op1=ALU.add), waits=[ev_beta], reads=("beta_bc",),
        writes=("ombeta_bc",), tag="prep.beta.om")

    ev_b2d = dma(rowf[0:1, 0:D], db2, waits=[ev_ombeta], tag="prep.b2.d")
    ev_b2c = vop(lambda eng: eng.tensor_copy(out=rowbf[0:1, 0:D].bitcast(BF16),
                                             in_=rowf[0:1, 0:D]),
                 waits=[ev_b2d], reads=("rowf",), writes=("rowbf",),
                 tag="prep.b2.c")
    GC.new_gen()
    ev_b2m = pe([lambda eng: eng.matmul(PS_C[:, 0:512], ones_row,
                                        rowbf[0:1, 0:512].bitcast(BF16),
                                        start=True, stop=True),
                 lambda eng: eng.matmul(PS_C[:, 512:1024], ones_row,
                                        rowbf[0:1, 512:1024].bitcast(BF16),
                                        start=True, stop=True)],
                waits=[ev_b2c, ev_beta], tag="prep.b2.mm")
    ev_b2 = vop(lambda eng: eng.tensor_copy(out=b2bc, in_=PS_C),
                waits=[ev_b2m], writes=("b2bc",), tag="prep.b2.e")

    GA.new_gen()
    bq_ps = {0: PS_A, 1: PS_C, 2: PS_B}
    ev_last = ev_b2
    ev_bq_mm = {0: None, 1: None, 2: None}
    for cch in range(8):
        for blk in range(3):
            evd = dma(scr_f, dwq[TT * cch:TT * (cch + 1),
                                 1024 * blk:1024 * (blk + 1)],
                      waits=[ev_last], tag=f"prep.wq{cch}.{blk}.d")
            evc = vop(lambda eng: eng.tensor_copy(out=scr_bf[:, 0:1024],
                                                  in_=scr_f),
                      waits=[evd], reads=("scr_f",), writes=("scr_bf",),
                      tag=f"prep.wq{cch}.{blk}.c")
            ps = bq_ps[blk]
            fns = []
            for g in range(2):
                def f(eng, g=g, ps=ps, cch=cch):
                    return eng.matmul(ps[0:1, 512 * g:512 * (g + 1)],
                                      lnbf[:, cch:cch + 1],
                                      scr_bf[:, 512 * g:512 * (g + 1)],
                                      start=(cch == 0), stop=(cch == 7))
                fns.append(f)
            ev_bq_mm[blk] = pe(fns, waits=[evc], tag=f"prep.bq{cch}.{blk}")
            dst = WQ[cch][:, 1024 * blk:1024 * (blk + 1)]
            if blk < 2:
                dst = dst.rearrange("p (h a j) -> p h a j", h=16, a=2,
                                    j=32).transpose([0, 1, 3, 2])
            def fw(eng, dst=dst, cch=cch):
                return eng.tensor_scalar(out=dst, in0=scr_f,
                                         scalar1=lngf[:, cch:cch + 1],
                                         scalar2=None, op0=ALU.mult)
            ev_last = vop(fw, waits=[ev_bq_mm[blk]], reads=("scr_f",),
                          writes=(f"WQ{cch}",), tag=f"prep.wqw{cch}.{blk}")
    ev_bq = ev_last
    for blk, ps in ((0, PS_A), (1, PS_C), (2, PS_B)):
        src = ps[0:1, 0:1024]
        dst = bq_row[0:1, 1024 * blk:1024 * (blk + 1)]
        if blk < 2:
            dst = dst.rearrange("p (h a j) -> p h a j", h=16, a=2,
                                j=32).transpose([0, 1, 3, 2])
        ev_bq = aop(lambda eng, dst=dst, src=src: eng.copy(out=dst, in_=src),
                    waits=[ev_last], writes=("bq_row",), tag=f"prep.bqe{blk}")

    def load_cast(pieces, piece_cols, chain, tag):
        last = chain
        for i, (dst, src) in enumerate(pieces):
            evd = dma(scr_f[:, 0:piece_cols], src, waits=[last],
                      tag=f"{tag}.d{i}")
            last = vop(lambda eng, dst=dst: eng.tensor_copy(
                out=dst, in_=scr_f[:, 0:piece_cols]),
                waits=[evd], reads=("scr_f",), writes=("wgt",),
                tag=f"{tag}.c{i}")
        return last

    if "xn2" in taps:
        dbq = nc.dram_tensor("tap_bq", [1, DQKV], BF16, kind="ExternalOutput").ap()
        dwq0 = nc.dram_tensor("tap_wq0", [TT, DQKV], BF16, kind="ExternalOutput").ap()
        dlngf = nc.dram_tensor("tap_lngf", [TT, 8], F32, kind="ExternalOutput").ap()
        dma(dbq, bq_row, waits=[ev_bq], sem="dmao", tag="prep.tapbq")
        dlnbf = nc.dram_tensor("tap_lnbf", [TT, 8], BF16, kind="ExternalOutput").ap()
        dma(dlnbf, lnbf, waits=[ev_bq], sem="dmao", tag="prep.taplnbf")
        dma(dwq0, WQ[0], waits=[ev_bq], sem="dmao", tag="prep.tapwq")
        dma(dlngf, lngf, waits=[ev_bq], sem="dmao", tag="prep.taplngf")
    ev_w = load_cast([(WO[c], dwo[TT * c:TT * (c + 1), :]) for c in range(8)],
                     D, ev_bq, "prep.wo")
    ev_w = load_cast([(U1w[c], dU1[TT * c:TT * (c + 1), :]) for c in range(8)],
                     RANK, ev_w, "prep.u1")
    pieces = []
    for r in range(2):
        for gp in range(4):
            pieces.append((V1w[r][:, 1024 * gp:1024 * (gp + 1)],
                           dV1[TT * r:TT * (r + 1),
                               1024 * gp:1024 * (gp + 1)]))
    ev_w = load_cast(pieces, 1024, ev_w, "prep.v1")
    ev_w = load_cast([(U2w[c], dU2[TT * c:TT * (c + 1), :])
                      for c in range(32)], RANK, ev_w, "prep.u2")
    ev_wlast = load_cast([(V2w[r], dV2[TT * r:TT * (r + 1), :])
                          for r in range(2)], D, ev_w, "prep.v2")

    P.barrier("prep")

    ev_x2_free = [None] * ST_TILES
    ev_state_delta = ev_stinit
    ev_zbd_prev = ev_zbdbf
    ev_x2_of, ev_kr_of = {}, {}
    st_ctx = {}

    for it in range(nt):
        ib = it % 2
        ic = it % 4
        t0 = it * TT
        tg = f"t{it}"
        bw_pe = ("pe", P.cnt["pe"])
        bw_dve = ("dve", P.cnt["dve"])
        bw_act = ("act", P.cnt["act"])
        bw_pool = ("pool", P.cnt["pool"])
        if it < 2:
            ev_xd = ev_xd_pre[it]
        else:
            ev_xd = dma(xin[ib], dx[t0:t0 + TT, :], sem="dmax",
                        waits=list(prev_xd) + [ev_x2_of[it - 2]],
                        tag=tg + ".xd")
            prev_xd = [ev_xd]
        if it < 4:
            ev_cd = ev_cd_pre[it]
        else:
            ev_cd = dma(csb[ic], dcs[t0:t0 + TT, :], sem="dmac",
                        waits=list(prev_cd) + [ev_kr_of[it - 4]],
                        tag=tg + ".cd")
            prev_cd = [ev_cd]

        ev_rstd0 = ln_site(xin[ib], f"xin{ib}", 0,
                           [ev_xd, bw_dve, bw_pool], True, tg + ".ln0")
        ev_xn2 = vop(lambda eng, ib=ib: eng.tensor_scalar(
            out=xn2, in0=xin[ib], scalar1=negmu[:, 0:1], scalar2=rstd[:, 0:1],
            op0=ALU.add, op1=ALU.mult),
            waits=[ev_rstd0, bw_pe, bw_act],
            reads=(f"xin{ib}", "negmu", "rstd"),
            writes=("xn2",), tag=tg + ".xn2")

        ev_xn2f = transpose_set(xn2, xn2f, PS_A, 8,
                                waits=[ev_xn2, bw_act, bw_dve],
                                tag=tg + ".xn2T")

        qkv_ps = {0: PS_A, 1: PS_B, 2: PS_C}
        ev_qkv = {}
        for blk in range(3):
            ps = qkv_ps[blk]
            GEN[id(ps)].new_gen()
            fns = []
            for gidx in range(2):
                c0 = 512 * gidx
                def fb(eng, blk=blk, c0=c0, ps=ps):
                    return eng.matmul(
                        ps[:, c0:c0 + 512], ones_row,
                        bq_row[0:1, 1024 * blk + c0:1024 * blk + c0 + 512],
                        start=True, stop=False)
                fns.append(fb)
                for cch in range(8):
                    def fm_(eng, blk=blk, cch=cch, c0=c0, ps=ps):
                        return eng.matmul(
                            ps[:, c0:c0 + 512],
                            xn2f[:, TT * cch:TT * (cch + 1)],
                            WQ[cch][:, 1024 * blk + c0:1024 * blk + c0 + 512],
                            start=False, stop=(cch == 7))
                    fns.append(fm_)
            ev_qkv[blk] = pe(fns, waits=[ev_xn2f], tag=tg + f".qkv{blk}")
        ev_qe = aop(lambda eng: eng.copy(out=q_bf, in_=PS_A),
                    waits=[ev_qkv[0]], writes=("q_bf",), tag=tg + ".qe")
        ev_ke = aop(lambda eng: eng.copy(out=k_bf, in_=PS_B),
                    waits=[ev_qkv[1]], writes=("k_bf",), tag=tg + ".ke")
        ev_ve = aop(lambda eng: eng.copy(out=v_bf, in_=PS_C),
                    waits=[ev_qkv[2]], writes=("v_bf",), tag=tg + ".ve")

        cc_b = csb[ic][:, 0:32].unsqueeze(1).broadcast_to([TT, 16, 32])
        ss_b = csb[ic][:, 32:64].unsqueeze(1).broadcast_to([TT, 16, 32])
        def rope(src, dst, ev_src, ev_scr_free, tagr):
            v3 = src.rearrange("p (h a j) -> p (h a) j", a=2, j=32)
            d3 = dst.rearrange("p (h a j) -> p (h a) j", a=2, j=32)
            se, so = v3[:, 0::2, :], v3[:, 1::2, :]
            de, do = d3[:, 0::2, :], d3[:, 1::2, :]
            tA = scr_bf[:, 0:512].rearrange("p (h j) -> p h j", j=32)
            tB = scr_bf[:, 512:1024].rearrange("p (h j) -> p h j", j=32)
            dnop = lambda t: vop(lambda eng: eng.engine_nop(), tag=t)
            e1 = vop(lambda eng: eng.tensor_tensor(out=tA, in0=se, in1=cc_b,
                                                   op=ALU.mult),
                     waits=[ev_src, ev_cd, ev_scr_free], reads=(src.name,),
                     writes=("scr_bf",), tag=tagr + ".1")
            e2 = vop(lambda eng: eng.tensor_tensor(out=tB, in0=so, in1=ss_b,
                                                   op=ALU.mult),
                     waits=[ev_src, ev_cd, ev_scr_free], reads=(src.name,),
                     writes=("scr_bf",), tag=tagr + ".2")
            dnop(tagr + ".n2")
            e5 = vop(lambda eng: eng.tensor_tensor(out=de, in0=tA, in1=tB,
                                                   op=ALU.subtract),
                     waits=[e2], reads=("scr_bf",), writes=(dst.name,),
                     tag=tagr + ".5")
            dnop(tagr + ".n5")
            e3 = vop(lambda eng: eng.tensor_tensor(out=tA, in0=so, in1=cc_b,
                                                   op=ALU.mult),
                     reads=(src.name,), writes=("scr_bf",), tag=tagr + ".3")
            e4 = vop(lambda eng: eng.tensor_tensor(out=tB, in0=se, in1=ss_b,
                                                   op=ALU.mult),
                     waits=[e5], reads=(src.name,), writes=("scr_bf",),
                     tag=tagr + ".4")
            dnop(tagr + ".n4")
            e6 = vop(lambda eng: eng.tensor_tensor(out=do, in0=tA, in1=tB,
                                                   op=ALU.add),
                     waits=[e4], reads=("scr_bf",), writes=(dst.name,),
                     tag=tagr + ".6")
            dnop(tagr + ".n6")
            return e6
        ev_qr = rope(q_bf, qr, ev_qe, None, tg + ".ropeq")
        ev_tapqr = None
        if "bqr" in dtap and it == 0:
            def ftq(eng):
                return eng.dma_start(out=dtap["bqr"][0:TT, :], in_=qr)
            ev_tapqr = P.step("sync", ftq, waits=[ev_qr], inc="dmao",
                              inc_amt=16, tag=tg + ".bqr")
        ev_kr = rope(k_bf, kr, ev_ke, ev_qr, tg + ".ropek")
        ev_kr_of[it] = ev_kr
        if "canq" in dtap and it == 0:
            ecq = vop(lambda eng: eng.tensor_copy(out=canq, in_=qr),
                      waits=[ev_qr], reads=("qr",), writes=("canq",),
                      tag=tg + ".canq")
            ecs2 = aop(lambda eng: eng.copy(out=cancs, in_=csb[0]),
                       waits=[ev_cd], writes=("cancs",), tag=tg + ".cancs")
            ecqb = gop(lambda eng: eng.tensor_copy(out=canqb, in_=q_bf),
                       waits=[ev_qe], reads=("q_bf",), writes=("canqb",),
                       tag=tg + ".canqb")
            tap("canq", ecq, 0)
            tap("cancs", ecs2, 0)
            tap("canqb", ecqb, 0)

        escr = scr_bf[:, 0:1024]
        def elu(src, dst, ev_src, waits_m, tagr):
            e1 = aop(lambda eng: eng.activation(out=escr, in_=src,
                                                func=AF.Exp),
                     waits=[ev_src] + waits_m, reads=(src.name,),
                     writes=("scr_bf",), tag=tagr + ".e")
            e2 = vop(lambda eng: eng.tensor_scalar(out=escr, in0=escr,
                                                   scalar1=1.0, scalar2=None,
                                                   op0=ALU.min),
                     waits=[e1], reads=("scr_bf",), writes=("scr_bf",),
                     tag=tagr + ".m")
            e3 = vop(lambda eng: eng.scalar_tensor_tensor(
                out=dst, in0=src, scalar=0.0, in1=escr, op0=ALU.max,
                op1=ALU.add), waits=[e2], reads=(src.name, "scr_bf"),
                writes=(dst.name,), tag=tagr + ".s")
            return e3
        ev_SQ = elu(q_bf, SQm, ev_qr, [ev_kr], tg + ".eluq")
        ev_SK = elu(k_bf, SKm, ev_kr, [ev_SQ], tg + ".eluk")

        wq_settle, wk_settle = [], []
        if it < 2:
            wq_settle = [ev_kr]
            ev_kset = vop(lambda eng: eng.tensor_copy(out=kr[:, 0:64],
                                                      in_=kr[:, 0:64]),
                          waits=[ev_kr], reads=("kr",), writes=("kr",),
                          tag=tg + ".kset")
            wk_settle = [ev_kset]
        ev_qrf = transpose_set(qr, qrf, PS_B, 8,
                               waits=[ev_qr, ev_ke, ev_tapqr] + wq_settle,
                               tag=tg + ".qrT")
        ev_krz = transpose_zsplit(kr, krz, PS_C,
                                  waits=[ev_kr, ev_ve] + wk_settle,
                                  tag=tg + ".krT")
        ev_SQz = transpose_zsplit(SQm, SQz, PS_A, waits=[ev_SQ, ev_qe],
                                  tag=tg + ".sqT")
        ev_SKf = transpose_set(SKm, SKf, PS_B, 8, waits=[ev_SK, ev_qrf],
                               tag=tg + ".skT")

        ev_nms_r = [None, None]
        ev_nm_r = [None, None]
        for r, ps in ((0, PS_A), (1, PS_C)):
            g = GEN[id(ps)]
            g.new_gen()
            fns = []
            for hh in range(8):
                h = 8 * r + hh
                st = g.start(TT * hh, TT * (hh + 1))
                def f(eng, h=h, hh=hh, ps=ps, st=st):
                    return eng.matmul(ps[:, TT * hh:TT * (hh + 1)],
                                      qrf[:, TT * (h // 2):TT * (h // 2) + TT],
                                      krz[:, TT * h:TT * (h + 1)],
                                      start=st, stop=True)
                fns.append(f)
            w = [ev_qrf, ev_krz] + ([ev_SQz] if r == 0 else [])
            ev_mm = pe(fns, waits=w, tag=tg + f".sc{r}")
            ev_nm_r[r] = vop(lambda eng, ps=ps, r=r: eng.tensor_reduce(
                out=negmax[:, 8 * r:8 * r + 8],
                in_=ps.rearrange("p (h k) -> p h k", k=TT),
                axis=AX.X, op=ALU.max, negate=True),
                waits=[ev_mm], writes=("negmax",), tag=tg + f".nm{r}")
            ev_nms_r[r] = vop(lambda eng, r=r: eng.tensor_scalar(
                out=negmax[:, 8 * r:8 * r + 8],
                in0=negmax[:, 8 * r:8 * r + 8], scalar1=SCALE, scalar2=None,
                op0=ALU.mult), waits=[ev_nm_r[r]], reads=("negmax",),
                writes=("negmax",), tag=tg + f".nms{r}")
        ev_P = [None, None]
        ev_pmsk_r = [None, None]
        cb3 = causal.unsqueeze(1).broadcast_to([TT, 8, TT])
        for r, ps in ((0, PS_A), (1, PS_C)):
            evp = None
            for hh in range(8):
                h = 8 * r + hh
                def fe(eng, h=h, hh=hh, ps=ps):
                    return eng.activation(out=Pbuf[:, TT * h:TT * (h + 1)],
                                          in_=ps[:, TT * hh:TT * (hh + 1)],
                                          func=AF.Exp, bias=negmax[:, h:h + 1],
                                          scale=SCALE)
                if hh == 0:
                    ws = ([ev_nms_r[0], ev_nm_r[1]] if r == 0
                          else [ev_nms_r[1], ev_pmsk_r[0]])
                else:
                    ws = ()
                evp = aop(fe, waits=ws, reads=("negmax",), writes=("Pbuf",),
                          tag=tg + f".exp{h}")
            ev_P[r] = evp
            ev_pmsk_r[r] = vop(lambda eng, r=r: eng.tensor_tensor(
                out=Pm[:, 1024 * r:1024 * (r + 1)].rearrange(
                    "p (h k) -> p h k", k=TT),
                in0=Pbuf[:, 1024 * r:1024 * (r + 1)].rearrange(
                    "p (h k) -> p h k", k=TT), in1=cb3,
                op=ALU.mult), waits=[ev_P[r]], reads=("Pbuf",),
                writes=("Pm",), tag=tg + f".pmsk{r}")
        ev_tappb = None
        if "bpb" in dtap and it == 0:
            def ftp(eng):
                return eng.dma_start(out=dtap["bpb"][0:TT, :], in_=Pbuf)
            ev_tappb = P.step("sync", ftp, waits=[ev_P[0], ev_P[1]],
                              inc="dmao", inc_amt=16, tag=tg + ".bpb")
        ev_ttr = vop(lambda eng: eng.tensor_reduce(
            out=lsum, in_=Pm.rearrange("p (h k) -> p h k", k=TT), axis=AX.X,
            op=ALU.add), waits=[ev_pmsk_r[1]], reads=("Pm",),
            writes=("lsum",), tag=tg + ".lsum")
        ev_rl = vop(lambda eng: eng.reciprocal(out=rl, in_=lsum),
                    waits=[ev_ttr], reads=("lsum",), writes=("rl",),
                    tag=tg + ".rl")
        ev_rl2 = vop(lambda eng: eng.tensor_tensor(out=rl, in0=rl,
                                                   in1=ombeta_bc,
                                                   op=ALU.mult),
                     waits=[ev_rl], reads=("rl",), writes=("rl",),
                     tag=tg + ".rl2")

        ev_PT = [None, None]
        for r, ps in ((0, PS_A), (1, PS_C)):
            g = GEN[id(ps)]
            g.new_gen()
            fns = []
            for hh in range(8):
                h = 8 * r + hh
                st = g.start(TT * hh, TT * (hh + 1))
                def f(eng, h=h, hh=hh, ps=ps, st=st):
                    return eng.matmul(ps[:, TT * hh:TT * (hh + 1)],
                                      Pm[:, TT * h:TT * (h + 1)], ident,
                                      start=st, stop=True)
                fns.append(f)
            ev_mm = pe(fns, waits=[ev_ttr], tag=tg + f".pT{r}")
            def fe(eng, r=r, ps=ps):
                return eng.copy(out=PTb[:, 1024 * r:1024 * (r + 1)],
                                in_=ps[:, 0:1024])
            ev_PT[r] = aop(fe, waits=[ev_mm, ev_tappb], writes=("Pbuf",),
                           tag=tg + f".pTe{r}")

        GB.new_gen()
        fns = []
        for h in range(HEADS):
            st = GB.start(DH * h, DH * (h + 1))
            def f(eng, h=h, st=st):
                return eng.matmul(PS_B[:, DH * h:DH * (h + 1)],
                                  PTb[:, TT * h:TT * (h + 1)],
                                  v_bf[:, DH * h:DH * (h + 1)],
                                  start=st, stop=True)
            fns.append(f)
        ev_sdpa = pe(fns, waits=[ev_PT[0], ev_PT[1], ev_ve, ev_SKf],
                     tag=tg + ".sdpa")
        ev_omix1 = vop(lambda eng: eng.tensor_tensor(
            out=scr_f.rearrange("p (h d) -> p h d", d=DH),
            in0=PS_B.rearrange("p (h d) -> p h d", d=DH), in1=bcast16(rl),
            op=ALU.mult), waits=[ev_sdpa, ev_rl2], reads=("rl",),
            writes=("scr_f",), tag=tg + ".omix1")

        GA.new_gen()
        ev_zp = pe([
            lambda eng: eng.matmul(PS_A[:, 0:512], maskup, SKm[:, 0:512],
                                   start=True, stop=True),
            lambda eng: eng.matmul(PS_A[:, 512:1024], maskup, SKm[:, 512:1024],
                                   start=True, stop=True)],
            waits=[ev_SK, ev_PT[0]], tag=tg + ".zp")
        GC.new_gen()
        fns = []
        for c in range(8):
            def f(eng, c=c):
                return eng.matmul(PS_C[:, 992 + 2 * c:994 + 2 * c],
                                  SKf[:, TT * c:TT * (c + 1)],
                                  zbd_bf[:, 2 * c:2 * c + 2],
                                  start=True, stop=True)
            fns.append(f)
        for h in range(HEADS):
            def f(eng, h=h):
                return eng.matmul(PS_C[:, 976 + h:977 + h],
                                  SQz[:, TT * h:TT * (h + 1)],
                                  zbd_bf[:, h:h + 1],
                                  start=True, stop=True)
            fns.append(f)
        ev_z0 = pe(fns, waits=[ev_SKf, ev_SQz, ev_zbd_prev, ev_PT[1]],
                   tag=tg + ".z0")
        ev_dp = vop(lambda eng: eng.tensor_tensor(
            out=scr_bf[:, 0:1024], in0=SKm, in1=PS_A, op=ALU.mult),
            waits=[ev_zp, ev_kr, ev_qr], reads=("SKm",), writes=("scr_bf",),
            tag=tg + ".dp")
        ev_dr = vop(lambda eng: eng.tensor_reduce(
            out=den, in_=scr_bf[:, 0:1024].rearrange("p (h d) -> p h d",
                                                     d=DH),
            axis=AX.X, op=ALU.add), waits=[ev_dp], reads=("scr_bf",),
            writes=("den",), tag=tg + ".dr")
        ev_den = vop(lambda eng: eng.tensor_tensor(
            out=den, in0=den, in1=PS_C[:, 992:1008], op=ALU.add),
            waits=[ev_dr, ev_z0], reads=("den",), writes=("den",),
            tag=tg + ".den")
        ev_qp = vop(lambda eng: eng.tensor_tensor(
            out=scr_bf[:, 0:1024], in0=SQm, in1=PS_A, op=ALU.mult),
            waits=[ev_zp, ev_dr], reads=("SQm",), writes=("scr_bf",),
            tag=tg + ".qp")
        ev_qred = vop(lambda eng: eng.tensor_reduce(
            out=denq, in_=scr_bf[:, 0:1024].rearrange(
                "p (h d) -> p h d", d=DH), axis=AX.X, op=ALU.add),
            waits=[ev_qp], reads=("scr_bf",), writes=("denq",),
            tag=tg + ".qred")
        ev_denq = vop(lambda eng: eng.scalar_tensor_tensor(
            out=denq, in0=PS_C[:, 976:992], scalar=1e-6, in1=denq,
            op0=ALU.add, op1=ALU.add), waits=[ev_qred, ev_z0],
            reads=("denq",), writes=("denq",), tag=tg + ".denq")
        ev_rden = vop(lambda eng: eng.reciprocal(out=rden, in_=den),
                      waits=[ev_den], reads=("den",), writes=("rden",),
                      tag=tg + ".rden")
        ev_rdq0 = vop(lambda eng: eng.reciprocal(out=rdq, in_=denq),
                      waits=[ev_denq], reads=("denq",), writes=("rdq",),
                      tag=tg + ".rdq0")
        ev_rdq = vop(lambda eng: eng.tensor_tensor(out=rdq, in0=rdq,
                                                   in1=beta_bc, op=ALU.mult),
                     waits=[ev_rdq0], reads=("rdq",), writes=("rdq",),
                     tag=tg + ".rdq")
        ev_skt = vop(lambda eng: eng.tensor_tensor(
            out=SKt.rearrange("p (h d) -> p h d", d=DH),
            in0=SKm.rearrange("p (h d) -> p h d", d=DH),
            in1=bcast16(rden), op=ALU.mult),
            waits=[ev_rden], reads=("SKm", "rden"), writes=("SKt",),
            tag=tg + ".skt")
        ev_sktz = transpose_zsplit(SKt, SKtz, PS_A,
                                   waits=[ev_skt, ev_dp, ev_qp],
                                   tag=tg + ".sktT")

        mupb = maskup.unsqueeze(1).broadcast_to([TT, 8, TT])
        ev_KT = [None, None]
        for r, ps in ((0, PS_B), (1, PS_C)):
            g = GEN[id(ps)]
            g.new_gen()
            fns = []
            for hh in range(8):
                h = 8 * r + hh
                st = g.start(TT * hh, TT * (hh + 1))
                def f(eng, h=h, hh=hh, ps=ps, st=st):
                    return eng.matmul(ps[:, TT * hh:TT * (hh + 1)],
                                      SKf[:, TT * (h // 2):TT * (h // 2) + TT],
                                      SKtz[:, TT * h:TT * (h + 1)],
                                      start=st, stop=True)
                fns.append(f)
            w = [ev_sktz, ev_omix1 if r == 0 else ev_denq]
            if r == 1:
                w.append(ev_den)
            ev_mm = pe(fns, waits=w, tag=tg + f".kt{r}")
            def fe(eng, r=r, ps=ps):
                return eng.tensor_tensor(
                    out=KTm[:, 1024 * r:1024 * (r + 1)].rearrange(
                        "p (h k) -> p h k", k=TT),
                    in0=ps.rearrange("p (h k) -> p h k", k=TT),
                    in1=mupb, op=ALU.mult)
            ev_KT[r] = vop(fe, waits=[ev_mm], writes=("KTm",),
                           tag=tg + f".kte{r}")
        ev_Sq = [None, None]
        for r, ps in ((0, PS_B), (1, PS_C)):
            g = GEN[id(ps)]
            g.new_gen()
            fns = []
            for hh in range(8):
                h = 8 * r + hh
                st = g.start(TT * hh, TT * (hh + 1))
                def f(eng, h=h, hh=hh, ps=ps, st=st):
                    return eng.matmul(ps[:, TT * hh:TT * (hh + 1)],
                                      SKf[:, TT * (h // 2):TT * (h // 2) + TT],
                                      SQz[:, TT * h:TT * (h + 1)],
                                      start=st, stop=True)
                fns.append(f)
            ev_mm = pe(fns, waits=[ev_KT[r]], tag=tg + f".sqm{r}")
            def fe(eng, r=r, ps=ps):
                return eng.tensor_tensor(
                    out=SmU[:, 1024 * r:1024 * (r + 1)].rearrange(
                        "p (h k) -> p h k", k=TT),
                    in0=ps.rearrange("p (h k) -> p h k", k=TT),
                    in1=mupb, op=ALU.mult)
            ev_Sq[r] = vop(fe, waits=[ev_mm], writes=("SmU",),
                           tag=tg + f".sqe{r}")

        GA.new_gen()
        fns = []
        for h in range(HEADS):
            st = GA.start(DH * h, DH * (h + 1))
            mcol = TT * (h // 2) + DH * (h % 2)
            def f(eng, h=h, st=st, mcol=mcol):
                return eng.matmul(PS_A[:, DH * h:DH * (h + 1)],
                                  SKtz[:, TT * h:TT * (h + 1)],
                                  membf[:, mcol:mcol + DH],
                                  start=st, stop=True)
            fns.append(f)
        ev_pv = pe(fns, waits=[ev_sktz, ev_membf], tag=tg + ".pv")
        ev_b = vop(lambda eng: eng.scalar_tensor_tensor(
            out=bvec, in0=PS_A, scalar=-1.0, in1=v_bf, op0=ALU.mult,
            op1=ALU.add), waits=[ev_pv, ev_ve], writes=("bvec",),
            tag=tg + ".b")
        ev_bbf = gop(lambda eng: eng.tensor_copy(out=b_bf, in_=bvec),
                     waits=[ev_b], reads=("bvec",), writes=("b_bf",),
                     tag=tg + ".bbf")

        prev_reader = {id(PS_B): ev_Sq[0], id(PS_C): ev_Sq[1]}
        g_in, ev_in = b_bf, ev_bbf
        ev_u = None
        ev_h3_mm = None
        for i3, (ps, dst) in enumerate(((PS_B, g1), (PS_C, g2),
                                        (PS_B, uvec))):
            g = GEN[id(ps)]
            g.new_gen()
            fns = []
            for h in range(HEADS):
                st = g.start(DH * h, DH * (h + 1))
                def f(eng, h=h, ps=ps, g_in=g_in, st=st):
                    return eng.matmul(ps[:, DH * h:DH * (h + 1)],
                                      KTm[:, TT * h:TT * (h + 1)],
                                      g_in[:, DH * h:DH * (h + 1)],
                                      start=st, stop=True)
                fns.append(f)
            ev_mm = pe(fns, waits=[ev_in, prev_reader[id(ps)], ev_KT[1]],
                       tag=tg + f".h{i3}")
            ev_h3_mm = ev_mm
            def fe(eng, ps=ps, dst=dst):
                return eng.scalar_tensor_tensor(out=dst, in0=ps, scalar=-1.0,
                                                in1=bvec, op0=ALU.mult,
                                                op1=ALU.add)
            evg = vop(fe, waits=[ev_mm], reads=("bvec",), writes=(dst.name,),
                      tag=tg + f".g{i3}")
            prev_reader[id(ps)] = evg
            g_in, ev_in = dst, evg
        ev_u = ev_in

        GC.new_gen()
        fns = []
        for h in range(HEADS):
            GC.start(DH * h, DH * (h + 1))
            mcol = TT * (h // 2) + DH * (h % 2)
            def f(eng, h=h, mcol=mcol):
                return eng.matmul(PS_C[:, DH * h:DH * (h + 1)],
                                  SQz[:, TT * h:TT * (h + 1)],
                                  membf[:, mcol:mcol + DH],
                                  start=True, stop=False)
            fns.append(f)
            def f2(eng, h=h):
                return eng.matmul(PS_C[:, DH * h:DH * (h + 1)],
                                  SmU[:, TT * h:TT * (h + 1)],
                                  uvec[:, DH * h:DH * (h + 1)],
                                  start=False, stop=True)
            fns.append(f2)
        ev_num = pe(fns, waits=[ev_u, ev_membf, prev_reader[id(PS_C)]],
                    tag=tg + ".num")
        ev_omix2 = vop(lambda eng: eng.tensor_tensor(
            out=o_acc.rearrange("p (h d) -> p h d", d=DH),
            in0=PS_C.rearrange("p (h d) -> p h d", d=DH), in1=bcast16(rdq),
            op=ALU.mult), waits=[ev_num, ev_rdq], reads=("rdq",),
            writes=("o_acc",), tag=tg + ".omix2")
        ev_o = gop(lambda eng: eng.tensor_tensor(out=o_bf, in0=o_acc,
                                                 in1=scr_f, op=ALU.add),
                   waits=[ev_omix2, ev_omix1], reads=("o_acc", "scr_f"),
                   writes=("q_bf",), tag=tg + ".o")

        fns = []
        for h in range(HEADS):
            def f(eng, h=h):
                ro = 64 * (h % 2)
                c0 = TT * (h // 2) + DH * (h % 2)
                return eng.matmul(PS_ST[ro:ro + 64, c0:c0 + DH],
                                  SKm[:, DH * h:DH * (h + 1)],
                                  uvec[:, DH * h:DH * (h + 1)],
                                  start=False, stop=True)
            fns.append(f)
        for h in range(HEADS):
            def f(eng, h=h):
                ro = 64 * (h % 2)
                return eng.matmul(PS_B[ro:ro + 64,
                                       1016 + (h // 2):1017 + (h // 2)],
                                  SKm[:, DH * h:DH * (h + 1)], onesc,
                                  start=True, stop=True)
            fns.append(f)
        ev_dmem = pe(fns, waits=[ev_u, ev_state_delta, ev_membf,
                                 prev_reader[id(PS_B)]],
                     tag=tg + ".dmem")
        ev_state_delta = ev_dmem
        zview_e = zbd[0:64, :].rearrange("p (c a) -> p a c", a=2)[:, 0, :]
        zview_o = zbd[64:128, :].rearrange("p (c a) -> p a c", a=2)[:, 1, :]
        ev_z1 = vop(lambda eng: eng.tensor_tensor(
            out=zview_e, in0=zview_e, in1=PS_B[0:64, 1016:1024], op=ALU.add),
            waits=[ev_dmem], reads=("zbd",), writes=("zbd",), tag=tg + ".z1")
        ev_z2 = vop(lambda eng: eng.tensor_tensor(
            out=zview_o, in0=zview_o, in1=PS_B[64:128, 1016:1024],
            op=ALU.add), waits=[ev_dmem], reads=("zbd",), writes=("zbd",),
            tag=tg + ".z2")
        ev_zbd_prev = vop(lambda eng: eng.tensor_copy(out=zbd_bf, in_=zbd),
                          waits=[ev_z1, ev_z2], reads=("zbd",),
                          writes=("zbd_bf",), tag=tg + ".zbf")
        ev_membf = aop(lambda eng: eng.copy(out=membf, in_=PS_ST),
                       waits=[ev_dmem], writes=("membf",), tag=tg + ".stev")

        ev_ofm = transpose_set(o_bf, o_fm, PS_B, 8,
                               waits=[ev_o, ev_z1, ev_z2], tag=tg + ".oT")
        GC.new_gen()
        fns = []
        for gidx in range(2):
            c0 = 512 * gidx
            for cch in range(8):
                def f(eng, cch=cch, c0=c0):
                    return eng.matmul(PS_C[:, c0:c0 + 512],
                                      o_fm[:, TT * cch:TT * (cch + 1)],
                                      WO[cch][:, c0:c0 + 512],
                                      start=(cch == 0), stop=(cch == 7))
                fns.append(f)
        ev_womm = pe(fns, waits=[ev_ofm, ev_omix2], tag=tg + ".wo")
        ev_ao = vop(lambda eng: eng.scalar_tensor_tensor(
            out=ao, in0=PS_C, scalar=0.0,
            in1=zerocol.broadcast_to([TT, D]), op0=ALU.add, op1=ALU.add,
            accum_out=sumv), waits=[ev_womm, ev_bbf],
            writes=("bvec", "sumv"), tag=tg + ".ao")
        ev_aosq = aop(lambda eng: eng.activation(out=sq_scr, in_=ao,
                                                 func=AF.Square,
                                                 accum_out=sumsq),
                      waits=[ev_ao], reads=("bvec",),
                      writes=("sq_scr", "sumsq"), tag=tg + ".aosq")
        P.step("gpsimd", lambda eng: eng.engine_nop(),
               waits=[ev_ao, ev_aosq], tag=tg + ".lnw")
        ln_stats_tail(1, tg + ".ln1")
        ev_rstd1 = rsqrt_chain(stats[:, 1:2], rstd[:, 1:2], tag=tg + ".ln1r")
        ev_t1 = vop(lambda eng: eng.tensor_scalar(
            out=scr_f, in0=ao, scalar1=negmu[:, 1:2], scalar2=rstd[:, 1:2],
            op0=ALU.add, op1=ALU.mult),
            waits=[ev_rstd1, ev_o], reads=("bvec", "negmu", "rstd"),
            writes=("scr_f",), tag=tg + ".lnap1")
        jslot = it % ST_TILES
        ev_x2 = vop(lambda eng, ib=ib, jslot=jslot: eng.tensor_tensor(
            out=x2[jslot], in0=scr_f, in1=xin[ib], op=ALU.add),
            waits=[ev_t1, ev_x2_free[jslot]], reads=("scr_f", f"xin{ib}"),
            writes=(f"x2_{jslot}",), tag=tg + ".x2")
        ev_x2_of[it] = ev_x2

        ev_rstd2 = ln_site(x2[jslot], f"x2_{jslot}", 2,
                           [ev_x2, ev_aosq], False, tg + ".ln2")
        ev_xn3 = vop(lambda eng, jslot=jslot: eng.tensor_scalar(
            out=xn3, in0=x2[jslot], scalar1=negmu[:, 2:3],
            scalar2=rstd[:, 2:3], op0=ALU.add, op1=ALU.mult),
            waits=[ev_rstd2], reads=(f"x2_{jslot}", "negmu", "rstd"),
            writes=("xn3",), tag=tg + ".xn3")
        ev_xn3f = transpose_set(xn3, xn3f, PS_B, 8, waits=[ev_xn3, ev_ofm],
                                tag=tg + ".xn3T")
        GC.new_gen()
        fns = []
        for cch in range(8):
            def f(eng, cch=cch):
                return eng.matmul(PS_C[:, 0:256],
                                  xn3f[:, TT * cch:TT * (cch + 1)],
                                  U1w[cch], start=(cch == 0), stop=(cch == 7))
            fns.append(f)
        ev_u1mm = pe(fns, waits=[ev_xn3f, ev_ao], tag=tg + ".u1")
        ev_p2 = aop(lambda eng: eng.copy(out=P2, in_=PS_C[:, 0:256]),
                    waits=[ev_u1mm], writes=("P2",), tag=tg + ".p2")
        fns = []
        for rc in range(2):
            def f(eng, rc=rc):
                return eng.matmul(PS_C[:, 256 + TT * rc:256 + TT * (rc + 1)],
                                  P2[:, TT * rc:TT * (rc + 1)], ident,
                                  start=True, stop=True)
            fns.append(f)
        ev_p2t = pe(fns, waits=[ev_p2], tag=tg + ".p2T")
        p2dst = P2fw.rearrange("p (r t k) -> p r t k", r=2, t=ST_TILES)
        def fe(eng, jslot=jslot):
            return eng.copy(out=p2dst[:, :, jslot, :],
                            in_=PS_C[:, 256:512].rearrange(
                                "p (r k) -> p r k", r=2))
        ev_p2fw = aop(fe, waits=[ev_p2t], writes=("P2fw",), tag=tg + ".p2fw")

        tap("xn2", ev_xn2, t0)
        tap("q", ev_qe, t0)
        tap("k", ev_ke, t0)
        tap("v", ev_ve, t0)
        tap("qqr", ev_qr, t0)
        tap("kkr", ev_kr, t0)
        tap("sq", ev_SQ, t0)
        tap("sk", ev_SK, t0)
        tap("u", ev_u, t0)
        tap("o", ev_o, t0)
        tap("aao", ev_ao, t0)
        tap("den", ev_den, t0)
        tap("denq", ev_denq, t0)
        tap("mem", ev_membf, t0)
        tap("lsum", ev_ttr, t0)
        tap("b", ev_b, t0)
        tap("ktm", ev_KT[1], t0)
        if jslot == 0:
            tap("x2_0", ev_x2, t0)
        tap("pm", ev_ttr, t0)
        tap("negmax", ev_ttr, t0)

        tap("xn3t", ev_xn3, t0)
        st_ctx[jslot] = dict(ev_p2fw=ev_p2fw, it=it)
        if jslot != ST_TILES - 1 and it != nt - 1:
            continue

        nslots = jslot + 1
        ev_allp2 = [st_ctx[j]["ev_p2fw"] for j in range(nslots)]
        for _ev in ev_allp2:
            tap("p2f", _ev, 0)
        GA.new_gen()
        GB.new_gen()
        ev_gel = [None, None]
        ev_q2mm = None
        for fc in range(32):
            pp = fc % 2
            hb = Hbuf[pp]
            c0 = 512 * pp
            fns = []
            for r2 in range(2):
                def f(eng, r2=r2, fc=fc, c0=c0):
                    return eng.matmul(PS_A[:, c0:c0 + TT * nslots],
                                      V1w[r2][:, TT * fc:TT * (fc + 1)],
                                      P2fw[:, 512 * r2:512 * r2 + TT * nslots],
                                      start=(r2 == 0), stop=(r2 == 1))
                fns.append(f)
            if fc < 2:
                w = list(ev_allp2) + [ev_b, ev_pv]
            else:
                w = [ev_gel[pp]]
            ev_v1mm = pe(fns, waits=w, tag=tg + f".v1_{fc}")
            def fg(eng, fc=fc, hb=hb, c0=c0):
                return eng.activation(out=hb[:, 0:TT * nslots],
                                      in_=PS_A[:, c0:c0 + TT * nslots],
                                      func=AF.Gelu_apprx_tanh,
                                      bias=b1fm[:, fc:fc + 1])
            ev_g = aop(fg, waits=[ev_v1mm], writes=(hb.name,),
                       tag=tg + f".gel{fc}")
            fns = []
            if fc == 0:
                for j in range(nslots):
                    GB.start(RANK * j, RANK * (j + 1))
                    def fz(eng, j=j):
                        return eng.matmul(PS_B[:, RANK * j:RANK * (j + 1)],
                                          ones_row,
                                          zero_row[0:1, 0:RANK],
                                          start=True, stop=False)
                    fns.append(fz)
            for j in range(nslots):
                def f(eng, j=j, fc=fc, hb=hb):
                    return eng.matmul(PS_B[:, RANK * j:RANK * (j + 1)],
                                      hb[:, TT * j:TT * (j + 1)], U2w[fc],
                                      start=False, stop=(fc == 31))
                fns.append(f)
            w = [ev_g]
            if fc == 0:
                w += [ev_dmem, ev_z1, ev_z2, ev_xn3f]
            ev_q2mm = pe(fns, waits=w, tag=tg + f".u2_{fc}")
            ev_gel[pp] = ev_q2mm

        ev_prev_tail = None
        for j in reversed(range(nslots)):
            jt = st_ctx[j]["it"]
            jt0 = jt * TT
            ttg = f"t{jt}.ffn"
            ev_q2 = aop(lambda eng, j=j: eng.copy(
                out=Q2, in_=PS_B[:, RANK * j:RANK * (j + 1)]),
                waits=[ev_q2mm, ev_prev_tail], writes=("Q2",),
                tag=ttg + ".q2")
            GC.new_gen()
            fns = []
            for rc in range(2):
                st = GC.start(TT * rc, TT * (rc + 1))
                def f(eng, rc=rc, st=st):
                    return eng.matmul(PS_C[:, TT * rc:TT * (rc + 1)],
                                      Q2[:, TT * rc:TT * (rc + 1)], ident,
                                      start=st, stop=True)
                fns.append(f)
            ev_q2t = pe(fns, waits=[ev_q2, ev_p2fw], tag=ttg + ".q2T")
            ev_q2f = aop(lambda eng: eng.copy(out=Q2f, in_=PS_C[:, 0:256]),
                         waits=[ev_q2t], writes=("Q2f",), tag=ttg + ".q2f")
            GC.new_gen()
            fns = []
            for gidx in range(2):
                c0 = 512 * gidx
                for rc in range(2):
                    def f(eng, rc=rc, c0=c0):
                        return eng.matmul(PS_C[:, c0:c0 + 512],
                                          Q2f[:, TT * rc:TT * (rc + 1)],
                                          V2w[rc][:, c0:c0 + 512],
                                          start=(rc == 0), stop=(rc == 1))
                    fns.append(f)
            ev_v2 = pe(fns, waits=[ev_q2f], tag=ttg + ".v2")
            ev_f = vop(lambda eng: eng.scalar_tensor_tensor(
                out=fout, in0=PS_C, scalar=0.0, in1=b2bc, op0=ALU.add,
                op1=ALU.add, accum_out=sumv),
                waits=[ev_v2], writes=("o_acc", "sumv"), tag=ttg + ".f")
            tap("fs0", ev_f, jt0)
            tap("q2s0", ev_q2, jt0)
            ev_fsq = aop(lambda eng: eng.activation(out=sq_scr, in_=fout,
                                                    func=AF.Square,
                                                    accum_out=sumsq),
                         waits=[ev_f], reads=("o_acc",),
                         writes=("sq_scr", "sumsq"), tag=ttg + ".fsq")
            P.step("gpsimd", lambda eng: eng.engine_nop(),
                   waits=[ev_f, ev_fsq], tag=ttg + ".w")
            ln_stats_tail(3, ttg + ".ln3")
            ev_rstd3 = rsqrt_chain(stats[:, 1:2], rstd[:, 3:4],
                                   tag=ttg + ".ln3r")
            ev_t3 = vop(lambda eng: eng.tensor_scalar(
                out=scr_f, in0=fout, scalar1=negmu[:, 3:4],
                scalar2=rstd[:, 3:4], op0=ALU.add, op1=ALU.mult),
                waits=[ev_rstd3], reads=("o_acc", "negmu", "rstd"),
                writes=("scr_f",), tag=ttg + ".lap")
            ev_y = vop(lambda eng, j=j: eng.tensor_tensor(
                out=x2[j], in0=scr_f, in1=x2[j], op=ALU.add),
                waits=[ev_t3], reads=("scr_f",), writes=(f"x2_{j}",),
                tag=ttg + ".y")
            ev_od = dma(dout[jt0:jt0 + TT, :], x2[j], waits=[ev_y],
                        sem="dmao", tag=ttg + ".od")
            ev_x2_free[j] = ev_od
            ev_prev_tail = ev_t3
        st_ctx = {}

    if "x2late" in taps:
        late = dict(x2late=x2[0], aol=bvec, scrfl=scr_f, negmul=negmu,
                    rstdl=rstd, sumvl=sumv, sumsql=sumsq, obl=q_bf,
                    oaccl=o_acc, lsuml=lsum, rll=rl, xinl=xin[0],
                    pml=Pm, pbl=Pbuf, negmaxl=negmax, qrl=qr, krzl=krz,
                    **({"canql": canq, "canqbl": canqb, "cancsl": cancs}
                       if canq is not None else {}))
        w = [(s, P.cnt[s]) for s in ("pe", "dve", "act", "pool")]
        for nm, buf in late.items():
            dl = nc.dram_tensor(f"tap_{nm}", list(buf.shape), buf.dtype,
                                kind="ExternalOutput").ap()
            def fxl(eng, dl=dl, buf=buf):
                return eng.dma_start(out=dl, in_=buf)
            P.step("sync", fxl, inc="dmao", inc_amt=16, tag=f"late.{nm}",
                   force=True, waits=w)
            w = []
    if P.frozen:
        def ffz(eng):
            return eng.dma_start(out=dout[0:TT, :], in_=xin[0])
        P.step("sync", ffz, inc="dmao", inc_amt=16, tag="final.frz",
               force=True)
    P.step("sync", lambda eng: eng.nop(),
           waits=[("dmao", P.cnt["dmao"])], tag="final", force=True)

    with (nc.semaphore("s_dmaw") as s_dmaw, nc.semaphore("s_dmax") as s_dmax,
          nc.semaphore("s_dmac") as s_dmac,
          nc.semaphore("s_dmao") as s_dmao, nc.semaphore("s_pe") as s_pe,
          nc.semaphore("s_dve") as s_dve, nc.semaphore("s_act") as s_act,
          nc.semaphore("s_pool") as s_pool,
          nc.semaphore("s_esync") as s_esync, nc.Block() as block):
        P.sems = dict(dmaw=s_dmaw, dmax=s_dmax, dmac=s_dmac, dmao=s_dmao,
                      pe=s_pe, dve=s_dve, act=s_act, pool=s_pool,
                      esync=s_esync)

        @block.sync
        def _(eng):
            P.run_engine(eng, "sync")

        @block.tensor
        def _(eng):
            P.run_engine(eng, "tensor")

        @block.vector
        def _(eng):
            P.run_engine(eng, "vector")

        @block.scalar
        def _(eng):
            P.run_engine(eng, "scalar")

        @block.gpsimd
        def _(eng):
            P.run_engine(eng, "gpsimd")

    return nc


def kernel(L0_attn_x, L0_attn_w_qkv, L0_attn_w_out, L0_attn_ln_g, L0_attn_ln_b,
           L0_attn_mem_beta, L0_ffn_U1, L0_ffn_V1, L0_ffn_b1, L0_ffn_U2,
           L0_ffn_V2, L0_ffn_b2):
    x = np.ascontiguousarray(np.asarray(L0_attn_x, dtype=np.float32))
    B = x.shape[0]
    nc = build(NT_FULL)
    common = dict(
        w_qkv=np.ascontiguousarray(L0_attn_w_qkv, dtype=np.float32),
        w_out=np.ascontiguousarray(L0_attn_w_out, dtype=np.float32),
        ln_g=np.asarray(L0_attn_ln_g, dtype=np.float32).reshape(1, D),
        ln_b=np.asarray(L0_attn_ln_b, dtype=np.float32).reshape(1, D),
        mem_beta=np.asarray(L0_attn_mem_beta,
                            dtype=np.float32).reshape(1, HEADS),
        U1=np.ascontiguousarray(L0_ffn_U1, dtype=np.float32),
        V1=np.ascontiguousarray(L0_ffn_V1, dtype=np.float32),
        b1=np.asarray(L0_ffn_b1, dtype=np.float32).reshape(1, DFF),
        U2=np.ascontiguousarray(L0_ffn_U2, dtype=np.float32),
        V2=np.ascontiguousarray(L0_ffn_V2, dtype=np.float32),
        b2=np.asarray(L0_ffn_b2, dtype=np.float32).reshape(1, D),
    )
    in_maps = [dict(common, x=np.ascontiguousarray(x[b])) for b in range(B)]
    res = run_bass_kernel_spmd(nc, in_maps, core_ids=list(range(B)))
    return np.stack([res.results[b]["out"] for b in range(B)], axis=0)
